# version 04
# baseline (speedup 1.0000x reference)
import numpy as np
import concourse.bass as bass
import concourse.mybir as mybir
from concourse.bass_utils import run_bass_kernel_spmd
from contextlib import ExitStack

F32 = mybir.dt.float32
BF16 = mybir.dt.bfloat16
U8 = mybir.dt.uint8
AF = mybir.ActivationFunctionType
ALU = mybir.AluOpType
AX = mybir.AxisListType

P = 128
T = 2048
D = 4096
I = 11008
ND = D // P
NTB = T // P
NIT = I // P
K_KEEP = I // 2
IC = 512
ICB = list(range(0, I, IC))
ICW = [min(IC, I - s0) for s0 in ICB]
NIC = len(ICB)
NHC = D // 512
NQ = 4
TQ = T // NQ
NTBQ = TQ // P

MED_ITERS = 12
MED_BISECT = 3
MED_HI = 2.56
MED_THR0 = 0.27
MED_CLAMP = 0.02
LN_EPS = 1.0e-5

NG = NIC * NTB


def build_program():
    nc = bass.Bass("TRN2", target_bir_lowering=False, detect_race_conditions=False)

    x_in = nc.dram_tensor("x_in", [T, D], F32, kind="ExternalInput").ap()
    wgT_in = nc.dram_tensor("wgT_in", [D, I], F32, kind="ExternalInput").ap()
    wuT_in = nc.dram_tensor("wuT_in", [D, I], F32, kind="ExternalInput").ap()
    wdT_in = nc.dram_tensor("wdT_in", [I, D], F32, kind="ExternalInput").ap()
    aT_in = nc.dram_tensor("aT_in", [D, 128], F32, kind="ExternalInput").ap()
    bF_in = nc.dram_tensor("bF_in", [128, D], F32, kind="ExternalInput").ap()
    wrT_in = nc.dram_tensor("wrT_in", [D, 8], F32, kind="ExternalInput").ap()
    br_in = nc.dram_tensor("br_in", [1, 8], F32, kind="ExternalInput").ap()
    eye_in = nc.dram_tensor("eye_in", [128, 128], F32, kind="ExternalInput").ap()
    y_out = nc.dram_tensor("y_out", [T, D], F32, kind="ExternalOutput").ap()

    _SPK = "ExternalOutput" if globals().get("DEBUG_SPILL") else "Internal"
    s_sp = nc.dram_tensor("s_sp", [T, I], F32, kind=_SPK).ap()
    u_sp = nc.dram_tensor("u_sp", [T, I], BF16, kind=_SPK).ap()
    sm_sp = nc.dram_tensor("sm_sp", [T, I], BF16, kind=_SPK).ap()
    ffn_sp = nc.dram_tensor("ffn_sp", [T, D], BF16, kind=_SPK).ap()

    wgT_t = wgT_in.rearrange("(nd p) i -> p nd i", p=P)
    wuT_t = wuT_in.rearrange("(nd p) i -> p nd i", p=P)
    aT_t = aT_in.rearrange("(nd p) c -> p nd c", p=P)
    wrT_t = wrT_in.rearrange("(nd p) c -> p nd c", p=P)

    with ExitStack() as ctx:
        e = ctx.enter_context

        eyeF = e(nc.sbuf_tensor("eyeF", [128, 128], F32))
        eyeB = e(nc.sbuf_tensor("eyeB", [128, 128], BF16))
        ones1 = e(nc.sbuf_tensor("ones1", [1, 128], BF16))
        br_sb = e(nc.sbuf_tensor("br_sb", [1, 8], BF16))

        def st(name):
            return e(nc.sbuf_tensor(name, [128, 1], F32))
        sum1, mean, ssq, var, ve, sq, r0, t1, t2, t3, rstd, eps_t = (
            st(n) for n in ["sum1", "mean", "ssq", "var", "ve", "sq", "r0",
                            "t1", "t2", "t3", "rstd", "eps_t"])
        loS = [st("lo0"), st("lo1")]
        hiS = [st("hi0"), st("hi1")]
        cloS = [st("clo0"), st("clo1")]
        chiS = [st("chi0"), st("chi1")]
        thr, nthr, craw, cnt = (st(n) for n in ["thr", "nthr", "craw", "cnt"])
        mhi = e(nc.sbuf_tensor("mhi", [128, 1], U8))
        mlo = e(nc.sbuf_tensor("mlo", [128, 1], U8))
        den, num, rq, dd, rd, marg, lob, hib, tmp1 = (
            st(n) for n in ["den", "num", "rq", "dd", "rd", "marg", "lob", "hib", "tmp1"])
        mx, nmx, sume, rs = (st(n) for n in ["mx", "nmx", "sume", "rs"])
        exps = e(nc.sbuf_tensor("exps", [128, 8], F32))
        sco = e(nc.sbuf_tensor("sco", [128, 8], F32))
        ones16 = e(nc.sbuf_tensor("ones16", [128, 16], F32))

        ARENA = 205824
        arena = e(nc.sbuf_tensor("arena", [128, ARENA], U8))
        base = nc.lookup_mloc(arena).addr

        def place(name, shape, dtype, off):
            assert off + int(np.prod(shape[1:])) * mybir.dt.size(dtype) <= ARENA, name
            return nc.alloc_sbuf_tensor_at(name, shape, dtype, offset=base + off)

        xnT = place("xnT", [128, ND * T], BF16, 0)
        xb = [place(f"xb{j}", [128, D], F32, 131072 + j * 16384) for j in range(2)]
        xn_sb = place("xn_sb", [128, D], F32, 163840)
        trash_sq = place("trash_sq", [128, D], F32, 180224)
        wslab = [place(f"wslab{j}", [128, ND * IC], BF16, 131072 + j * 32768)
                 for j in range(2)]
        stage_s = [place(f"stage_s{j}", [128, IC], F32, 196608 + j * 2048)
                   for j in range(4)]
        stage_u = [place(f"stage_u{j}", [128, IC], BF16, 196608 + j * 1024)
                   for j in range(4)]
        aT_sb = place("aT_sb", [128, ND * 128], BF16, 131072)
        wrT_sb = place("wrT_sb", [128, ND * 8], BF16, 139264)
        bF_sb = place("bF_sb", [128, D], BF16, 139776)
        ffnst = [place(f"ffnst{j}", [128, 2048], BF16, 147968 + j * 4096)
                 for j in range(2)]
        z_sb = place("z_sb", [128, 128], F32, 156160)
        zT_sb = place("zT_sb", [128, 128], BF16, 156672)
        srep = place("srep", [128, 128], F32, 156928)
        s_t = [place(f"s_t{j}", [128, I], F32, j * 44032) for j in range(2)]
        a_sb = place("a_sb", [128, I], F32, 88064)
        sm0_sb = place("sm0_sb", [128, I], BF16, 132096)
        trash_m = place("trash_m", [128, I], BF16, 154112)
        smtq = place("smtq", [128, NIT * 512], BF16, 0)
        smt_in = [place(f"smt_in{j}", [128, 512], BF16, 88064 + j * 1024) for j in range(2)]
        u_in = [place(f"u_in{j}", [128, 512], BF16, 90112 + j * 1024) for j in range(2)]
        sprod = [place(f"sprod{j}", [128, 512], BF16, 92160 + j * 1024) for j in range(2)]
        wdring = [place(f"wdring{j}", [128, 4 * 512], BF16, 94208 + j * 4096) for j in range(4)]
        x_t = [place(f"x_t{j}", [128, 512], F32, 110592 + j * 2048) for j in range(4)]
        ffn_t = [place(f"ffn_t{j}", [128, 512], F32, 118784 + j * 2048) for j in range(4)]
        t_a = [place(f"t_a{j}", [128, 512], F32, 126976 + j * 2048) for j in range(2)]
        outst = [place(f"outst{j}", [128, 512], F32, 131072 + j * 2048) for j in range(4)]

        pA = nc.alloc_psum_tensor("pA", [128, 2048], F32)
        pB = nc.alloc_psum_tensor("pB", [128, 1024], F32)
        pC = nc.alloc_psum_tensor("pC", [128, 1024], BF16)
        pD = nc.alloc_psum_tensor("pD", [128, 1024], BF16)

        ph = e(nc.semaphore("ph"))
        tm = e(nc.semaphore("tm"))
        vm = e(nc.semaphore("vm"))
        sm = e(nc.semaphore("sm"))
        vv = e(nc.semaphore("vv"))
        qe = e(nc.semaphore("qe"))
        qx = [e(nc.semaphore(f"qx{j}")) for j in range(2)]
        qs = [e(nc.semaphore(f"qs{j}")) for j in range(2)]
        qt = [e(nc.semaphore(f"qt{j}")) for j in range(4)]
        qm = e(nc.semaphore("qm"))
        qc = e(nc.semaphore("qc"))
        qff = [e(nc.semaphore(f"qff{j}")) for j in range(2)]
        qp = [e(nc.semaphore(f"qp{j}")) for j in range(2)]
        qu = [e(nc.semaphore(f"qu{j}")) for j in range(2)]
        qd = [e(nc.semaphore(f"qd{j}")) for j in range(4)]
        qxt = [e(nc.semaphore(f"qxt{j}")) for j in range(4)]
        qf = [e(nc.semaphore(f"qf{j}")) for j in range(4)]
        qout = [e(nc.semaphore(f"qout{j}")) for j in range(4)]

        D_ = {
            "P0": dict(tm=4 * NTB, vm=2 * NTB, sm=5 * NTB),
            "P1": dict(tm=NG, vm=0, sm=NG),
            "P3": dict(tm=NG, vm=0, sm=NG),
            "P4": dict(tm=4 * NTB, vm=5 * NTB, sm=NTB),
            "P2": dict(tm=0, vm=(2 + MED_ITERS) * NTB, sm=(1 + MED_ITERS) * NTB),
            "P5": dict(tm=NQ * (NIT + NIC * NHC), vm=NQ * (2 * NIT + NHC * NTBQ), sm=0),
        }
        PH_ORDER = ["P0", "P1", "P3", "P4", "P2", "P5"]
        B = {}
        run = dict(tm=0, vm=0, sm=0)
        for pn in PH_ORDER:
            B[pn] = dict(run)
            for k in run:
                run[k] += D_[pn][k]

        xnT_v = xnT.ap().rearrange("p (nd t) -> p nd t", t=T)

        def xnT_tile(nd, tb):
            return xnT_v[:, nd, tb * P:(tb + 1) * P]

        TMQ = NIT + NIC * NHC
        VMQ = 2 * NIT + NHC * NTBQ

        blk = e(nc.Block())

        @blk.sync
        def _(sync):
            b = B["P0"]
            sync.dma_start(out=eyeF[:], in_=eye_in).then_inc(qe, 16)
            for tb in range(NTB):
                if tb >= 2:
                    sync.wait_ge(vm, b["vm"] + 2 * (tb - 2) + 2)
                sync.dma_start(out=xb[tb % 2][:], in_=x_in[tb * P:(tb + 1) * P, :]
                               ).then_inc(qx[tb % 2], 16)
            sync.nop().then_inc(ph, 1)
            sync.wait_ge(ph, 5)
            b = B["P1"]
            for ic in range(NIC):
                i0, iw = ICB[ic], ICW[ic]
                for tb in range(NTB):
                    g = ic * NTB + tb
                    sync.wait_ge(sm, b["sm"] + g + 1)
                    sync.dma_start(out=s_sp[tb * P:(tb + 1) * P, i0:i0 + iw],
                                   in_=stage_s[g % 4][:, 0:iw]).then_inc(qt[g % 4], 16)
            for j in range(4):
                sync.wait_ge(qt[j], 16 * (NG // 4))
            sync.nop().then_inc(ph, 1)
            sync.wait_ge(ph, 10)
            b = B["P3"]
            for ic in range(NIC):
                i0, iw = ICB[ic], ICW[ic]
                for tb in range(NTB):
                    g = ic * NTB + tb
                    sync.wait_ge(sm, b["sm"] + g + 1)
                    sync.dma_start(out=u_sp[tb * P:(tb + 1) * P, i0:i0 + iw],
                                   in_=stage_u[g % 4][:, 0:iw]).then_inc(qt[g % 4], 16)
            for j in range(4):
                sync.wait_ge(qt[j], 16 * (2 * NG // 4))
            sync.nop().then_inc(ph, 1)
            sync.wait_ge(ph, 15)
            b = B["P4"]
            for tb in range(NTB):
                for h2 in range(2):
                    sync.wait_ge(vm, b["vm"] + 5 * tb + 4 + h2)
                    sync.dma_start(out=ffn_sp[tb * P:(tb + 1) * P,
                                              h2 * 2048:(h2 + 1) * 2048],
                                   in_=ffnst[h2][:]).then_inc(qff[h2], 16)
            for j in range(2):
                sync.wait_ge(qff[j], 16 * NTB)
            sync.nop().then_inc(ph, 1)
            sync.wait_ge(ph, 20)
            b = B["P2"]
            ITb = 2 + MED_ITERS
            for tb in range(NTB):
                if tb >= 2:
                    sync.wait_ge(vm, b["vm"] + ITb * (tb - 1))
                sync.dma_start(out=s_t[tb % 2][:], in_=s_sp[tb * P:(tb + 1) * P, :]
                               ).then_inc(qx[tb % 2], 16)
                sync.wait_ge(vm, b["vm"] + ITb * tb + ITb)
                sync.dma_start(out=sm_sp[tb * P:(tb + 1) * P, :], in_=sm0_sb[:]
                               ).then_inc(qm, 16)
            sync.wait_ge(qm, 16 * NTB)
            sync.nop().then_inc(ph, 1)
            sync.wait_ge(ph, 25)
            b = B["P5"]
            for q in range(NQ):
                r0_ = q * TQ
                vmq = b["vm"] + q * VMQ
                for it in range(NIT):
                    if it >= 2:
                        sync.wait_ge(vm, vmq + 2 * (it - 2) + 1)
                    src_s = sm_sp[r0_:r0_ + TQ, it * P:(it + 1) * P]
                    src_u = u_sp[r0_:r0_ + TQ, it * P:(it + 1) * P]
                    sync.dma_start(out=smt_in[it % 2].ap().rearrange(
                                       "p (tt i) -> p tt i", i=P),
                                   in_=src_s.rearrange("(tt p) i -> p tt i", p=P)
                                   ).then_inc(qp[it % 2], 16)
                    sync.dma_start(out=u_in[it % 2].ap().rearrange(
                                       "p (tt i) -> p tt i", i=P),
                                   in_=src_u.rearrange("(tt p) i -> p tt i", p=P)
                                   ).then_inc(qu[it % 2], 16)
                NE = NHC * NTBQ
                def _xload(ev):
                    hc_, tb_ = divmod(ev, NTBQ)
                    if ev >= 4:
                        sync.wait_ge(vm, vmq + 2 * NIT + ev - 4 + 1)
                    sync.dma_start(
                        out=x_t[ev % 4][:],
                        in_=x_in[r0_ + tb_ * P:r0_ + (tb_ + 1) * P,
                                 hc_ * 512:(hc_ + 1) * 512]).then_inc(qxt[ev % 4], 16)
                _xload(0)
                _xload(1)
                for ev in range(NE):
                    if ev + 2 < NE:
                        _xload(ev + 2)
                    hc_, tb_ = divmod(ev, NTBQ)
                    sync.wait_ge(vm, vmq + 2 * NIT + ev + 1)
                    sync.dma_start(
                        out=y_out[r0_ + tb_ * P:r0_ + (tb_ + 1) * P,
                                  hc_ * 512:(hc_ + 1) * 512],
                        in_=outst[ev % 4][:]).then_inc(qout[ev % 4], 16)
            for j in range(4):
                sync.wait_ge(qout[j], 16 * (NQ * NHC * NTBQ // 4))
            sync.nop().then_inc(ph, 1)

        @blk.gpsimd
        def _(gp):
            gp.nop().then_inc(ph, 1)
            gp.wait_ge(ph, 5)
            b = B["P1"]
            for ic in range(NIC):
                i0, iw = ICB[ic], ICW[ic]
                if ic >= 2:
                    gp.wait_ge(tm, b["tm"] + (ic - 1) * NTB)
                dst = wslab[ic % 2].ap().rearrange("p (nd i) -> p nd i", i=IC)[:, :, 0:iw]
                gp.dma_start(out=dst, in_=wgT_t[:, :, i0:i0 + iw]).then_inc(qs[ic % 2], 16)
            gp.nop().then_inc(ph, 1)
            gp.wait_ge(ph, 10)
            b = B["P3"]
            for ic in range(NIC):
                i0, iw = ICB[ic], ICW[ic]
                if ic >= 2:
                    gp.wait_ge(tm, b["tm"] + (ic - 1) * NTB)
                dst = wslab[ic % 2].ap().rearrange("p (nd i) -> p nd i", i=IC)[:, :, 0:iw]
                gp.dma_start(out=dst, in_=wuT_t[:, :, i0:i0 + iw]).then_inc(qs[ic % 2], 16)
            gp.nop().then_inc(ph, 1)
            gp.wait_ge(ph, 15)
            gp.dma_start(out=aT_sb.ap().rearrange("p (nd c) -> p nd c", c=128),
                         in_=aT_t).then_inc(qc, 16)
            gp.dma_start(out=wrT_sb.ap().rearrange("p (nd c) -> p nd c", c=8),
                         in_=wrT_t).then_inc(qc, 16)
            gp.dma_start(out=bF_sb[:], in_=bF_in).then_inc(qc, 16)
            gp.dma_start(out=br_sb[:], in_=br_in).then_inc(qc, 16)
            gp.dma_start(out=eyeB[:], in_=eye_in).then_inc(qc, 16)
            gp.nop().then_inc(ph, 1)
            gp.wait_ge(ph, 20)
            gp.nop().then_inc(ph, 1)
            gp.wait_ge(ph, 25)
            b = B["P5"]
            for q in range(NQ):
                r0_ = q * TQ
                tmq = b["tm"] + q * TMQ
                vmq = b["vm"] + q * VMQ
                for hc in range(NHC):
                    for grp in range(NIC):
                        gi0 = grp * 4
                        gw = min(4, NIT - gi0)
                        gidx = hc * NIC + grp
                        if gidx >= 4:
                            gp.wait_ge(tm, tmq + NIT + gidx - 4 + 1)
                        elif q >= 1:
                            gp.wait_ge(tm, tmq - TMQ + NIT + (NHC * NIC - 4 + gidx) + 1)
                        src = wdT_in[gi0 * P:gi0 * P + gw * P, hc * 512:(hc + 1) * 512]
                        gp.dma_start(
                            out=wdring[gidx % 4].ap().rearrange(
                                "p (g h) -> p g h", h=512)[:, 0:gw, :],
                            in_=src.rearrange("(g p) h -> p g h", p=P)
                        ).then_inc(qd[gidx % 4], 16)
                    for tb in range(NTBQ):
                        ev = hc * NTBQ + tb
                        if ev >= 4:
                            gp.wait_ge(vm, vmq + 2 * NIT + ev - 4 + 1)
                        elif q >= 1:
                            gp.wait_ge(vm, vmq - VMQ + 2 * NIT + (NHC * NTBQ - 4 + ev) + 1)
                        gp.dma_start(
                            out=ffn_t[ev % 4][:],
                            in_=ffn_sp[r0_ + tb * P:r0_ + (tb + 1) * P,
                                       hc * 512:(hc + 1) * 512]).then_inc(qf[ev % 4], 16)
            gp.nop().then_inc(ph, 1)

        @blk.tensor
        def _(t):
            b = B["P0"]
            t.wait_ge(qe, 16)
            pregs = [pA[:, 0:1024], pA[:, 1024:2048], pB[:, 0:1024]]
            for tb in range(NTB):
                t.wait_ge(vm, b["vm"] + 2 * tb + 2)
                for bt in range(4):
                    gb = 4 * tb + bt
                    if gb >= 3:
                        pt, pb_ = divmod(gb - 3, 4)
                        t.wait_ge(sm, b["sm"] + 5 * pt + 2 + pb_)
                    reg = pregs[gb % 3]
                    for j in range(8):
                        nd = 8 * bt + j
                        mi = t.transpose(reg[:, j * 128:(j + 1) * 128],
                                         xn_sb[:, nd * 128:(nd + 1) * 128], eyeF[:])
                    mi.then_inc(tm, 1)
            t.drain().then_inc(ph, 1)
            t.wait_ge(ph, 5)
            b = B["P1"]
            for ic in range(NIC):
                iw = ICW[ic]
                t.wait_ge(qs[ic % 2], 16 * (ic // 2 + 1))
                sl = wslab[ic % 2].ap().rearrange("p (nd i) -> p nd i", i=IC)
                for tb in range(NTB):
                    g = ic * NTB + tb
                    pb_ = pB[:, (g % 2) * 512:(g % 2) * 512 + iw]
                    if g >= 2:
                        t.wait_ge(sm, b["sm"] + g - 1)
                    for nd in range(ND):
                        mi = t.matmul(pb_, xnT_tile(nd, tb), sl[:, nd, 0:iw],
                                      start=(nd == 0), stop=(nd == ND - 1))
                    mi.then_inc(tm, 1)
            t.drain().then_inc(ph, 1)
            t.wait_ge(ph, 10)
            b = B["P3"]
            for ic in range(NIC):
                iw = ICW[ic]
                t.wait_ge(qs[ic % 2], 16 * (NIC // 2 + ic // 2 + 1))
                sl = wslab[ic % 2].ap().rearrange("p (nd i) -> p nd i", i=IC)
                for tb in range(NTB):
                    g = ic * NTB + tb
                    pb_ = pB[:, (g % 2) * 512:(g % 2) * 512 + iw]
                    if g >= 2:
                        t.wait_ge(sm, b["sm"] + g - 1)
                    for nd in range(ND):
                        mi = t.matmul(pb_, xnT_tile(nd, tb), sl[:, nd, 0:iw],
                                      start=(nd == 0), stop=(nd == ND - 1))
                    mi.then_inc(tm, 1)
            t.drain().then_inc(ph, 1)
            t.wait_ge(ph, 15)
            b = B["P4"]
            t.wait_ge(qc, 16 * 5)
            wr_v = wrT_sb.ap().rearrange("p (nd c) -> p nd c", c=8)
            aT_v = aT_sb.ap().rearrange("p (nd c) -> p nd c", c=128)
            for tb in range(NTB):
                if tb >= 1:
                    t.wait_ge(vm, b["vm"] + 5 * (tb - 1) + 5)
                for nd in range(ND):
                    t.matmul(pB[:, 0:128], xnT_tile(nd, tb), aT_v[:, nd, :],
                             start=(nd == 0), stop=(nd == ND - 1))
                    t.matmul(pB[:, 512:520], xnT_tile(nd, tb), wr_v[:, nd, :],
                             start=(nd == 0), stop=False)
                mi = t.matmul(pB[:, 512:520], ones1[:, :], br_sb[:, :],
                              start=False, stop=True)
                mi.then_inc(tm, 1)
                t.wait_ge(vm, b["vm"] + 5 * tb + 2)
                t.transpose(pB[:, 128:256], z_sb[:], eyeF[:]).then_inc(tm, 1)
                t.wait_ge(vm, b["vm"] + 5 * tb + 3)
                for h2 in range(2):
                    if h2 == 1:
                        t.wait_ge(vm, b["vm"] + 5 * tb + 4)
                    for k4 in range(4):
                        mi = t.matmul(pA[:, k4 * 512:(k4 + 1) * 512], zT_sb[:],
                                      bF_sb[:, h2 * 2048 + k4 * 512:
                                            h2 * 2048 + (k4 + 1) * 512],
                                      start=True, stop=True)
                    mi.then_inc(tm, 1)
            t.drain().then_inc(ph, 1)
            t.wait_ge(ph, 20)
            t.drain().then_inc(ph, 1)
            t.wait_ge(ph, 25)
            b = B["P5"]
            for q in range(NQ):
                vmq = b["vm"] + q * VMQ
                tmq = b["tm"] + q * TMQ
                pslots = [pC[:, 0:512], pC[:, 512:1024], pD[:, 0:512], pD[:, 512:1024]]
                for it in range(NIT):
                    t.wait_ge(vm, vmq + 2 * it + 1)
                    if it >= 4:
                        t.wait_ge(vm, vmq + 2 * (it - 4) + 2)
                    reg = pslots[it % 4]
                    for j in range(4):
                        mi = t.transpose(reg[:, j * 128:(j + 1) * 128],
                                         sprod[it % 2][:, j * 128:(j + 1) * 128],
                                         eyeB[:])
                    mi.then_inc(tm, 1)
                t.wait_ge(vm, vmq + 2 * NIT)
                smtq_v = smtq.ap().rearrange("p (it t) -> p it t", t=512)
                for hc in range(NHC):
                    if hc >= 1:
                        t.wait_ge(vm, vmq + 2 * NIT + hc * NTBQ)
                    elif q >= 1:
                        t.wait_ge(vm, vmq - VMQ + 2 * NIT + NHC * NTBQ)
                    for grp in range(NIC):
                        gi0 = grp * 4
                        gw = min(4, NIT - gi0)
                        gidx = hc * NIC + grp
                        t.wait_ge(qd[gidx % 4],
                                  16 * (q * (NHC * NIC // 4) + gidx // 4 + 1))
                        wdv = wdring[gidx % 4].ap().rearrange("p (g h) -> p g h", h=512)
                        for j in range(gw):
                            it = gi0 + j
                            for tb in range(NTBQ):
                                mi = t.matmul(pA[:, tb * 512:(tb + 1) * 512],
                                              smtq_v[:, it, tb * P:(tb + 1) * P],
                                              wdv[:, j, :],
                                              start=(it == 0), stop=(it == NIT - 1))
                        mi.then_inc(tm, 1)
            t.drain().then_inc(ph, 1)

        @blk.vector
        def _(v):
            vvc = [0]
            def edge(ins):
                vvc[0] += 1
                ins.then_inc(vv, 1)
                v.wait_ge(vv, vvc[0])
            b = B["P0"]
            v.memset(ones16[:], 1.0)
            v.memset(ones1[:], 1.0)
            v.memset(eps_t[:, 0:1], LN_EPS)
            for tb in range(NTB):
                xbuf = xb[tb % 2]
                v.wait_ge(qx[tb % 2], 16 * (tb // 2 + 1))
                edge(v.reduce_sum(sum1[:, 0:1], xbuf[:], axis=AX.X))
                edge(v.tensor_scalar_mul(mean[:, 0:1], sum1[:, 0:1], 1.0 / D))
                v.scalar_tensor_tensor(out=trash_sq[:], in0=xbuf[:], scalar=mean[:, 0:1],
                                       in1=xbuf[:], op0=ALU.subtract, op1=ALU.mult)
                edge(v.reduce_sum(ssq[:, 0:1], trash_sq[:], axis=AX.X))
                edge(v.tensor_scalar_mul(var[:, 0:1], ssq[:, 0:1], 1.0 / D))
                v.tensor_scalar_add(ve[:, 0:1], var[:, 0:1], LN_EPS).then_inc(vm, 1)
                v.wait_ge(sm, b["sm"] + 5 * tb + 1)
                edge(v.reciprocal(r0[:, 0:1], sq[:, 0:1]))
                edge(v.tensor_mul(t1[:, 0:1], r0[:, 0:1], r0[:, 0:1]))
                edge(v.tensor_mul(t2[:, 0:1], ve[:, 0:1], t1[:, 0:1]))
                edge(v.tensor_scalar(out=t3[:, 0:1], in0=t2[:, 0:1], scalar1=-0.5,
                                     scalar2=1.5, op0=ALU.mult, op1=ALU.add))
                edge(v.tensor_mul(rstd[:, 0:1], r0[:, 0:1], t3[:, 0:1]))
                if tb >= 1:
                    v.wait_ge(tm, b["tm"] + 4 * tb)
                v.tensor_scalar(out=xn_sb[:], in0=xbuf[:], scalar1=mean[:, 0:1],
                                scalar2=rstd[:, 0:1], op0=ALU.subtract,
                                op1=ALU.mult).then_inc(vm, 1)
            v.drain().then_inc(ph, 1)
            v.wait_ge(ph, 5)
            v.drain().then_inc(ph, 1)
            v.wait_ge(ph, 10)
            v.drain().then_inc(ph, 1)
            v.wait_ge(ph, 15)
            b = B["P4"]
            for tb in range(NTB):
                v.wait_ge(tm, b["tm"] + 4 * tb + 1)
                edge(v.reduce_max(mx[:, 0:1], pB[:, 512:520], axis=AX.X))
                v.tensor_scalar_mul(nmx[:, 0:1], mx[:, 0:1], -1.0).then_inc(vm, 1)
                v.wait_ge(sm, b["sm"] + tb + 1)
                edge(v.reciprocal(rs[:, 0:1], sume[:, 0:1]))
                edge(v.tensor_scalar(out=sco[:], in0=exps[:], scalar1=rs[:, 0:1],
                                     scalar2=16.0, op0=ALU.mult, op1=ALU.mult))
                for h in range(8):
                    ins = v.tensor_scalar_mul(srep[:, h * 16:(h + 1) * 16], ones16[:],
                                              sco[:, h:h + 1])
                edge(ins)
                v.tensor_mul(z_sb[:], pB[:, 0:128], srep[:]).then_inc(vm, 1)
                v.wait_ge(tm, b["tm"] + 4 * tb + 2)
                v.tensor_copy(zT_sb[:], pB[:, 128:256]).then_inc(vm, 1)
                for h2 in range(2):
                    v.wait_ge(tm, b["tm"] + 4 * tb + 3 + h2)
                    if tb >= 1:
                        v.wait_ge(qff[h2], 16 * tb)
                    v.tensor_copy(ffnst[h2][:], pA[:, 0:2048]).then_inc(vm, 1)
            v.drain().then_inc(ph, 1)
            v.wait_ge(ph, 20)
            b = B["P2"]
            ITb = 2 + MED_ITERS
            for tb in range(NTB):
                vmt = b["vm"] + ITb * tb
                smt = b["sm"] + (1 + MED_ITERS) * tb
                v.memset(loS[0][:, 0:1], 0.0)
                v.memset(hiS[0][:, 0:1], MED_HI)
                v.memset(cloS[0][:, 0:1], float(I))
                v.memset(chiS[0][:, 0:1], -float(I))
                v.memset(thr[:, 0:1], MED_THR0)
                v.memset(nthr[:, 0:1], -MED_THR0).then_inc(vm, 1)
                for k in range(MED_ITERS):
                    cs, ns = k % 2, (k + 1) % 2
                    v.wait_ge(sm, smt + 2 + k)
                    edge(v.tensor_single_scalar(mhi[:, 0:1], craw[:, 0:1], 0.0,
                                                op=ALU.is_lt))
                    v.tensor_single_scalar(mlo[:, 0:1], craw[:, 0:1], 0.0, op=ALU.is_ge)
                    edge(v.select(hiS[ns][:, 0:1], mhi[:, 0:1], thr[:, 0:1],
                                  hiS[cs][:, 0:1]))
                    v.select(chiS[ns][:, 0:1], mhi[:, 0:1], craw[:, 0:1], chiS[cs][:, 0:1])
                    v.select(loS[ns][:, 0:1], mlo[:, 0:1], thr[:, 0:1], loS[cs][:, 0:1])
                    edge(v.select(cloS[ns][:, 0:1], mlo[:, 0:1], craw[:, 0:1],
                                  cloS[cs][:, 0:1]))
                    if k < MED_BISECT:
                        edge(v.tensor_add(tmp1[:, 0:1], loS[ns][:, 0:1], hiS[ns][:, 0:1]))
                        edge(v.tensor_scalar_mul(thr[:, 0:1], tmp1[:, 0:1], 0.5))
                    else:
                        v.tensor_sub(den[:, 0:1], cloS[ns][:, 0:1], chiS[ns][:, 0:1])
                        edge(v.tensor_sub(dd[:, 0:1], hiS[ns][:, 0:1], loS[ns][:, 0:1]))
                        v.reciprocal(tmp1[:, 0:1], den[:, 0:1])
                        edge(v.tensor_scalar_mul(marg[:, 0:1], dd[:, 0:1], MED_CLAMP))
                        v.tensor_mul(rq[:, 0:1], cloS[ns][:, 0:1], tmp1[:, 0:1])
                        v.tensor_add(lob[:, 0:1], loS[ns][:, 0:1], marg[:, 0:1])
                        edge(v.tensor_sub(hib[:, 0:1], hiS[ns][:, 0:1], marg[:, 0:1]))
                        edge(v.tensor_mul(rd[:, 0:1], rq[:, 0:1], dd[:, 0:1]))
                        edge(v.tensor_add(thr[:, 0:1], loS[ns][:, 0:1], rd[:, 0:1]))
                        edge(v.tensor_max(thr[:, 0:1], thr[:, 0:1], lob[:, 0:1]))
                        edge(v.tensor_tensor(thr[:, 0:1], thr[:, 0:1], hib[:, 0:1],
                                             op=ALU.min))
                    v.tensor_scalar_mul(nthr[:, 0:1], thr[:, 0:1], -1.0).then_inc(vm, 1)
                if tb >= 1:
                    v.wait_ge(qm, 16 * tb)
                v.scalar_tensor_tensor(out=sm0_sb[:], in0=a_sb[:], scalar=thr[:, 0:1],
                                       in1=s_t[tb % 2][:], op0=ALU.is_gt,
                                       op1=ALU.mult).then_inc(vm, 1)
            v.drain().then_inc(ph, 1)
            v.wait_ge(ph, 25)
            b = B["P5"]
            for q in range(NQ):
                vmq = b["vm"] + q * VMQ
                tmq = b["tm"] + q * TMQ
                pslots = [pC[:, 0:512], pC[:, 512:1024], pD[:, 0:512], pD[:, 512:1024]]
                for it in range(NIT):
                    v.wait_ge(qp[it % 2], 16 * (q * ((NIT + 1) // 2) + it // 2 + 1))
                    v.wait_ge(qu[it % 2], 16 * (q * ((NIT + 1) // 2) + it // 2 + 1))
                    if it >= 2:
                        v.wait_ge(tm, tmq + it - 1)
                    v.tensor_mul(sprod[it % 2][:], smt_in[it % 2][:],
                                 u_in[it % 2][:]).then_inc(vm, 1)
                    v.wait_ge(tm, tmq + it + 1)
                    v.tensor_copy(smtq[:, it * 512:(it + 1) * 512],
                                  pslots[it % 4]).then_inc(vm, 1)
                for hc in range(NHC):
                    for tb in range(NTBQ):
                        ev = hc * NTBQ + tb
                        v.wait_ge(tm, tmq + NIT + (hc + 1) * NIC)
                        v.wait_ge(qxt[ev % 4],
                                  16 * (q * (NHC * NTBQ // 4) + ev // 4 + 1))
                        v.wait_ge(qf[ev % 4],
                                  16 * (q * (NHC * NTBQ // 4) + ev // 4 + 1))
                        v.tensor_add(t_a[ev % 2][:], pA[:, tb * 512:(tb + 1) * 512],
                                     x_t[ev % 4][:])
                        if ev >= 4:
                            v.wait_ge(qout[ev % 4],
                                      16 * (q * (NHC * NTBQ // 4) + ev // 4))
                        v.tensor_add(outst[ev % 4][:], t_a[ev % 2][:],
                                     ffn_t[ev % 4][:]).then_inc(vm, 1)
            v.drain().then_inc(ph, 1)

        @blk.scalar
        def _(sc):
            b = B["P0"]
            pregs = [pA[:, 0:1024], pA[:, 1024:2048], pB[:, 0:1024]]
            for tb in range(NTB):
                sc.wait_ge(vm, b["vm"] + 2 * tb + 1)
                sc.activation(sq[:, 0:1], var[:, 0:1], AF.Sqrt,
                              bias=eps_t[:, 0:1]).then_inc(sm, 1)
                for bt in range(4):
                    gb = 4 * tb + bt
                    sc.wait_ge(tm, b["tm"] + gb + 1)
                    src = pregs[gb % 3].rearrange("p (j t) -> p j t", t=128)
                    dst = xnT_v[:, 8 * bt:8 * bt + 8, tb * P:(tb + 1) * P]
                    sc.copy(dst, src).then_inc(sm, 1)
            sc.drain().then_inc(ph, 1)
            sc.wait_ge(ph, 5)
            b = B["P1"]
            for ic in range(NIC):
                iw = ICW[ic]
                for tb in range(NTB):
                    g = ic * NTB + tb
                    sc.wait_ge(tm, b["tm"] + g + 1)
                    if g >= 4:
                        sc.wait_ge(qt[g % 4], 16 * (g // 4))
                    sc.activation(stage_s[g % 4][:, 0:iw],
                                  pB[:, (g % 2) * 512:(g % 2) * 512 + iw],
                                  AF.Silu).then_inc(sm, 1)
            sc.drain().then_inc(ph, 1)
            sc.wait_ge(ph, 10)
            b = B["P3"]
            for ic in range(NIC):
                iw = ICW[ic]
                for tb in range(NTB):
                    g = ic * NTB + tb
                    sc.wait_ge(tm, b["tm"] + g + 1)
                    if g >= 4:
                        sc.wait_ge(qt[g % 4], 16 * (NG // 4 + g // 4))
                    sc.copy(stage_u[g % 4][:, 0:iw],
                            pB[:, (g % 2) * 512:(g % 2) * 512 + iw]).then_inc(sm, 1)
            sc.drain().then_inc(ph, 1)
            sc.wait_ge(ph, 15)
            b = B["P4"]
            for tb in range(NTB):
                sc.wait_ge(vm, b["vm"] + 5 * tb + 1)
                sc.activation(exps[:], pB[:, 512:520], AF.Exp, bias=nmx[:, 0:1],
                              accum_out=sume[:, 0:1]).then_inc(sm, 1)
            sc.drain().then_inc(ph, 1)
            sc.wait_ge(ph, 20)
            b = B["P2"]
            ITb = 2 + MED_ITERS
            for tb in range(NTB):
                vmt = b["vm"] + ITb * tb
                sc.wait_ge(qx[tb % 2], 16 * (NTB // 2 + tb // 2 + 1))
                if tb >= 1:
                    sc.wait_ge(vm, b["vm"] + ITb * tb)
                sc.activation(a_sb[:], s_t[tb % 2][:], AF.Abs).then_inc(sm, 1)
                for k in range(MED_ITERS):
                    sc.wait_ge(vm, vmt + k + 1)
                    sc.activation(trash_m[:], a_sb[:], AF.Sign, bias=nthr[:, 0:1],
                                  accum_out=craw[:, 0:1]).then_inc(sm, 1)
            sc.drain().then_inc(ph, 1)
            sc.wait_ge(ph, 25)
            sc.drain().then_inc(ph, 1)

    return nc


_nc_cache = None


def kernel(L0_attn_x, L0_attn_w_gate, L0_attn_w_up, L0_attn_w_down,
           L0_ffn_A, L0_ffn_B, L0_ffn_Wr, L0_ffn_br):
    global _nc_cache
    x = np.ascontiguousarray(np.asarray(L0_attn_x, dtype=np.float32).reshape(16384, 4096))
    wgT = np.ascontiguousarray(np.asarray(L0_attn_w_gate, dtype=np.float32).T)
    wuT = np.ascontiguousarray(np.asarray(L0_attn_w_up, dtype=np.float32).T)
    wdT = np.ascontiguousarray(np.asarray(L0_attn_w_down, dtype=np.float32).T)
    aT = np.ascontiguousarray(np.asarray(L0_ffn_A, dtype=np.float32).reshape(128, 4096).T)
    bF = np.ascontiguousarray(
        np.asarray(L0_ffn_B, dtype=np.float32).transpose(0, 2, 1).reshape(128, 4096))
    wrT = np.ascontiguousarray(np.asarray(L0_ffn_Wr, dtype=np.float32).T)
    br1 = np.ascontiguousarray(np.asarray(L0_ffn_br, dtype=np.float32).reshape(1, 8))
    eye = np.eye(128, dtype=np.float32)

    if _nc_cache is None:
        _nc_cache = build_program()
    nc = _nc_cache

    in_maps = []
    for c in range(8):
        in_maps.append(dict(
            x_in=np.ascontiguousarray(x[c * 2048:(c + 1) * 2048]),
            wgT_in=wgT, wuT_in=wuT, wdT_in=wdT,
            aT_in=aT, bF_in=bF, wrT_in=wrT, br_in=br1, eye_in=eye))
    res = run_bass_kernel_spmd(nc, in_maps, core_ids=list(range(8)))
    y = np.concatenate([r["y_out"] for r in res.results], axis=0)
    return y.reshape(2, 8192, 4096)


# version 05
# speedup vs baseline: 1.0872x; 1.0872x over previous
import numpy as np
import concourse.bass as bass
import concourse.mybir as mybir
from concourse.bass_utils import run_bass_kernel_spmd
from contextlib import ExitStack

F32 = mybir.dt.float32
BF16 = mybir.dt.bfloat16
U8 = mybir.dt.uint8
AF = mybir.ActivationFunctionType
ALU = mybir.AluOpType
AX = mybir.AxisListType

P = 128
T = 2048
D = 4096
I = 11008
ND = D // P
NTB = T // P
NIT = I // P
K_KEEP = I // 2
IC = 512
ICB = list(range(0, I, IC))
ICW = [min(IC, I - s0) for s0 in ICB]
NIC = len(ICB)
NHC = D // 512
NQ = 4
TQ = T // NQ
NTBQ = TQ // P

MED_ITERS = 10
MED_BISECT = 3
MED_HI = 2.56
MED_THR0 = 0.27
MED_CLAMP = 0.02
LN_EPS = 1.0e-5

NG = NIC * NTB


def build_program():
    nc = bass.Bass("TRN2", target_bir_lowering=False, detect_race_conditions=False)

    x_in = nc.dram_tensor("x_in", [T, D], F32, kind="ExternalInput").ap()
    wgT_in = nc.dram_tensor("wgT_in", [D, I], F32, kind="ExternalInput").ap()
    wuT_in = nc.dram_tensor("wuT_in", [D, I], F32, kind="ExternalInput").ap()
    wdT_in = nc.dram_tensor("wdT_in", [I, D], F32, kind="ExternalInput").ap()
    aT_in = nc.dram_tensor("aT_in", [D, 128], F32, kind="ExternalInput").ap()
    bF_in = nc.dram_tensor("bF_in", [128, D], F32, kind="ExternalInput").ap()
    wrT_in = nc.dram_tensor("wrT_in", [D, 8], F32, kind="ExternalInput").ap()
    br_in = nc.dram_tensor("br_in", [1, 8], F32, kind="ExternalInput").ap()
    eye_in = nc.dram_tensor("eye_in", [128, 128], F32, kind="ExternalInput").ap()
    y_out = nc.dram_tensor("y_out", [T, D], F32, kind="ExternalOutput").ap()

    _SPK = "ExternalOutput" if globals().get("DEBUG_SPILL") else "Internal"
    s_sp = nc.dram_tensor("s_sp", [T, I], F32, kind=_SPK).ap()
    u_sp = nc.dram_tensor("u_sp", [T, I], BF16, kind=_SPK).ap()
    sm_sp = nc.dram_tensor("sm_sp", [T, I], BF16, kind=_SPK).ap()
    ffn_sp = nc.dram_tensor("ffn_sp", [T, D], BF16, kind=_SPK).ap()

    wgT_t = wgT_in.rearrange("(nd p) i -> p nd i", p=P)
    wuT_t = wuT_in.rearrange("(nd p) i -> p nd i", p=P)
    aT_t = aT_in.rearrange("(nd p) c -> p nd c", p=P)
    wrT_t = wrT_in.rearrange("(nd p) c -> p nd c", p=P)

    with ExitStack() as ctx:
        e = ctx.enter_context

        eyeF = e(nc.sbuf_tensor("eyeF", [128, 128], F32))
        eyeB = e(nc.sbuf_tensor("eyeB", [128, 128], BF16))
        ones1 = e(nc.sbuf_tensor("ones1", [1, 128], BF16))
        br_sb = e(nc.sbuf_tensor("br_sb", [1, 8], BF16))

        def st(name):
            return e(nc.sbuf_tensor(name, [128, 1], F32))
        sum1, mean, ssq, var, ve, sq, r0, t1, t2, t3, rstd, eps_t = (
            st(n) for n in ["sum1", "mean", "ssq", "var", "ve", "sq", "r0",
                            "t1", "t2", "t3", "rstd", "eps_t"])
        loS = [[st(f"lo{pi}_{j}") for j in range(2)] for pi in range(2)]
        hiS = [[st(f"hi{pi}_{j}") for j in range(2)] for pi in range(2)]
        cloS = [[st(f"clo{pi}_{j}") for j in range(2)] for pi in range(2)]
        chiS = [[st(f"chi{pi}_{j}") for j in range(2)] for pi in range(2)]
        thrP = [st("thr0"), st("thr1")]
        nthrP = [st("nthr0"), st("nthr1")]
        crawP = [st("craw0"), st("craw1")]
        mhi = e(nc.sbuf_tensor("mhi", [128, 1], U8))
        mlo = e(nc.sbuf_tensor("mlo", [128, 1], U8))
        den, num, rq, dd, rd, marg, lob, hib, tmp1 = (
            st(n) for n in ["den", "num", "rq", "dd", "rd", "marg", "lob", "hib", "tmp1"])
        mx, nmx, sume, rs = (st(n) for n in ["mx", "nmx", "sume", "rs"])
        exps = e(nc.sbuf_tensor("exps", [128, 8], F32))
        sco = e(nc.sbuf_tensor("sco", [128, 8], F32))
        ones16 = e(nc.sbuf_tensor("ones16", [128, 16], F32))

        ARENA = 205824
        arena = e(nc.sbuf_tensor("arena", [128, ARENA], U8))
        base = nc.lookup_mloc(arena).addr

        def place(name, shape, dtype, off):
            assert off + int(np.prod(shape[1:])) * mybir.dt.size(dtype) <= ARENA, name
            return nc.alloc_sbuf_tensor_at(name, shape, dtype, offset=base + off)

        xnT = place("xnT", [128, ND * T], BF16, 0)
        xb = [place(f"xb{j}", [128, D], F32, 131072 + j * 16384) for j in range(2)]
        xn_sb = place("xn_sb", [128, D], F32, 163840)
        trash_sq = place("trash_sq", [128, D], F32, 180224)
        wslab = [place(f"wslab{j}", [128, ND * IC], BF16, 131072 + j * 32768)
                 for j in range(2)]
        stage_s = [place(f"stage_s{j}", [128, IC], F32, 196608 + j * 2048)
                   for j in range(4)]
        stage_u = [place(f"stage_u{j}", [128, IC], BF16, 196608 + j * 1024)
                   for j in range(4)]
        aT_sb = place("aT_sb", [128, ND * 128], BF16, 131072)
        wrT_sb = place("wrT_sb", [128, ND * 8], BF16, 139264)
        bF_sb = place("bF_sb", [128, D], BF16, 139776)
        ffnst = [place(f"ffnst{j}", [128, 2048], BF16, 147968 + j * 4096)
                 for j in range(2)]
        z_sb = place("z_sb", [128, 128], F32, 156160)
        zT_sb = place("zT_sb", [128, 128], BF16, 156672)
        srep = place("srep", [128, 128], F32, 156928)
        s_t = [place(f"s_t{j}", [128, I], F32, j * 44032) for j in range(2)]
        aP = [place(f"aP{j}", [128, I], BF16, 88064 + j * 22016) for j in range(2)]
        sm0_sb = [place(f"sm0_sb{j}", [128, I], BF16, 132096 + j * 22016)
                  for j in range(2)]
        trash_m = place("trash_m", [128, I], BF16, 176128)
        smtq = place("smtq", [128, NIT * 512], BF16, 0)
        smt_in = [place(f"smt_in{j}", [128, 512], BF16, 88064 + j * 1024) for j in range(4)]
        u_in = [place(f"u_in{j}", [128, 512], BF16, 92160 + j * 1024) for j in range(4)]
        sprod = [place(f"sprod{j}", [128, 512], BF16, 96256 + j * 1024) for j in range(4)]
        wdring = [place(f"wdring{j}", [128, 4 * 512], BF16, 100352 + j * 4096) for j in range(4)]
        x_t = [place(f"x_t{j}", [128, 512], F32, 116736 + j * 2048) for j in range(4)]
        ffn_t = [place(f"ffn_t{j}", [128, 512], F32, 124928 + j * 2048) for j in range(4)]
        t_a = [place(f"t_a{j}", [128, 512], F32, 133120 + j * 2048) for j in range(2)]
        outst = [place(f"outst{j}", [128, 512], F32, 137216 + j * 2048) for j in range(4)]

        pA = nc.alloc_psum_tensor("pA", [128, 2048], F32)
        pB = nc.alloc_psum_tensor("pB", [128, 1024], F32)
        pC = nc.alloc_psum_tensor("pC", [128, 1024], BF16)
        pD = nc.alloc_psum_tensor("pD", [128, 1024], BF16)

        ph = e(nc.semaphore("ph"))
        tm = e(nc.semaphore("tm"))
        vm = e(nc.semaphore("vm"))
        sm = e(nc.semaphore("sm"))
        vv = e(nc.semaphore("vv"))
        qe = e(nc.semaphore("qe"))
        qx = [e(nc.semaphore(f"qx{j}")) for j in range(2)]
        qs = [e(nc.semaphore(f"qs{j}")) for j in range(2)]
        qt = [e(nc.semaphore(f"qt{j}")) for j in range(4)]
        qmP = [e(nc.semaphore(f"qm{j}")) for j in range(2)]
        qc = e(nc.semaphore("qc"))
        qff = [e(nc.semaphore(f"qff{j}")) for j in range(2)]
        qp = [e(nc.semaphore(f"qp{j}")) for j in range(4)]
        qu = [e(nc.semaphore(f"qu{j}")) for j in range(4)]
        qd = [e(nc.semaphore(f"qd{j}")) for j in range(4)]
        qxt = [e(nc.semaphore(f"qxt{j}")) for j in range(4)]
        qf = [e(nc.semaphore(f"qf{j}")) for j in range(4)]
        qout = [e(nc.semaphore(f"qout{j}")) for j in range(4)]

        D_ = {
            "P0": dict(tm=4 * NTB, vm=2 * NTB, sm=5 * NTB),
            "P1": dict(tm=NG, vm=0, sm=NG),
            "P3": dict(tm=NG, vm=0, sm=NG),
            "P4": dict(tm=4 * NTB, vm=5 * NTB, sm=NTB),
            "P2": dict(tm=0, vm=(MED_ITERS + 2) * NTB, sm=(1 + MED_ITERS) * NTB),
            "P5": dict(tm=NQ * (NIT + NIC * NHC), vm=NQ * (2 * NIT + NHC * NTBQ), sm=0),
        }
        PH_ORDER = ["P0", "P1", "P3", "P4", "P2", "P5"]
        B = {}
        run = dict(tm=0, vm=0, sm=0)
        for pn in PH_ORDER:
            B[pn] = dict(run)
            for k in run:
                run[k] += D_[pn][k]

        xnT_v = xnT.ap().rearrange("p (nd t) -> p nd t", t=T)

        def xnT_tile(nd, tb):
            return xnT_v[:, nd, tb * P:(tb + 1) * P]

        TMQ = NIT + NIC * NHC
        VMQ = 2 * NIT + NHC * NTBQ

        blk = e(nc.Block())

        @blk.sync
        def _(sync):
            b = B["P0"]
            sync.dma_start(out=eyeF[:], in_=eye_in).then_inc(qe, 16)
            for tb in range(NTB):
                if tb >= 2:
                    sync.wait_ge(vm, b["vm"] + 2 * (tb - 2) + 2)
                sync.dma_start(out=xb[tb % 2][:], in_=x_in[tb * P:(tb + 1) * P, :]
                               ).then_inc(qx[tb % 2], 16)
            sync.nop().then_inc(ph, 1)
            sync.wait_ge(ph, 5)
            b = B["P1"]
            for ic in range(NIC):
                i0, iw = ICB[ic], ICW[ic]
                for tb in range(NTB):
                    g = ic * NTB + tb
                    sync.wait_ge(sm, b["sm"] + g + 1)
                    sync.dma_start(out=s_sp[tb * P:(tb + 1) * P, i0:i0 + iw],
                                   in_=stage_s[g % 4][:, 0:iw]).then_inc(qt[g % 4], 16)
            for j in range(4):
                sync.wait_ge(qt[j], 16 * (NG // 4))
            sync.nop().then_inc(ph, 1)
            sync.wait_ge(ph, 10)
            b = B["P3"]
            for ic in range(NIC):
                i0, iw = ICB[ic], ICW[ic]
                for tb in range(NTB):
                    g = ic * NTB + tb
                    sync.wait_ge(sm, b["sm"] + g + 1)
                    sync.dma_start(out=u_sp[tb * P:(tb + 1) * P, i0:i0 + iw],
                                   in_=stage_u[g % 4][:, 0:iw]).then_inc(qt[g % 4], 16)
            for j in range(4):
                sync.wait_ge(qt[j], 16 * (2 * NG // 4))
            sync.nop().then_inc(ph, 1)
            sync.wait_ge(ph, 15)
            b = B["P4"]
            for tb in range(NTB):
                for h2 in range(2):
                    sync.wait_ge(vm, b["vm"] + 5 * tb + 4 + h2)
                    sync.dma_start(out=ffn_sp[tb * P:(tb + 1) * P,
                                              h2 * 2048:(h2 + 1) * 2048],
                                   in_=ffnst[h2][:]).then_inc(qff[h2], 16)
            for j in range(2):
                sync.wait_ge(qff[j], 16 * NTB)
            sync.nop().then_inc(ph, 1)
            sync.wait_ge(ph, 20)
            b = B["P2"]
            VMP = 2 * MED_ITERS + 4
            NPAIR = NTB // 2
            for p_ in range(NPAIR):
                for pi in range(2):
                    tb = 2 * p_ + pi
                    if p_ >= 1:
                        sync.wait_ge(vm, b["vm"] + (p_ - 1) * VMP + 2 * MED_ITERS + 3 + pi)
                    sync.dma_start(out=s_t[pi][:], in_=s_sp[tb * P:(tb + 1) * P, :]
                                   ).then_inc(qx[pi], 16)
                for pi in range(2):
                    tb = 2 * p_ + pi
                    sync.wait_ge(vm, b["vm"] + p_ * VMP + 2 * MED_ITERS + 3 + pi)
                    sync.dma_start(out=sm_sp[tb * P:(tb + 1) * P, :], in_=sm0_sb[pi][:]
                                   ).then_inc(qmP[pi], 16)
            for pi in range(2):
                sync.wait_ge(qmP[pi], 16 * NPAIR)
            sync.nop().then_inc(ph, 1)
            sync.wait_ge(ph, 25)
            b = B["P5"]
            for q in range(NQ):
                r0_ = q * TQ
                vmq = b["vm"] + q * VMQ
                for it in range(NIT):
                    if it >= 4:
                        sync.wait_ge(vm, vmq + 2 * (it - 4) + 1)
                    src_s = sm_sp[r0_:r0_ + TQ, it * P:(it + 1) * P]
                    src_u = u_sp[r0_:r0_ + TQ, it * P:(it + 1) * P]
                    sync.dma_start(out=smt_in[it % 4].ap().rearrange(
                                       "p (tt i) -> p tt i", i=P),
                                   in_=src_s.rearrange("(tt p) i -> p tt i", p=P)
                                   ).then_inc(qp[it % 4], 16)
                    sync.dma_start(out=u_in[it % 4].ap().rearrange(
                                       "p (tt i) -> p tt i", i=P),
                                   in_=src_u.rearrange("(tt p) i -> p tt i", p=P)
                                   ).then_inc(qu[it % 4], 16)
                NE = NHC * NTBQ
                def _xload(ev):
                    hc_, tb_ = divmod(ev, NTBQ)
                    if ev >= 4:
                        sync.wait_ge(vm, vmq + 2 * NIT + ev - 4 + 1)
                    sync.dma_start(
                        out=x_t[ev % 4][:],
                        in_=x_in[r0_ + tb_ * P:r0_ + (tb_ + 1) * P,
                                 hc_ * 512:(hc_ + 1) * 512]).then_inc(qxt[ev % 4], 16)
                _xload(0)
                _xload(1)
                for ev in range(NE):
                    if ev + 2 < NE:
                        _xload(ev + 2)
                    hc_, tb_ = divmod(ev, NTBQ)
                    sync.wait_ge(vm, vmq + 2 * NIT + ev + 1)
                    sync.dma_start(
                        out=y_out[r0_ + tb_ * P:r0_ + (tb_ + 1) * P,
                                  hc_ * 512:(hc_ + 1) * 512],
                        in_=outst[ev % 4][:]).then_inc(qout[ev % 4], 16)
            for j in range(4):
                sync.wait_ge(qout[j], 16 * (NQ * NHC * NTBQ // 4))
            sync.nop().then_inc(ph, 1)

        @blk.gpsimd
        def _(gp):
            gp.nop().then_inc(ph, 1)
            gp.wait_ge(ph, 5)
            b = B["P1"]
            for ic in range(NIC):
                i0, iw = ICB[ic], ICW[ic]
                if ic >= 2:
                    gp.wait_ge(tm, b["tm"] + (ic - 1) * NTB)
                dst = wslab[ic % 2].ap().rearrange("p (nd i) -> p nd i", i=IC)[:, :, 0:iw]
                gp.dma_start(out=dst, in_=wgT_t[:, :, i0:i0 + iw]).then_inc(qs[ic % 2], 16)
            gp.nop().then_inc(ph, 1)
            gp.wait_ge(ph, 10)
            b = B["P3"]
            for ic in range(NIC):
                i0, iw = ICB[ic], ICW[ic]
                if ic >= 2:
                    gp.wait_ge(tm, b["tm"] + (ic - 1) * NTB)
                dst = wslab[ic % 2].ap().rearrange("p (nd i) -> p nd i", i=IC)[:, :, 0:iw]
                gp.dma_start(out=dst, in_=wuT_t[:, :, i0:i0 + iw]).then_inc(qs[ic % 2], 16)
            gp.nop().then_inc(ph, 1)
            gp.wait_ge(ph, 15)
            gp.dma_start(out=aT_sb.ap().rearrange("p (nd c) -> p nd c", c=128),
                         in_=aT_t).then_inc(qc, 16)
            gp.dma_start(out=wrT_sb.ap().rearrange("p (nd c) -> p nd c", c=8),
                         in_=wrT_t).then_inc(qc, 16)
            gp.dma_start(out=bF_sb[:], in_=bF_in).then_inc(qc, 16)
            gp.dma_start(out=br_sb[:], in_=br_in).then_inc(qc, 16)
            gp.dma_start(out=eyeB[:], in_=eye_in).then_inc(qc, 16)
            gp.nop().then_inc(ph, 1)
            gp.wait_ge(ph, 20)
            gp.nop().then_inc(ph, 1)
            gp.wait_ge(ph, 25)
            b = B["P5"]
            for q in range(NQ):
                r0_ = q * TQ
                tmq = b["tm"] + q * TMQ
                vmq = b["vm"] + q * VMQ
                for hc in range(NHC):
                    for grp in range(NIC):
                        gi0 = grp * 4
                        gw = min(4, NIT - gi0)
                        gidx = hc * NIC + grp
                        if gidx >= 4:
                            gp.wait_ge(tm, tmq + NIT + gidx - 4 + 1)
                        elif q >= 1:
                            gp.wait_ge(tm, tmq - TMQ + NIT + (NHC * NIC - 4 + gidx) + 1)
                        src = wdT_in[gi0 * P:gi0 * P + gw * P, hc * 512:(hc + 1) * 512]
                        gp.dma_start(
                            out=wdring[gidx % 4].ap().rearrange(
                                "p (g h) -> p g h", h=512)[:, 0:gw, :],
                            in_=src.rearrange("(g p) h -> p g h", p=P)
                        ).then_inc(qd[gidx % 4], 16)
                    for tb in range(NTBQ):
                        ev = hc * NTBQ + tb
                        if ev >= 4:
                            gp.wait_ge(vm, vmq + 2 * NIT + ev - 4 + 1)
                        elif q >= 1:
                            gp.wait_ge(vm, vmq - VMQ + 2 * NIT + (NHC * NTBQ - 4 + ev) + 1)
                        gp.dma_start(
                            out=ffn_t[ev % 4][:],
                            in_=ffn_sp[r0_ + tb * P:r0_ + (tb + 1) * P,
                                       hc * 512:(hc + 1) * 512]).then_inc(qf[ev % 4], 16)
            gp.nop().then_inc(ph, 1)

        @blk.tensor
        def _(t):
            b = B["P0"]
            t.wait_ge(qe, 16)
            pregs = [pA[:, 0:1024], pA[:, 1024:2048], pB[:, 0:1024]]
            for tb in range(NTB):
                t.wait_ge(vm, b["vm"] + 2 * tb + 2)
                for bt in range(4):
                    gb = 4 * tb + bt
                    if gb >= 3:
                        pt, pb_ = divmod(gb - 3, 4)
                        t.wait_ge(sm, b["sm"] + 5 * pt + 2 + pb_)
                    reg = pregs[gb % 3]
                    for j in range(8):
                        nd = 8 * bt + j
                        mi = t.transpose(reg[:, j * 128:(j + 1) * 128],
                                         xn_sb[:, nd * 128:(nd + 1) * 128], eyeF[:])
                    mi.then_inc(tm, 1)
            t.drain().then_inc(ph, 1)
            t.wait_ge(ph, 5)
            b = B["P1"]
            for ic in range(NIC):
                iw = ICW[ic]
                t.wait_ge(qs[ic % 2], 16 * (ic // 2 + 1))
                sl = wslab[ic % 2].ap().rearrange("p (nd i) -> p nd i", i=IC)
                for tb in range(NTB):
                    g = ic * NTB + tb
                    pb_ = pB[:, (g % 2) * 512:(g % 2) * 512 + iw]
                    if g >= 2:
                        t.wait_ge(sm, b["sm"] + g - 1)
                    for nd in range(ND):
                        mi = t.matmul(pb_, xnT_tile(nd, tb), sl[:, nd, 0:iw],
                                      start=(nd == 0), stop=(nd == ND - 1))
                    mi.then_inc(tm, 1)
            t.drain().then_inc(ph, 1)
            t.wait_ge(ph, 10)
            b = B["P3"]
            for ic in range(NIC):
                iw = ICW[ic]
                t.wait_ge(qs[ic % 2], 16 * (NIC // 2 + ic // 2 + 1))
                sl = wslab[ic % 2].ap().rearrange("p (nd i) -> p nd i", i=IC)
                for tb in range(NTB):
                    g = ic * NTB + tb
                    pb_ = pB[:, (g % 2) * 512:(g % 2) * 512 + iw]
                    if g >= 2:
                        t.wait_ge(sm, b["sm"] + g - 1)
                    for nd in range(ND):
                        mi = t.matmul(pb_, xnT_tile(nd, tb), sl[:, nd, 0:iw],
                                      start=(nd == 0), stop=(nd == ND - 1))
                    mi.then_inc(tm, 1)
            t.drain().then_inc(ph, 1)
            t.wait_ge(ph, 15)
            b = B["P4"]
            t.wait_ge(qc, 16 * 5)
            wr_v = wrT_sb.ap().rearrange("p (nd c) -> p nd c", c=8)
            aT_v = aT_sb.ap().rearrange("p (nd c) -> p nd c", c=128)
            for tb in range(NTB):
                if tb >= 1:
                    t.wait_ge(vm, b["vm"] + 5 * (tb - 1) + 5)
                for nd in range(ND):
                    t.matmul(pB[:, 0:128], xnT_tile(nd, tb), aT_v[:, nd, :],
                             start=(nd == 0), stop=(nd == ND - 1))
                    t.matmul(pB[:, 512:520], xnT_tile(nd, tb), wr_v[:, nd, :],
                             start=(nd == 0), stop=False)
                mi = t.matmul(pB[:, 512:520], ones1[:, :], br_sb[:, :],
                              start=False, stop=True)
                mi.then_inc(tm, 1)
                t.wait_ge(vm, b["vm"] + 5 * tb + 2)
                t.transpose(pB[:, 128:256], z_sb[:], eyeF[:]).then_inc(tm, 1)
                t.wait_ge(vm, b["vm"] + 5 * tb + 3)
                for h2 in range(2):
                    if h2 == 1:
                        t.wait_ge(vm, b["vm"] + 5 * tb + 4)
                    for k4 in range(4):
                        mi = t.matmul(pA[:, k4 * 512:(k4 + 1) * 512], zT_sb[:],
                                      bF_sb[:, h2 * 2048 + k4 * 512:
                                            h2 * 2048 + (k4 + 1) * 512],
                                      start=True, stop=True)
                    mi.then_inc(tm, 1)
            t.drain().then_inc(ph, 1)
            t.wait_ge(ph, 20)
            t.drain().then_inc(ph, 1)
            t.wait_ge(ph, 25)
            b = B["P5"]
            for q in range(NQ):
                vmq = b["vm"] + q * VMQ
                tmq = b["tm"] + q * TMQ
                pslots = [pC[:, 0:512], pC[:, 512:1024], pD[:, 0:512], pD[:, 512:1024]]
                for it in range(NIT):
                    t.wait_ge(vm, vmq + 2 * it + 1)
                    if it >= 4:
                        t.wait_ge(vm, vmq + 2 * (it - 4) + 2)
                    reg = pslots[it % 4]
                    for j in range(4):
                        mi = t.transpose(reg[:, j * 128:(j + 1) * 128],
                                         sprod[it % 4][:, j * 128:(j + 1) * 128],
                                         eyeB[:])
                    mi.then_inc(tm, 1)
                t.wait_ge(vm, vmq + 2 * NIT)
                smtq_v = smtq.ap().rearrange("p (it t) -> p it t", t=512)
                for hc in range(NHC):
                    if hc >= 1:
                        t.wait_ge(vm, vmq + 2 * NIT + hc * NTBQ)
                    elif q >= 1:
                        t.wait_ge(vm, vmq - VMQ + 2 * NIT + NHC * NTBQ)
                    for grp in range(NIC):
                        gi0 = grp * 4
                        gw = min(4, NIT - gi0)
                        gidx = hc * NIC + grp
                        t.wait_ge(qd[gidx % 4],
                                  16 * (q * (NHC * NIC // 4) + gidx // 4 + 1))
                        wdv = wdring[gidx % 4].ap().rearrange("p (g h) -> p g h", h=512)
                        for j in range(gw):
                            it = gi0 + j
                            for tb in range(NTBQ):
                                mi = t.matmul(pA[:, tb * 512:(tb + 1) * 512],
                                              smtq_v[:, it, tb * P:(tb + 1) * P],
                                              wdv[:, j, :],
                                              start=(it == 0), stop=(it == NIT - 1))
                        mi.then_inc(tm, 1)
            t.drain().then_inc(ph, 1)

        @blk.vector
        def _(v):
            vvc = [0]
            def edge(ins):
                vvc[0] += 1
                ins.then_inc(vv, 1)
                v.wait_ge(vv, vvc[0])
            b = B["P0"]
            v.memset(ones16[:], 1.0)
            v.memset(ones1[:], 1.0)
            v.memset(eps_t[:, 0:1], LN_EPS)
            for tb in range(NTB):
                xbuf = xb[tb % 2]
                v.wait_ge(qx[tb % 2], 16 * (tb // 2 + 1))
                edge(v.reduce_sum(sum1[:, 0:1], xbuf[:], axis=AX.X))
                edge(v.tensor_scalar_mul(mean[:, 0:1], sum1[:, 0:1], 1.0 / D))
                v.scalar_tensor_tensor(out=trash_sq[:], in0=xbuf[:], scalar=mean[:, 0:1],
                                       in1=xbuf[:], op0=ALU.subtract, op1=ALU.mult)
                edge(v.reduce_sum(ssq[:, 0:1], trash_sq[:], axis=AX.X))
                edge(v.tensor_scalar_mul(var[:, 0:1], ssq[:, 0:1], 1.0 / D))
                v.tensor_scalar_add(ve[:, 0:1], var[:, 0:1], LN_EPS).then_inc(vm, 1)
                v.wait_ge(sm, b["sm"] + 5 * tb + 1)
                edge(v.reciprocal(r0[:, 0:1], sq[:, 0:1]))
                edge(v.tensor_mul(t1[:, 0:1], r0[:, 0:1], r0[:, 0:1]))
                edge(v.tensor_mul(t2[:, 0:1], ve[:, 0:1], t1[:, 0:1]))
                edge(v.tensor_scalar(out=t3[:, 0:1], in0=t2[:, 0:1], scalar1=-0.5,
                                     scalar2=1.5, op0=ALU.mult, op1=ALU.add))
                edge(v.tensor_mul(rstd[:, 0:1], r0[:, 0:1], t3[:, 0:1]))
                if tb >= 1:
                    v.wait_ge(tm, b["tm"] + 4 * tb)
                v.tensor_scalar(out=xn_sb[:], in0=xbuf[:], scalar1=mean[:, 0:1],
                                scalar2=rstd[:, 0:1], op0=ALU.subtract,
                                op1=ALU.mult).then_inc(vm, 1)
            v.drain().then_inc(ph, 1)
            v.wait_ge(ph, 5)
            v.drain().then_inc(ph, 1)
            v.wait_ge(ph, 10)
            v.drain().then_inc(ph, 1)
            v.wait_ge(ph, 15)
            b = B["P4"]
            for tb in range(NTB):
                v.wait_ge(tm, b["tm"] + 4 * tb + 1)
                edge(v.reduce_max(mx[:, 0:1], pB[:, 512:520], axis=AX.X))
                v.tensor_scalar_mul(nmx[:, 0:1], mx[:, 0:1], -1.0).then_inc(vm, 1)
                v.wait_ge(sm, b["sm"] + tb + 1)
                edge(v.reciprocal(rs[:, 0:1], sume[:, 0:1]))
                edge(v.tensor_scalar(out=sco[:], in0=exps[:], scalar1=rs[:, 0:1],
                                     scalar2=16.0, op0=ALU.mult, op1=ALU.mult))
                for h in range(8):
                    ins = v.tensor_scalar_mul(srep[:, h * 16:(h + 1) * 16], ones16[:],
                                              sco[:, h:h + 1])
                edge(ins)
                v.tensor_mul(z_sb[:], pB[:, 0:128], srep[:]).then_inc(vm, 1)
                v.wait_ge(tm, b["tm"] + 4 * tb + 2)
                v.tensor_copy(zT_sb[:], pB[:, 128:256]).then_inc(vm, 1)
                for h2 in range(2):
                    v.wait_ge(tm, b["tm"] + 4 * tb + 3 + h2)
                    if tb >= 1:
                        v.wait_ge(qff[h2], 16 * tb)
                    v.tensor_copy(ffnst[h2][:], pA[:, 0:2048]).then_inc(vm, 1)
            v.drain().then_inc(ph, 1)
            v.wait_ge(ph, 20)
            b = B["P2"]
            VMP = 2 * MED_ITERS + 4
            SMP = 2 * MED_ITERS + 2
            NPAIR = NTB // 2
            for p_ in range(NPAIR):
                vmp = b["vm"] + p_ * VMP
                smp = b["sm"] + p_ * SMP
                for pi in range(2):
                    v.memset(loS[pi][0][:, 0:1], 0.0)
                    v.memset(hiS[pi][0][:, 0:1], MED_HI)
                    v.memset(cloS[pi][0][:, 0:1], float(I))
                    v.memset(chiS[pi][0][:, 0:1], -float(I))
                    v.memset(thrP[pi][:, 0:1], MED_THR0)
                    v.memset(nthrP[pi][:, 0:1], -MED_THR0).then_inc(vm, 1)
                for k in range(MED_ITERS):
                    for pi in range(2):
                        thr = thrP[pi]
                        nthr = nthrP[pi]
                        craw = crawP[pi]
                        cs, ns = k % 2, (k + 1) % 2
                        v.wait_ge(sm, smp + 2 * k + 3 + pi)
                        edge(v.tensor_single_scalar(mhi[:, 0:1], craw[:, 0:1], 0.0,
                                                    op=ALU.is_lt))
                        v.tensor_single_scalar(mlo[:, 0:1], craw[:, 0:1], 0.0,
                                               op=ALU.is_ge)
                        edge(v.select(hiS[pi][ns][:, 0:1], mhi[:, 0:1], thr[:, 0:1],
                                      hiS[pi][cs][:, 0:1]))
                        v.select(chiS[pi][ns][:, 0:1], mhi[:, 0:1], craw[:, 0:1],
                                 chiS[pi][cs][:, 0:1])
                        v.select(loS[pi][ns][:, 0:1], mlo[:, 0:1], thr[:, 0:1],
                                 loS[pi][cs][:, 0:1])
                        edge(v.select(cloS[pi][ns][:, 0:1], mlo[:, 0:1], craw[:, 0:1],
                                      cloS[pi][cs][:, 0:1]))
                        if k < MED_BISECT:
                            edge(v.tensor_add(tmp1[:, 0:1], loS[pi][ns][:, 0:1],
                                              hiS[pi][ns][:, 0:1]))
                            edge(v.tensor_scalar_mul(thr[:, 0:1], tmp1[:, 0:1], 0.5))
                        else:
                            v.tensor_sub(den[:, 0:1], cloS[pi][ns][:, 0:1],
                                         chiS[pi][ns][:, 0:1])
                            edge(v.tensor_sub(dd[:, 0:1], hiS[pi][ns][:, 0:1],
                                              loS[pi][ns][:, 0:1]))
                            v.reciprocal(tmp1[:, 0:1], den[:, 0:1])
                            edge(v.tensor_scalar_mul(marg[:, 0:1], dd[:, 0:1], MED_CLAMP))
                            v.tensor_mul(rq[:, 0:1], cloS[pi][ns][:, 0:1], tmp1[:, 0:1])
                            v.tensor_add(lob[:, 0:1], loS[pi][ns][:, 0:1], marg[:, 0:1])
                            edge(v.tensor_sub(hib[:, 0:1], hiS[pi][ns][:, 0:1],
                                              marg[:, 0:1]))
                            edge(v.tensor_mul(rd[:, 0:1], rq[:, 0:1], dd[:, 0:1]))
                            edge(v.tensor_add(thr[:, 0:1], loS[pi][ns][:, 0:1],
                                              rd[:, 0:1]))
                            edge(v.tensor_max(thr[:, 0:1], thr[:, 0:1], lob[:, 0:1]))
                            edge(v.tensor_tensor(thr[:, 0:1], thr[:, 0:1], hib[:, 0:1],
                                                 op=ALU.min))
                        v.tensor_scalar_mul(nthr[:, 0:1], thr[:, 0:1],
                                            -1.0).then_inc(vm, 1)
                for pi in range(2):
                    if p_ >= 1:
                        v.wait_ge(qmP[pi], 16 * p_)
                    v.scalar_tensor_tensor(out=sm0_sb[pi][:], in0=aP[pi][:],
                                           scalar=thrP[pi][:, 0:1],
                                           in1=s_t[pi][:], op0=ALU.is_gt,
                                           op1=ALU.mult).then_inc(vm, 1)
            v.drain().then_inc(ph, 1)
            v.wait_ge(ph, 25)
            b = B["P5"]
            for q in range(NQ):
                vmq = b["vm"] + q * VMQ
                tmq = b["tm"] + q * TMQ
                pslots = [pC[:, 0:512], pC[:, 512:1024], pD[:, 0:512], pD[:, 512:1024]]
                for it in range(NIT):
                    nslot = (NIT + 3 - (it % 4)) // 4
                    v.wait_ge(qp[it % 4], 16 * (q * nslot + it // 4 + 1))
                    v.wait_ge(qu[it % 4], 16 * (q * nslot + it // 4 + 1))
                    if it >= 4:
                        v.wait_ge(tm, tmq + it - 3)
                    v.tensor_mul(sprod[it % 4][:], smt_in[it % 4][:],
                                 u_in[it % 4][:]).then_inc(vm, 1)
                    v.wait_ge(tm, tmq + it + 1)
                    v.tensor_copy(smtq[:, it * 512:(it + 1) * 512],
                                  pslots[it % 4]).then_inc(vm, 1)
                for hc in range(NHC):
                    for tb in range(NTBQ):
                        ev = hc * NTBQ + tb
                        v.wait_ge(tm, tmq + NIT + (hc + 1) * NIC)
                        v.wait_ge(qxt[ev % 4],
                                  16 * (q * (NHC * NTBQ // 4) + ev // 4 + 1))
                        v.wait_ge(qf[ev % 4],
                                  16 * (q * (NHC * NTBQ // 4) + ev // 4 + 1))
                        v.tensor_add(t_a[ev % 2][:], pA[:, tb * 512:(tb + 1) * 512],
                                     x_t[ev % 4][:])
                        if ev >= 4:
                            v.wait_ge(qout[ev % 4],
                                      16 * (q * (NHC * NTBQ // 4) + ev // 4))
                        v.tensor_add(outst[ev % 4][:], t_a[ev % 2][:],
                                     ffn_t[ev % 4][:]).then_inc(vm, 1)
            v.drain().then_inc(ph, 1)

        @blk.scalar
        def _(sc):
            b = B["P0"]
            pregs = [pA[:, 0:1024], pA[:, 1024:2048], pB[:, 0:1024]]
            for tb in range(NTB):
                sc.wait_ge(vm, b["vm"] + 2 * tb + 1)
                sc.activation(sq[:, 0:1], var[:, 0:1], AF.Sqrt,
                              bias=eps_t[:, 0:1]).then_inc(sm, 1)
                for bt in range(4):
                    gb = 4 * tb + bt
                    sc.wait_ge(tm, b["tm"] + gb + 1)
                    src = pregs[gb % 3].rearrange("p (j t) -> p j t", t=128)
                    dst = xnT_v[:, 8 * bt:8 * bt + 8, tb * P:(tb + 1) * P]
                    sc.copy(dst, src).then_inc(sm, 1)
            sc.drain().then_inc(ph, 1)
            sc.wait_ge(ph, 5)
            b = B["P1"]
            for ic in range(NIC):
                iw = ICW[ic]
                for tb in range(NTB):
                    g = ic * NTB + tb
                    sc.wait_ge(tm, b["tm"] + g + 1)
                    if g >= 4:
                        sc.wait_ge(qt[g % 4], 16 * (g // 4))
                    sc.activation(stage_s[g % 4][:, 0:iw],
                                  pB[:, (g % 2) * 512:(g % 2) * 512 + iw],
                                  AF.Silu).then_inc(sm, 1)
            sc.drain().then_inc(ph, 1)
            sc.wait_ge(ph, 10)
            b = B["P3"]
            for ic in range(NIC):
                iw = ICW[ic]
                for tb in range(NTB):
                    g = ic * NTB + tb
                    sc.wait_ge(tm, b["tm"] + g + 1)
                    if g >= 4:
                        sc.wait_ge(qt[g % 4], 16 * (NG // 4 + g // 4))
                    sc.copy(stage_u[g % 4][:, 0:iw],
                            pB[:, (g % 2) * 512:(g % 2) * 512 + iw]).then_inc(sm, 1)
            sc.drain().then_inc(ph, 1)
            sc.wait_ge(ph, 15)
            b = B["P4"]
            for tb in range(NTB):
                sc.wait_ge(vm, b["vm"] + 5 * tb + 1)
                sc.activation(exps[:], pB[:, 512:520], AF.Exp, bias=nmx[:, 0:1],
                              accum_out=sume[:, 0:1]).then_inc(sm, 1)
            sc.drain().then_inc(ph, 1)
            sc.wait_ge(ph, 20)
            b = B["P2"]
            VMP = 2 * MED_ITERS + 4
            SMP = 2 * MED_ITERS + 2
            NPAIR = NTB // 2
            for p_ in range(NPAIR):
                smp = b["sm"] + p_ * SMP
                vmp = b["vm"] + p_ * VMP
                for pi in range(2):
                    sc.wait_ge(qx[pi], 16 * (NTB // 2 + p_ + 1))
                    sc.activation(aP[pi][:], s_t[pi][:], AF.Abs).then_inc(sm, 1)
                for k in range(MED_ITERS):
                    for pi in range(2):
                        sc.wait_ge(vm, vmp + 2 * k + 1 + pi)
                        sc.activation(trash_m[:], aP[pi][:], AF.Sign,
                                      bias=nthrP[pi][:, 0:1],
                                      accum_out=crawP[pi][:, 0:1]).then_inc(sm, 1)
            sc.drain().then_inc(ph, 1)
            sc.wait_ge(ph, 25)
            sc.drain().then_inc(ph, 1)

    return nc


_nc_cache = None


def kernel(L0_attn_x, L0_attn_w_gate, L0_attn_w_up, L0_attn_w_down,
           L0_ffn_A, L0_ffn_B, L0_ffn_Wr, L0_ffn_br):
    global _nc_cache
    x = np.ascontiguousarray(np.asarray(L0_attn_x, dtype=np.float32).reshape(16384, 4096))
    wgT = np.ascontiguousarray(np.asarray(L0_attn_w_gate, dtype=np.float32).T)
    wuT = np.ascontiguousarray(np.asarray(L0_attn_w_up, dtype=np.float32).T)
    wdT = np.ascontiguousarray(np.asarray(L0_attn_w_down, dtype=np.float32).T)
    aT = np.ascontiguousarray(np.asarray(L0_ffn_A, dtype=np.float32).reshape(128, 4096).T)
    bF = np.ascontiguousarray(
        np.asarray(L0_ffn_B, dtype=np.float32).transpose(0, 2, 1).reshape(128, 4096))
    wrT = np.ascontiguousarray(np.asarray(L0_ffn_Wr, dtype=np.float32).T)
    br1 = np.ascontiguousarray(np.asarray(L0_ffn_br, dtype=np.float32).reshape(1, 8))
    eye = np.eye(128, dtype=np.float32)

    if _nc_cache is None:
        _nc_cache = build_program()
    nc = _nc_cache

    in_maps = []
    for c in range(8):
        in_maps.append(dict(
            x_in=np.ascontiguousarray(x[c * 2048:(c + 1) * 2048]),
            wgT_in=wgT, wuT_in=wuT, wdT_in=wdT,
            aT_in=aT, bF_in=bF, wrT_in=wrT, br_in=br1, eye_in=eye))
    res = run_bass_kernel_spmd(nc, in_maps, core_ids=list(range(8)))
    y = np.concatenate([r["y_out"] for r in res.results], axis=0)
    return y.reshape(2, 8192, 4096)


# version 08
# speedup vs baseline: 1.1159x; 1.0264x over previous
import numpy as np
import concourse.bass as bass
import concourse.mybir as mybir
from concourse.bass_utils import run_bass_kernel_spmd
from contextlib import ExitStack

F32 = mybir.dt.float32
BF16 = mybir.dt.bfloat16
U8 = mybir.dt.uint8
AF = mybir.ActivationFunctionType
ALU = mybir.AluOpType
AX = mybir.AxisListType

P = 128
T = 2048
D = 4096
I = 11008
ND = D // P
NTB = T // P
NIT = I // P
K_KEEP = I // 2
IC = 512
ICB = list(range(0, I, IC))
ICW = [min(IC, I - s0) for s0 in ICB]
NIC = len(ICB)
NHC = D // 512
NQ = 4
TQ = T // NQ
NTBQ = TQ // P

MED_ITERS = 10
MED_BISECT = 3
MED_HI = 2.56
MED_THR0 = 0.27
MED_CLAMP = 0.02
LN_EPS = 1.0e-5

NG = NIC * NTB


def build_program():
    nc = bass.Bass("TRN2", target_bir_lowering=False, detect_race_conditions=False)

    x_in = nc.dram_tensor("x_in", [T, D], F32, kind="ExternalInput").ap()
    wgT_in = nc.dram_tensor("wgT_in", [D, I], F32, kind="ExternalInput").ap()
    wuT_in = nc.dram_tensor("wuT_in", [D, I], F32, kind="ExternalInput").ap()
    wdT_in = nc.dram_tensor("wdT_in", [I, D], F32, kind="ExternalInput").ap()
    aT_in = nc.dram_tensor("aT_in", [D, 128], F32, kind="ExternalInput").ap()
    bF_in = nc.dram_tensor("bF_in", [128, D], F32, kind="ExternalInput").ap()
    wrT_in = nc.dram_tensor("wrT_in", [D, 8], F32, kind="ExternalInput").ap()
    br_in = nc.dram_tensor("br_in", [1, 8], F32, kind="ExternalInput").ap()
    eye_in = nc.dram_tensor("eye_in", [128, 128], F32, kind="ExternalInput").ap()
    y_out = nc.dram_tensor("y_out", [T, D], F32, kind="ExternalOutput").ap()

    _SPK = "ExternalOutput" if globals().get("DEBUG_SPILL") else "Internal"
    s_sp = nc.dram_tensor("s_sp", [T, I], F32, kind=_SPK).ap()
    u_sp = nc.dram_tensor("u_sp", [T, I], BF16, kind=_SPK).ap()
    sm_sp = nc.dram_tensor("sm_sp", [T, I], BF16, kind=_SPK).ap()
    ffn_sp = nc.dram_tensor("ffn_sp", [T, D], BF16, kind=_SPK).ap()
    wd_bf = nc.dram_tensor("wd_bf", [I, D], BF16, kind="Internal").ap()

    wgT_t = wgT_in.rearrange("(nd p) i -> p nd i", p=P)
    wuT_t = wuT_in.rearrange("(nd p) i -> p nd i", p=P)
    aT_t = aT_in.rearrange("(nd p) c -> p nd c", p=P)
    wrT_t = wrT_in.rearrange("(nd p) c -> p nd c", p=P)

    with ExitStack() as ctx:
        e = ctx.enter_context

        eyeF = e(nc.sbuf_tensor("eyeF", [128, 128], F32))
        eyeB = e(nc.sbuf_tensor("eyeB", [128, 128], BF16))
        ones1 = e(nc.sbuf_tensor("ones1", [1, 128], BF16))
        br_sb = e(nc.sbuf_tensor("br_sb", [1, 8], BF16))

        def st(name):
            return e(nc.sbuf_tensor(name, [128, 1], F32))
        sum1, mean, ssq, var, ve, sq, r0, t1, t2, t3, rstd, eps_t = (
            st(n) for n in ["sum1", "mean", "ssq", "var", "ve", "sq", "r0",
                            "t1", "t2", "t3", "rstd", "eps_t"])
        loS = [[st(f"lo{pi}_{j}") for j in range(2)] for pi in range(2)]
        hiS = [[st(f"hi{pi}_{j}") for j in range(2)] for pi in range(2)]
        cloS = [[st(f"clo{pi}_{j}") for j in range(2)] for pi in range(2)]
        chiS = [[st(f"chi{pi}_{j}") for j in range(2)] for pi in range(2)]
        thrP = [st("thr0"), st("thr1")]
        nthrP = [st("nthr0"), st("nthr1")]
        crawP = [st("craw0"), st("craw1")]
        mhi = e(nc.sbuf_tensor("mhi", [128, 1], U8))
        mlo = e(nc.sbuf_tensor("mlo", [128, 1], U8))
        den, num, rq, dd, rd, marg, lob, hib, tmp1 = (
            st(n) for n in ["den", "num", "rq", "dd", "rd", "marg", "lob", "hib", "tmp1"])
        mx, nmx, sume, rs = (st(n) for n in ["mx", "nmx", "sume", "rs"])
        exps = e(nc.sbuf_tensor("exps", [128, 8], F32))
        sco = e(nc.sbuf_tensor("sco", [128, 8], F32))
        ones16 = e(nc.sbuf_tensor("ones16", [128, 16], F32))

        ARENA = 205824
        arena = e(nc.sbuf_tensor("arena", [128, ARENA], U8))
        base = nc.lookup_mloc(arena).addr

        def place(name, shape, dtype, off):
            assert off + int(np.prod(shape[1:])) * mybir.dt.size(dtype) <= ARENA, name
            return nc.alloc_sbuf_tensor_at(name, shape, dtype, offset=base + off)

        xnT = place("xnT", [128, ND * T], BF16, 0)
        xb = [place(f"xb{j}", [128, D], F32, 131072 + j * 16384) for j in range(2)]
        xn_sb = place("xn_sb", [128, D], F32, 163840)
        trash_sq = place("trash_sq", [128, D], F32, 180224)
        wslab = [place(f"wslab{j}", [128, ND * IC], BF16, 131072 + j * 32768)
                 for j in range(2)]
        stage_s = [place(f"stage_s{j}", [128, IC], F32, 196608 + j * 2048)
                   for j in range(4)]
        stage_u = [place(f"stage_u{j}", [128, IC], BF16, 196608 + j * 1024)
                   for j in range(4)]
        aT_sb = place("aT_sb", [128, ND * 128], BF16, 131072)
        wrT_sb = place("wrT_sb", [128, ND * 8], BF16, 139264)
        bF_sb = place("bF_sb", [128, D], BF16, 139776)
        ffnst = [place(f"ffnst{j}", [128, 2048], BF16, 147968 + j * 4096)
                 for j in range(2)]
        z_sb = place("z_sb", [128, 128], F32, 156160)
        zT_sb = place("zT_sb", [128, 128], BF16, 156672)
        srep = place("srep", [128, 128], F32, 156928)
        s_t = [place(f"s_t{j}", [128, I], F32, j * 44032) for j in range(2)]
        aP = [place(f"aP{j}", [128, I], BF16, 88064 + j * 22016) for j in range(2)]
        sm0_sb = [place(f"sm0_sb{j}", [128, I], BF16, 132096 + j * 22016)
                  for j in range(2)]
        trash_m = place("trash_m", [128, I], BF16, 176128)
        smtq = place("smtq", [128, NIT * 512], BF16, 0)
        smt_in = [place(f"smt_in{j}", [128, 512], BF16, 88064 + j * 1024) for j in range(4)]
        u_in = [place(f"u_in{j}", [128, 512], BF16, 92160 + j * 1024) for j in range(4)]
        sprod = [place(f"sprod{j}", [128, 512], BF16, 96256 + j * 1024) for j in range(4)]
        wdring = [place(f"wdring{j}", [128, 4 * 512], BF16, 100352 + j * 4096) for j in range(4)]
        x_t = [place(f"x_t{j}", [128, 512], F32, 116736 + j * 2048) for j in range(4)]
        ffn_t = [place(f"ffn_t{j}", [128, 512], F32, 124928 + j * 2048) for j in range(4)]
        t_a = [place(f"t_a{j}", [128, 512], F32, 133120 + j * 2048) for j in range(2)]
        outst = [place(f"outst{j}", [128, 512], F32, 137216 + j * 2048) for j in range(4)]

        pA = nc.alloc_psum_tensor("pA", [128, 2048], F32)
        pB = nc.alloc_psum_tensor("pB", [128, 1024], F32)
        pC = nc.alloc_psum_tensor("pC", [128, 1024], BF16)
        pD = nc.alloc_psum_tensor("pD", [128, 1024], BF16)

        ph = e(nc.semaphore("ph"))
        tm = e(nc.semaphore("tm"))
        vm = e(nc.semaphore("vm"))
        sm = e(nc.semaphore("sm"))
        vv = e(nc.semaphore("vv"))
        qe = e(nc.semaphore("qe"))
        qx = [e(nc.semaphore(f"qx{j}")) for j in range(2)]
        qs = [e(nc.semaphore(f"qs{j}")) for j in range(2)]
        qt = [e(nc.semaphore(f"qt{j}")) for j in range(4)]
        qmP = [e(nc.semaphore(f"qm{j}")) for j in range(2)]
        qc = e(nc.semaphore("qc"))
        qwc = e(nc.semaphore("qwc"))
        qff = [e(nc.semaphore(f"qff{j}")) for j in range(2)]
        qp = [e(nc.semaphore(f"qp{j}")) for j in range(4)]
        qu = [e(nc.semaphore(f"qu{j}")) for j in range(4)]
        qd = [e(nc.semaphore(f"qd{j}")) for j in range(4)]
        qxt = [e(nc.semaphore(f"qxt{j}")) for j in range(4)]
        qf = [e(nc.semaphore(f"qf{j}")) for j in range(4)]
        qout = [e(nc.semaphore(f"qout{j}")) for j in range(4)]

        D_ = {
            "P0": dict(tm=4 * NTB, vm=2 * NTB, sm=5 * NTB),
            "P1": dict(tm=NG, vm=0, sm=NG),
            "P3": dict(tm=NG, vm=0, sm=NG),
            "P4": dict(tm=4 * NTB, vm=5 * NTB, sm=NTB),
            "P2": dict(tm=0, vm=(MED_ITERS + 2) * NTB, sm=(1 + MED_ITERS) * NTB),
            "P5": dict(tm=NQ * (NIT + NIC * NHC), vm=NQ * (2 * NIT + NHC * NTBQ), sm=0),
        }
        PH_ORDER = ["P0", "P1", "P3", "P4", "P2", "P5"]
        B = {}
        run = dict(tm=0, vm=0, sm=0)
        for pn in PH_ORDER:
            B[pn] = dict(run)
            for k in run:
                run[k] += D_[pn][k]

        xnT_v = xnT.ap().rearrange("p (nd t) -> p nd t", t=T)

        def xnT_tile(nd, tb):
            return xnT_v[:, nd, tb * P:(tb + 1) * P]

        TMQ = NIT + NIC * NHC
        VMQ = 2 * NIT + NHC * NTBQ

        blk = e(nc.Block())

        @blk.sync
        def _(sync):
            b = B["P0"]
            sync.dma_start(out=eyeF[:], in_=eye_in).then_inc(qe, 16)
            for tb in range(NTB):
                if tb >= 2:
                    sync.wait_ge(vm, b["vm"] + 2 * (tb - 2) + 2)
                sync.dma_start(out=xb[tb % 2][:], in_=x_in[tb * P:(tb + 1) * P, :]
                               ).then_inc(qx[tb % 2], 16)
            sync.nop().then_inc(ph, 1)
            sync.wait_ge(ph, 5)
            b = B["P1"]
            for ic in range(NIC):
                i0, iw = ICB[ic], ICW[ic]
                for tb in range(NTB):
                    g = ic * NTB + tb
                    sync.wait_ge(sm, b["sm"] + g + 1)
                    sync.dma_start(out=s_sp[tb * P:(tb + 1) * P, i0:i0 + iw],
                                   in_=stage_s[g % 4][:, 0:iw]).then_inc(qt[g % 4], 16)
            for j in range(4):
                sync.wait_ge(qt[j], 16 * (NG // 4))
            sync.nop().then_inc(ph, 1)
            sync.wait_ge(ph, 10)
            b = B["P3"]
            for ic in range(NIC):
                i0, iw = ICB[ic], ICW[ic]
                for tb in range(NTB):
                    g = ic * NTB + tb
                    sync.wait_ge(sm, b["sm"] + g + 1)
                    sync.dma_start(out=u_sp[tb * P:(tb + 1) * P, i0:i0 + iw],
                                   in_=stage_u[g % 4][:, 0:iw]).then_inc(qt[g % 4], 16)
            for j in range(4):
                sync.wait_ge(qt[j], 16 * (2 * NG // 4))
            sync.nop().then_inc(ph, 1)
            sync.wait_ge(ph, 15)
            b = B["P4"]
            for tb in range(NTB):
                for h2 in range(2):
                    sync.wait_ge(vm, b["vm"] + 5 * tb + 4 + h2)
                    sync.dma_start(out=ffn_sp[tb * P:(tb + 1) * P,
                                              h2 * 2048:(h2 + 1) * 2048],
                                   in_=ffnst[h2][:]).then_inc(qff[h2], 16)
            for j in range(2):
                sync.wait_ge(qff[j], 16 * NTB)
            sync.nop().then_inc(ph, 1)
            sync.wait_ge(ph, 20)
            b = B["P2"]
            VMP = 2 * MED_ITERS + 4
            NPAIR = NTB // 2
            for p_ in range(NPAIR):
                for pi in range(2):
                    tb = 2 * p_ + pi
                    if p_ >= 1:
                        sync.wait_ge(vm, b["vm"] + (p_ - 1) * VMP + 2 * MED_ITERS + 3 + pi)
                    sync.dma_start(out=s_t[pi][:], in_=s_sp[tb * P:(tb + 1) * P, :]
                                   ).then_inc(qx[pi], 16)
                for pi in range(2):
                    tb = 2 * p_ + pi
                    sync.wait_ge(vm, b["vm"] + p_ * VMP + 2 * MED_ITERS + 3 + pi)
                    sync.dma_start(out=sm_sp[tb * P:(tb + 1) * P, :], in_=sm0_sb[pi][:]
                                   ).then_inc(qmP[pi], 16)
            for pi in range(2):
                sync.wait_ge(qmP[pi], 16 * NPAIR)
            sync.nop().then_inc(ph, 1)
            sync.wait_ge(ph, 25)
            b = B["P5"]
            for q in range(NQ):
                r0_ = q * TQ
                vmq = b["vm"] + q * VMQ
                for it in range(NIT):
                    if it >= 4:
                        sync.wait_ge(vm, vmq + 2 * (it - 4) + 1)
                    src_s = sm_sp[r0_:r0_ + TQ, it * P:(it + 1) * P]
                    src_u = u_sp[r0_:r0_ + TQ, it * P:(it + 1) * P]
                    sync.dma_start(out=smt_in[it % 4].ap().rearrange(
                                       "p (tt i) -> p tt i", i=P),
                                   in_=src_s.rearrange("(tt p) i -> p tt i", p=P)
                                   ).then_inc(qp[it % 4], 16)
                    sync.dma_start(out=u_in[it % 4].ap().rearrange(
                                       "p (tt i) -> p tt i", i=P),
                                   in_=src_u.rearrange("(tt p) i -> p tt i", p=P)
                                   ).then_inc(qu[it % 4], 16)
                NE = NHC * NTBQ
                def _xload(ev):
                    hc_, tb_ = divmod(ev, NTBQ)
                    if ev >= 4:
                        sync.wait_ge(vm, vmq + 2 * NIT + ev - 4 + 1)
                    sync.dma_start(
                        out=x_t[ev % 4][:],
                        in_=x_in[r0_ + tb_ * P:r0_ + (tb_ + 1) * P,
                                 hc_ * 512:(hc_ + 1) * 512]).then_inc(qxt[ev % 4], 16)
                _xload(0)
                _xload(1)
                for ev in range(NE):
                    if ev + 2 < NE:
                        _xload(ev + 2)
                    hc_, tb_ = divmod(ev, NTBQ)
                    sync.wait_ge(vm, vmq + 2 * NIT + ev + 1)
                    sync.dma_start(
                        out=y_out[r0_ + tb_ * P:r0_ + (tb_ + 1) * P,
                                  hc_ * 512:(hc_ + 1) * 512],
                        in_=outst[ev % 4][:]).then_inc(qout[ev % 4], 16)
            for j in range(4):
                sync.wait_ge(qout[j], 16 * (NQ * NHC * NTBQ // 4))
            sync.nop().then_inc(ph, 1)

        @blk.gpsimd
        def _(gp):
            gp.nop().then_inc(ph, 1)
            gp.wait_ge(ph, 5)
            b = B["P1"]
            for ic in range(NIC):
                i0, iw = ICB[ic], ICW[ic]
                if ic >= 2:
                    gp.wait_ge(tm, b["tm"] + (ic - 1) * NTB)
                dst = wslab[ic % 2].ap().rearrange("p (nd i) -> p nd i", i=IC)[:, :, 0:iw]
                gp.dma_start(out=dst, in_=wgT_t[:, :, i0:i0 + iw]).then_inc(qs[ic % 2], 16)
                if ic % 2 == 0:
                    c = ic // 2
                    r0c, rw = ICB[c], ICW[c]
                    gp.dma_start(out=wd_bf[r0c:r0c + rw, :],
                                 in_=wdT_in[r0c:r0c + rw, :]).then_inc(qwc, 16)
            gp.nop().then_inc(ph, 1)
            gp.wait_ge(ph, 10)
            b = B["P3"]
            for ic in range(NIC):
                i0, iw = ICB[ic], ICW[ic]
                if ic >= 2:
                    gp.wait_ge(tm, b["tm"] + (ic - 1) * NTB)
                dst = wslab[ic % 2].ap().rearrange("p (nd i) -> p nd i", i=IC)[:, :, 0:iw]
                gp.dma_start(out=dst, in_=wuT_t[:, :, i0:i0 + iw]).then_inc(qs[ic % 2], 16)
                if ic % 2 == 0:
                    c = 11 + ic // 2
                    r0c, rw = ICB[c], ICW[c]
                    gp.dma_start(out=wd_bf[r0c:r0c + rw, :],
                                 in_=wdT_in[r0c:r0c + rw, :]).then_inc(qwc, 16)
            gp.nop().then_inc(ph, 1)
            gp.wait_ge(ph, 15)
            gp.dma_start(out=aT_sb.ap().rearrange("p (nd c) -> p nd c", c=128),
                         in_=aT_t).then_inc(qc, 16)
            gp.dma_start(out=wrT_sb.ap().rearrange("p (nd c) -> p nd c", c=8),
                         in_=wrT_t).then_inc(qc, 16)
            gp.dma_start(out=bF_sb[:], in_=bF_in).then_inc(qc, 16)
            gp.dma_start(out=br_sb[:], in_=br_in).then_inc(qc, 16)
            gp.dma_start(out=eyeB[:], in_=eye_in).then_inc(qc, 16)
            gp.nop().then_inc(ph, 1)
            gp.wait_ge(ph, 20)
            gp.nop().then_inc(ph, 1)
            gp.wait_ge(ph, 25)
            gp.wait_ge(qwc, 16 * NIC)
            b = B["P5"]
            for q in range(NQ):
                r0_ = q * TQ
                tmq = b["tm"] + q * TMQ
                vmq = b["vm"] + q * VMQ
                for hc in range(NHC):
                    for grp in range(NIC):
                        gi0 = grp * 4
                        gw = min(4, NIT - gi0)
                        gidx = hc * NIC + grp
                        if gidx >= 4:
                            gp.wait_ge(tm, tmq + NIT + gidx - 4 + 1)
                        elif q >= 1:
                            gp.wait_ge(tm, tmq - TMQ + NIT + (NHC * NIC - 4 + gidx) + 1)
                        src = wd_bf[gi0 * P:gi0 * P + gw * P, hc * 512:(hc + 1) * 512]
                        gp.dma_start(
                            out=wdring[gidx % 4].ap().rearrange(
                                "p (g h) -> p g h", h=512)[:, 0:gw, :],
                            in_=src.rearrange("(g p) h -> p g h", p=P)
                        ).then_inc(qd[gidx % 4], 16)
                    for tb in range(NTBQ):
                        ev = hc * NTBQ + tb
                        if ev >= 4:
                            gp.wait_ge(vm, vmq + 2 * NIT + ev - 4 + 1)
                        elif q >= 1:
                            gp.wait_ge(vm, vmq - VMQ + 2 * NIT + (NHC * NTBQ - 4 + ev) + 1)
                        gp.dma_start(
                            out=ffn_t[ev % 4][:],
                            in_=ffn_sp[r0_ + tb * P:r0_ + (tb + 1) * P,
                                       hc * 512:(hc + 1) * 512]).then_inc(qf[ev % 4], 16)
            gp.nop().then_inc(ph, 1)

        @blk.tensor
        def _(t):
            b = B["P0"]
            t.wait_ge(qe, 16)
            pregs = [pA[:, 0:1024], pA[:, 1024:2048], pB[:, 0:1024]]
            for tb in range(NTB):
                t.wait_ge(vm, b["vm"] + 2 * tb + 2)
                for bt in range(4):
                    gb = 4 * tb + bt
                    if gb >= 3:
                        pt, pb_ = divmod(gb - 3, 4)
                        t.wait_ge(sm, b["sm"] + 5 * pt + 2 + pb_)
                    reg = pregs[gb % 3]
                    for j in range(8):
                        nd = 8 * bt + j
                        mi = t.transpose(reg[:, j * 128:(j + 1) * 128],
                                         xn_sb[:, nd * 128:(nd + 1) * 128], eyeF[:])
                    mi.then_inc(tm, 1)
            t.drain().then_inc(ph, 1)
            t.wait_ge(ph, 5)
            b = B["P1"]
            for ic in range(NIC):
                iw = ICW[ic]
                t.wait_ge(qs[ic % 2], 16 * (ic // 2 + 1))
                sl = wslab[ic % 2].ap().rearrange("p (nd i) -> p nd i", i=IC)
                for tb in range(NTB):
                    g = ic * NTB + tb
                    pb_ = pB[:, (g % 2) * 512:(g % 2) * 512 + iw]
                    if g >= 2:
                        t.wait_ge(sm, b["sm"] + g - 1)
                    for nd in range(ND):
                        mi = t.matmul(pb_, xnT_tile(nd, tb), sl[:, nd, 0:iw],
                                      start=(nd == 0), stop=(nd == ND - 1))
                    mi.then_inc(tm, 1)
            t.drain().then_inc(ph, 1)
            t.wait_ge(ph, 10)
            b = B["P3"]
            for ic in range(NIC):
                iw = ICW[ic]
                t.wait_ge(qs[ic % 2], 16 * (NIC // 2 + ic // 2 + 1))
                sl = wslab[ic % 2].ap().rearrange("p (nd i) -> p nd i", i=IC)
                for tb in range(NTB):
                    g = ic * NTB + tb
                    pb_ = pB[:, (g % 2) * 512:(g % 2) * 512 + iw]
                    if g >= 2:
                        t.wait_ge(sm, b["sm"] + g - 1)
                    for nd in range(ND):
                        mi = t.matmul(pb_, xnT_tile(nd, tb), sl[:, nd, 0:iw],
                                      start=(nd == 0), stop=(nd == ND - 1))
                    mi.then_inc(tm, 1)
            t.drain().then_inc(ph, 1)
            t.wait_ge(ph, 15)
            b = B["P4"]
            t.wait_ge(qc, 16 * 5)
            wr_v = wrT_sb.ap().rearrange("p (nd c) -> p nd c", c=8)
            aT_v = aT_sb.ap().rearrange("p (nd c) -> p nd c", c=128)
            for tb in range(NTB):
                if tb >= 1:
                    t.wait_ge(vm, b["vm"] + 5 * (tb - 1) + 5)
                for nd in range(ND):
                    t.matmul(pB[:, 0:128], xnT_tile(nd, tb), aT_v[:, nd, :],
                             start=(nd == 0), stop=(nd == ND - 1))
                    t.matmul(pB[:, 512:520], xnT_tile(nd, tb), wr_v[:, nd, :],
                             start=(nd == 0), stop=False)
                mi = t.matmul(pB[:, 512:520], ones1[:, :], br_sb[:, :],
                              start=False, stop=True)
                mi.then_inc(tm, 1)
                t.wait_ge(vm, b["vm"] + 5 * tb + 2)
                t.transpose(pB[:, 128:256], z_sb[:], eyeF[:]).then_inc(tm, 1)
                t.wait_ge(vm, b["vm"] + 5 * tb + 3)
                for h2 in range(2):
                    if h2 == 1:
                        t.wait_ge(vm, b["vm"] + 5 * tb + 4)
                    for k4 in range(4):
                        mi = t.matmul(pA[:, k4 * 512:(k4 + 1) * 512], zT_sb[:],
                                      bF_sb[:, h2 * 2048 + k4 * 512:
                                            h2 * 2048 + (k4 + 1) * 512],
                                      start=True, stop=True)
                    mi.then_inc(tm, 1)
            t.drain().then_inc(ph, 1)
            t.wait_ge(ph, 20)
            t.drain().then_inc(ph, 1)
            t.wait_ge(ph, 25)
            b = B["P5"]
            for q in range(NQ):
                vmq = b["vm"] + q * VMQ
                tmq = b["tm"] + q * TMQ
                pslots = [pC[:, 0:512], pC[:, 512:1024], pD[:, 0:512], pD[:, 512:1024]]
                for it in range(NIT):
                    t.wait_ge(vm, vmq + 2 * it + 1)
                    if it >= 4:
                        t.wait_ge(vm, vmq + 2 * (it - 4) + 2)
                    reg = pslots[it % 4]
                    for j in range(4):
                        mi = t.transpose(reg[:, j * 128:(j + 1) * 128],
                                         sprod[it % 4][:, j * 128:(j + 1) * 128],
                                         eyeB[:])
                    mi.then_inc(tm, 1)
                t.wait_ge(vm, vmq + 2 * NIT)
                smtq_v = smtq.ap().rearrange("p (it t) -> p it t", t=512)
                for hc in range(NHC):
                    if hc >= 1:
                        t.wait_ge(vm, vmq + 2 * NIT + hc * NTBQ)
                    elif q >= 1:
                        t.wait_ge(vm, vmq - VMQ + 2 * NIT + NHC * NTBQ)
                    for grp in range(NIC):
                        gi0 = grp * 4
                        gw = min(4, NIT - gi0)
                        gidx = hc * NIC + grp
                        t.wait_ge(qd[gidx % 4],
                                  16 * (q * (NHC * NIC // 4) + gidx // 4 + 1))
                        wdv = wdring[gidx % 4].ap().rearrange("p (g h) -> p g h", h=512)
                        for j in range(gw):
                            it = gi0 + j
                            for tb in range(NTBQ):
                                mi = t.matmul(pA[:, tb * 512:(tb + 1) * 512],
                                              smtq_v[:, it, tb * P:(tb + 1) * P],
                                              wdv[:, j, :],
                                              start=(it == 0), stop=(it == NIT - 1))
                        mi.then_inc(tm, 1)
            t.drain().then_inc(ph, 1)

        @blk.vector
        def _(v):
            vvc = [0]
            def edge(ins):
                vvc[0] += 1
                ins.then_inc(vv, 1)
                v.wait_ge(vv, vvc[0])
            b = B["P0"]
            v.memset(ones16[:], 1.0)
            v.memset(ones1[:], 1.0)
            v.memset(eps_t[:, 0:1], LN_EPS)
            for tb in range(NTB):
                xbuf = xb[tb % 2]
                v.wait_ge(qx[tb % 2], 16 * (tb // 2 + 1))
                edge(v.reduce_sum(sum1[:, 0:1], xbuf[:], axis=AX.X))
                edge(v.tensor_scalar_mul(mean[:, 0:1], sum1[:, 0:1], 1.0 / D))
                v.scalar_tensor_tensor(out=trash_sq[:], in0=xbuf[:], scalar=mean[:, 0:1],
                                       in1=xbuf[:], op0=ALU.subtract, op1=ALU.mult)
                edge(v.reduce_sum(ssq[:, 0:1], trash_sq[:], axis=AX.X))
                edge(v.tensor_scalar_mul(var[:, 0:1], ssq[:, 0:1], 1.0 / D))
                v.tensor_scalar_add(ve[:, 0:1], var[:, 0:1], LN_EPS).then_inc(vm, 1)
                v.wait_ge(sm, b["sm"] + 5 * tb + 1)
                edge(v.reciprocal(r0[:, 0:1], sq[:, 0:1]))
                edge(v.tensor_mul(t1[:, 0:1], r0[:, 0:1], r0[:, 0:1]))
                edge(v.tensor_mul(t2[:, 0:1], ve[:, 0:1], t1[:, 0:1]))
                edge(v.tensor_scalar(out=t3[:, 0:1], in0=t2[:, 0:1], scalar1=-0.5,
                                     scalar2=1.5, op0=ALU.mult, op1=ALU.add))
                edge(v.tensor_mul(rstd[:, 0:1], r0[:, 0:1], t3[:, 0:1]))
                if tb >= 1:
                    v.wait_ge(tm, b["tm"] + 4 * tb)
                v.tensor_scalar(out=xn_sb[:], in0=xbuf[:], scalar1=mean[:, 0:1],
                                scalar2=rstd[:, 0:1], op0=ALU.subtract,
                                op1=ALU.mult).then_inc(vm, 1)
            v.drain().then_inc(ph, 1)
            v.wait_ge(ph, 5)
            v.drain().then_inc(ph, 1)
            v.wait_ge(ph, 10)
            v.drain().then_inc(ph, 1)
            v.wait_ge(ph, 15)
            b = B["P4"]
            for tb in range(NTB):
                v.wait_ge(tm, b["tm"] + 4 * tb + 1)
                edge(v.reduce_max(mx[:, 0:1], pB[:, 512:520], axis=AX.X))
                v.tensor_scalar_mul(nmx[:, 0:1], mx[:, 0:1], -1.0).then_inc(vm, 1)
                v.wait_ge(sm, b["sm"] + tb + 1)
                edge(v.reciprocal(rs[:, 0:1], sume[:, 0:1]))
                edge(v.tensor_scalar(out=sco[:], in0=exps[:], scalar1=rs[:, 0:1],
                                     scalar2=16.0, op0=ALU.mult, op1=ALU.mult))
                for h in range(8):
                    ins = v.tensor_scalar_mul(srep[:, h * 16:(h + 1) * 16], ones16[:],
                                              sco[:, h:h + 1])
                edge(ins)
                v.tensor_mul(z_sb[:], pB[:, 0:128], srep[:]).then_inc(vm, 1)
                v.wait_ge(tm, b["tm"] + 4 * tb + 2)
                v.tensor_copy(zT_sb[:], pB[:, 128:256]).then_inc(vm, 1)
                for h2 in range(2):
                    v.wait_ge(tm, b["tm"] + 4 * tb + 3 + h2)
                    if tb >= 1:
                        v.wait_ge(qff[h2], 16 * tb)
                    v.tensor_copy(ffnst[h2][:], pA[:, 0:2048]).then_inc(vm, 1)
            v.drain().then_inc(ph, 1)
            v.wait_ge(ph, 20)
            b = B["P2"]
            VMP = 2 * MED_ITERS + 4
            SMP = 2 * MED_ITERS + 2
            NPAIR = NTB // 2
            for p_ in range(NPAIR):
                vmp = b["vm"] + p_ * VMP
                smp = b["sm"] + p_ * SMP
                for pi in range(2):
                    v.memset(loS[pi][0][:, 0:1], 0.0)
                    v.memset(hiS[pi][0][:, 0:1], MED_HI)
                    v.memset(cloS[pi][0][:, 0:1], float(I))
                    v.memset(chiS[pi][0][:, 0:1], -float(I))
                    v.memset(thrP[pi][:, 0:1], MED_THR0)
                    v.memset(nthrP[pi][:, 0:1], -MED_THR0).then_inc(vm, 1)
                for k in range(MED_ITERS):
                    for pi in range(2):
                        thr = thrP[pi]
                        nthr = nthrP[pi]
                        craw = crawP[pi]
                        cs, ns = k % 2, (k + 1) % 2
                        v.wait_ge(sm, smp + 2 * k + 3 + pi)
                        edge(v.tensor_single_scalar(mhi[:, 0:1], craw[:, 0:1], 0.0,
                                                    op=ALU.is_lt))
                        v.tensor_single_scalar(mlo[:, 0:1], craw[:, 0:1], 0.0,
                                               op=ALU.is_ge)
                        edge(v.select(hiS[pi][ns][:, 0:1], mhi[:, 0:1], thr[:, 0:1],
                                      hiS[pi][cs][:, 0:1]))
                        v.select(chiS[pi][ns][:, 0:1], mhi[:, 0:1], craw[:, 0:1],
                                 chiS[pi][cs][:, 0:1])
                        v.select(loS[pi][ns][:, 0:1], mlo[:, 0:1], thr[:, 0:1],
                                 loS[pi][cs][:, 0:1])
                        edge(v.select(cloS[pi][ns][:, 0:1], mlo[:, 0:1], craw[:, 0:1],
                                      cloS[pi][cs][:, 0:1]))
                        if k < MED_BISECT:
                            edge(v.tensor_add(tmp1[:, 0:1], loS[pi][ns][:, 0:1],
                                              hiS[pi][ns][:, 0:1]))
                            edge(v.tensor_scalar_mul(thr[:, 0:1], tmp1[:, 0:1], 0.5))
                        else:
                            v.tensor_sub(den[:, 0:1], cloS[pi][ns][:, 0:1],
                                         chiS[pi][ns][:, 0:1])
                            edge(v.tensor_sub(dd[:, 0:1], hiS[pi][ns][:, 0:1],
                                              loS[pi][ns][:, 0:1]))
                            v.reciprocal(tmp1[:, 0:1], den[:, 0:1])
                            edge(v.tensor_scalar_mul(marg[:, 0:1], dd[:, 0:1], MED_CLAMP))
                            v.tensor_mul(rq[:, 0:1], cloS[pi][ns][:, 0:1], tmp1[:, 0:1])
                            v.tensor_add(lob[:, 0:1], loS[pi][ns][:, 0:1], marg[:, 0:1])
                            edge(v.tensor_sub(hib[:, 0:1], hiS[pi][ns][:, 0:1],
                                              marg[:, 0:1]))
                            edge(v.tensor_mul(rd[:, 0:1], rq[:, 0:1], dd[:, 0:1]))
                            edge(v.tensor_add(thr[:, 0:1], loS[pi][ns][:, 0:1],
                                              rd[:, 0:1]))
                            edge(v.tensor_max(thr[:, 0:1], thr[:, 0:1], lob[:, 0:1]))
                            edge(v.tensor_tensor(thr[:, 0:1], thr[:, 0:1], hib[:, 0:1],
                                                 op=ALU.min))
                        v.tensor_scalar_mul(nthr[:, 0:1], thr[:, 0:1],
                                            -1.0).then_inc(vm, 1)
                for pi in range(2):
                    if p_ >= 1:
                        v.wait_ge(qmP[pi], 16 * p_)
                    v.scalar_tensor_tensor(out=sm0_sb[pi][:], in0=aP[pi][:],
                                           scalar=thrP[pi][:, 0:1],
                                           in1=s_t[pi][:], op0=ALU.is_gt,
                                           op1=ALU.mult).then_inc(vm, 1)
            v.drain().then_inc(ph, 1)
            v.wait_ge(ph, 25)
            b = B["P5"]
            for q in range(NQ):
                vmq = b["vm"] + q * VMQ
                tmq = b["tm"] + q * TMQ
                pslots = [pC[:, 0:512], pC[:, 512:1024], pD[:, 0:512], pD[:, 512:1024]]
                for it in range(NIT):
                    nslot = (NIT + 3 - (it % 4)) // 4
                    v.wait_ge(qp[it % 4], 16 * (q * nslot + it // 4 + 1))
                    v.wait_ge(qu[it % 4], 16 * (q * nslot + it // 4 + 1))
                    if it >= 4:
                        v.wait_ge(tm, tmq + it - 3)
                    v.tensor_mul(sprod[it % 4][:], smt_in[it % 4][:],
                                 u_in[it % 4][:]).then_inc(vm, 1)
                    v.wait_ge(tm, tmq + it + 1)
                    v.tensor_copy(smtq[:, it * 512:(it + 1) * 512],
                                  pslots[it % 4]).then_inc(vm, 1)
                for hc in range(NHC):
                    for tb in range(NTBQ):
                        ev = hc * NTBQ + tb
                        v.wait_ge(tm, tmq + NIT + (hc + 1) * NIC)
                        v.wait_ge(qxt[ev % 4],
                                  16 * (q * (NHC * NTBQ // 4) + ev // 4 + 1))
                        v.wait_ge(qf[ev % 4],
                                  16 * (q * (NHC * NTBQ // 4) + ev // 4 + 1))
                        v.tensor_add(t_a[ev % 2][:], pA[:, tb * 512:(tb + 1) * 512],
                                     x_t[ev % 4][:])
                        if ev >= 4:
                            v.wait_ge(qout[ev % 4],
                                      16 * (q * (NHC * NTBQ // 4) + ev // 4))
                        v.tensor_add(outst[ev % 4][:], t_a[ev % 2][:],
                                     ffn_t[ev % 4][:]).then_inc(vm, 1)
            v.drain().then_inc(ph, 1)

        @blk.scalar
        def _(sc):
            b = B["P0"]
            pregs = [pA[:, 0:1024], pA[:, 1024:2048], pB[:, 0:1024]]
            for tb in range(NTB):
                sc.wait_ge(vm, b["vm"] + 2 * tb + 1)
                sc.activation(sq[:, 0:1], var[:, 0:1], AF.Sqrt,
                              bias=eps_t[:, 0:1]).then_inc(sm, 1)
                for bt in range(4):
                    gb = 4 * tb + bt
                    sc.wait_ge(tm, b["tm"] + gb + 1)
                    src = pregs[gb % 3].rearrange("p (j t) -> p j t", t=128)
                    dst = xnT_v[:, 8 * bt:8 * bt + 8, tb * P:(tb + 1) * P]
                    sc.copy(dst, src).then_inc(sm, 1)
            sc.drain().then_inc(ph, 1)
            sc.wait_ge(ph, 5)
            b = B["P1"]
            for ic in range(NIC):
                iw = ICW[ic]
                for tb in range(NTB):
                    g = ic * NTB + tb
                    sc.wait_ge(tm, b["tm"] + g + 1)
                    if g >= 4:
                        sc.wait_ge(qt[g % 4], 16 * (g // 4))
                    sc.activation(stage_s[g % 4][:, 0:iw],
                                  pB[:, (g % 2) * 512:(g % 2) * 512 + iw],
                                  AF.Silu).then_inc(sm, 1)
            sc.drain().then_inc(ph, 1)
            sc.wait_ge(ph, 10)
            b = B["P3"]
            for ic in range(NIC):
                iw = ICW[ic]
                for tb in range(NTB):
                    g = ic * NTB + tb
                    sc.wait_ge(tm, b["tm"] + g + 1)
                    if g >= 4:
                        sc.wait_ge(qt[g % 4], 16 * (NG // 4 + g // 4))
                    sc.copy(stage_u[g % 4][:, 0:iw],
                            pB[:, (g % 2) * 512:(g % 2) * 512 + iw]).then_inc(sm, 1)
            sc.drain().then_inc(ph, 1)
            sc.wait_ge(ph, 15)
            b = B["P4"]
            for tb in range(NTB):
                sc.wait_ge(vm, b["vm"] + 5 * tb + 1)
                sc.activation(exps[:], pB[:, 512:520], AF.Exp, bias=nmx[:, 0:1],
                              accum_out=sume[:, 0:1]).then_inc(sm, 1)
            sc.drain().then_inc(ph, 1)
            sc.wait_ge(ph, 20)
            b = B["P2"]
            VMP = 2 * MED_ITERS + 4
            SMP = 2 * MED_ITERS + 2
            NPAIR = NTB // 2
            for p_ in range(NPAIR):
                smp = b["sm"] + p_ * SMP
                vmp = b["vm"] + p_ * VMP
                for pi in range(2):
                    sc.wait_ge(qx[pi], 16 * (NTB // 2 + p_ + 1))
                    sc.activation(aP[pi][:], s_t[pi][:], AF.Abs).then_inc(sm, 1)
                for k in range(MED_ITERS):
                    for pi in range(2):
                        sc.wait_ge(vm, vmp + 2 * k + 1 + pi)
                        sc.activation(trash_m[:], aP[pi][:], AF.Sign,
                                      bias=nthrP[pi][:, 0:1],
                                      accum_out=crawP[pi][:, 0:1]).then_inc(sm, 1)
            sc.drain().then_inc(ph, 1)
            sc.wait_ge(ph, 25)
            sc.drain().then_inc(ph, 1)

    return nc


_nc_cache = None


def kernel(L0_attn_x, L0_attn_w_gate, L0_attn_w_up, L0_attn_w_down,
           L0_ffn_A, L0_ffn_B, L0_ffn_Wr, L0_ffn_br):
    global _nc_cache
    x = np.ascontiguousarray(np.asarray(L0_attn_x, dtype=np.float32).reshape(16384, 4096))
    wgT = np.ascontiguousarray(np.asarray(L0_attn_w_gate, dtype=np.float32).T)
    wuT = np.ascontiguousarray(np.asarray(L0_attn_w_up, dtype=np.float32).T)
    wdT = np.ascontiguousarray(np.asarray(L0_attn_w_down, dtype=np.float32).T)
    aT = np.ascontiguousarray(np.asarray(L0_ffn_A, dtype=np.float32).reshape(128, 4096).T)
    bF = np.ascontiguousarray(
        np.asarray(L0_ffn_B, dtype=np.float32).transpose(0, 2, 1).reshape(128, 4096))
    wrT = np.ascontiguousarray(np.asarray(L0_ffn_Wr, dtype=np.float32).T)
    br1 = np.ascontiguousarray(np.asarray(L0_ffn_br, dtype=np.float32).reshape(1, 8))
    eye = np.eye(128, dtype=np.float32)

    if _nc_cache is None:
        _nc_cache = build_program()
    nc = _nc_cache

    in_maps = []
    for c in range(8):
        in_maps.append(dict(
            x_in=np.ascontiguousarray(x[c * 2048:(c + 1) * 2048]),
            wgT_in=wgT, wuT_in=wuT, wdT_in=wdT,
            aT_in=aT, bF_in=bF, wrT_in=wrT, br_in=br1, eye_in=eye))
    res = run_bass_kernel_spmd(nc, in_maps, core_ids=list(range(8)))
    y = np.concatenate([r["y_out"] for r in res.results], axis=0)
    return y.reshape(2, 8192, 4096)


# version 09
# speedup vs baseline: 1.1184x; 1.0022x over previous
import numpy as np
import concourse.bass as bass
import concourse.mybir as mybir
from concourse.bass_utils import run_bass_kernel_spmd
from contextlib import ExitStack

F32 = mybir.dt.float32
BF16 = mybir.dt.bfloat16
U8 = mybir.dt.uint8
AF = mybir.ActivationFunctionType
ALU = mybir.AluOpType
AX = mybir.AxisListType

P = 128
T = 2048
D = 4096
I = 11008
ND = D // P
NTB = T // P
NIT = I // P
K_KEEP = I // 2
IC = 512
ICB = list(range(0, I, IC))
ICW = [min(IC, I - s0) for s0 in ICB]
NIC = len(ICB)
NHC = D // 512
NQ = 4
TQ = T // NQ
NTBQ = TQ // P

MED_ITERS = 10
MED_BISECT = 3
MED_HI = 2.56
MED_THR0 = 0.27
MED_CLAMP = 0.02
LN_EPS = 1.0e-5

NG = NIC * NTB


def build_program():
    nc = bass.Bass("TRN2", target_bir_lowering=False, detect_race_conditions=False)

    x_in = nc.dram_tensor("x_in", [T, D], F32, kind="ExternalInput").ap()
    wgT_in = nc.dram_tensor("wgT_in", [D, I], F32, kind="ExternalInput").ap()
    wuT_in = nc.dram_tensor("wuT_in", [D, I], F32, kind="ExternalInput").ap()
    wdT_in = nc.dram_tensor("wdT_in", [I, D], F32, kind="ExternalInput").ap()
    aT_in = nc.dram_tensor("aT_in", [D, 128], F32, kind="ExternalInput").ap()
    bF_in = nc.dram_tensor("bF_in", [128, D], F32, kind="ExternalInput").ap()
    wrT_in = nc.dram_tensor("wrT_in", [D, 8], F32, kind="ExternalInput").ap()
    br_in = nc.dram_tensor("br_in", [1, 8], F32, kind="ExternalInput").ap()
    eye_in = nc.dram_tensor("eye_in", [128, 128], F32, kind="ExternalInput").ap()
    y_out = nc.dram_tensor("y_out", [T, D], F32, kind="ExternalOutput").ap()

    _SPK = "ExternalOutput" if globals().get("DEBUG_SPILL") else "Internal"
    s_sp = nc.dram_tensor("s_sp", [T, I], F32, kind=_SPK).ap()
    u_sp = nc.dram_tensor("u_sp", [T, I], BF16, kind=_SPK).ap()
    sm_sp = nc.dram_tensor("sm_sp", [T, I], BF16, kind=_SPK).ap()
    ffn_sp = nc.dram_tensor("ffn_sp", [T, D], BF16, kind=_SPK).ap()
    wd_bf = nc.dram_tensor("wd_bf", [I, D], BF16, kind="Internal").ap()

    wgT_t = wgT_in.rearrange("(nd p) i -> p nd i", p=P)
    wuT_t = wuT_in.rearrange("(nd p) i -> p nd i", p=P)
    aT_t = aT_in.rearrange("(nd p) c -> p nd c", p=P)
    wrT_t = wrT_in.rearrange("(nd p) c -> p nd c", p=P)

    with ExitStack() as ctx:
        e = ctx.enter_context

        eyeF = e(nc.sbuf_tensor("eyeF", [128, 128], F32))
        eyeB = e(nc.sbuf_tensor("eyeB", [128, 128], BF16))
        ones1 = e(nc.sbuf_tensor("ones1", [1, 128], BF16))
        br_sb = e(nc.sbuf_tensor("br_sb", [1, 8], BF16))

        def st(name):
            return e(nc.sbuf_tensor(name, [128, 1], F32))
        sum1, mean, ssq, var, ve, sq, r0, t1, t2, t3, rstd, eps_t = (
            st(n) for n in ["sum1", "mean", "ssq", "var", "ve", "sq", "r0",
                            "t1", "t2", "t3", "rstd", "eps_t"])
        loS = [[st(f"lo{pi}_{j}") for j in range(2)] for pi in range(2)]
        hiS = [[st(f"hi{pi}_{j}") for j in range(2)] for pi in range(2)]
        cloS = [[st(f"clo{pi}_{j}") for j in range(2)] for pi in range(2)]
        chiS = [[st(f"chi{pi}_{j}") for j in range(2)] for pi in range(2)]
        thrP = [st("thr0"), st("thr1")]
        nthrP = [st("nthr0"), st("nthr1")]
        crawP = [st("craw0"), st("craw1")]
        mhi = e(nc.sbuf_tensor("mhi", [128, 1], U8))
        mlo = e(nc.sbuf_tensor("mlo", [128, 1], U8))
        den, num, rq, dd, rd, marg, lob, hib, tmp1 = (
            st(n) for n in ["den", "num", "rq", "dd", "rd", "marg", "lob", "hib", "tmp1"])
        mx, nmx, sume, rs = (st(n) for n in ["mx", "nmx", "sume", "rs"])
        exps = e(nc.sbuf_tensor("exps", [128, 8], F32))
        sco = e(nc.sbuf_tensor("sco", [128, 8], F32))
        ones16 = e(nc.sbuf_tensor("ones16", [128, 16], F32))

        ARENA = 205824
        arena = e(nc.sbuf_tensor("arena", [128, ARENA], U8))
        base = nc.lookup_mloc(arena).addr

        def place(name, shape, dtype, off):
            assert off + int(np.prod(shape[1:])) * mybir.dt.size(dtype) <= ARENA, name
            return nc.alloc_sbuf_tensor_at(name, shape, dtype, offset=base + off)

        xnT = place("xnT", [128, ND * T], BF16, 0)
        xb = [place(f"xb{j}", [128, D], F32, 131072 + j * 16384) for j in range(2)]
        xn_sb = place("xn_sb", [128, D], F32, 163840)
        trash_sq = place("trash_sq", [128, D], F32, 180224)
        wslab = [place(f"wslab{j}", [128, ND * IC], BF16, 131072 + j * 32768)
                 for j in range(2)]
        stage_s = [place(f"stage_s{j}", [128, IC], F32, 196608 + j * 2048)
                   for j in range(4)]
        stage_u = [place(f"stage_u{j}", [128, IC], BF16, 196608 + j * 1024)
                   for j in range(4)]
        aT_sb = place("aT_sb", [128, ND * 128], BF16, 131072)
        wrT_sb = place("wrT_sb", [128, ND * 8], BF16, 139264)
        bF_sb = place("bF_sb", [128, D], BF16, 139776)
        ffnst = [place(f"ffnst{j}", [128, 2048], BF16, 147968 + j * 4096)
                 for j in range(2)]
        z_sb = place("z_sb", [128, 128], F32, 156160)
        zT_sb = place("zT_sb", [128, 128], BF16, 156672)
        srep = place("srep", [128, 128], F32, 156928)
        s_t = [place(f"s_t{j}", [128, I], F32, j * 44032) for j in range(2)]
        aP = [place(f"aP{j}", [128, I], BF16, 88064 + j * 22016) for j in range(2)]
        sm0_sb = [place(f"sm0_sb{j}", [128, I], BF16, 132096 + j * 22016)
                  for j in range(2)]
        trash_m = place("trash_m", [128, I], BF16, 176128)
        smtq = place("smtq", [128, NIT * 512], BF16, 0)
        smt_in = [place(f"smt_in{j}", [128, 512], BF16, 88064 + j * 1024) for j in range(4)]
        u_in = [place(f"u_in{j}", [128, 512], BF16, 92160 + j * 1024) for j in range(4)]
        sprod = [place(f"sprod{j}", [128, 512], BF16, 96256 + j * 1024) for j in range(4)]
        wdring = [place(f"wdring{j}", [128, 4 * 512], BF16, 100352 + j * 4096) for j in range(4)]
        x_t = [place(f"x_t{j}", [128, 512], F32, 116736 + j * 2048) for j in range(4)]
        ffn_t = [place(f"ffn_t{j}", [128, 512], F32, 124928 + j * 2048) for j in range(4)]
        t_a = [place(f"t_a{j}", [128, 512], F32, 133120 + j * 2048) for j in range(2)]
        outst = [place(f"outst{j}", [128, 512], F32, 137216 + j * 2048) for j in range(4)]

        pA = nc.alloc_psum_tensor("pA", [128, 2048], F32)
        pB = nc.alloc_psum_tensor("pB", [128, 1024], F32)
        pC = nc.alloc_psum_tensor("pC", [128, 1024], BF16)
        pD = nc.alloc_psum_tensor("pD", [128, 1024], BF16)

        ph = e(nc.semaphore("ph"))
        tm = e(nc.semaphore("tm"))
        vm = e(nc.semaphore("vm"))
        sm = e(nc.semaphore("sm"))
        vv = e(nc.semaphore("vv"))
        qe = e(nc.semaphore("qe"))
        qx = [e(nc.semaphore(f"qx{j}")) for j in range(2)]
        qs = [e(nc.semaphore(f"qs{j}")) for j in range(2)]
        qt = [e(nc.semaphore(f"qt{j}")) for j in range(4)]
        qmP = [e(nc.semaphore(f"qm{j}")) for j in range(2)]
        qc = e(nc.semaphore("qc"))
        qwc = e(nc.semaphore("qwc"))
        qff = [e(nc.semaphore(f"qff{j}")) for j in range(2)]
        qp = [e(nc.semaphore(f"qp{j}")) for j in range(4)]
        qu = [e(nc.semaphore(f"qu{j}")) for j in range(4)]
        qd = [e(nc.semaphore(f"qd{j}")) for j in range(4)]
        qxt = [e(nc.semaphore(f"qxt{j}")) for j in range(4)]
        qf = [e(nc.semaphore(f"qf{j}")) for j in range(4)]
        qout = [e(nc.semaphore(f"qout{j}")) for j in range(4)]

        D_ = {
            "P0": dict(tm=4 * NTB, vm=2 * NTB, sm=5 * NTB),
            "P1": dict(tm=NG, vm=0, sm=NG),
            "P3": dict(tm=NG, vm=0, sm=NG),
            "P4": dict(tm=4 * NTB, vm=5 * NTB, sm=NTB),
            "P2": dict(tm=0, vm=(MED_ITERS + 2) * NTB, sm=(1 + MED_ITERS) * NTB),
            "P5": dict(tm=NQ * NIC * NHC, vm=NQ * (NIT + NHC * NTBQ), sm=0),
        }
        PH_ORDER = ["P0", "P1", "P3", "P4", "P2", "P5"]
        B = {}
        run = dict(tm=0, vm=0, sm=0)
        for pn in PH_ORDER:
            B[pn] = dict(run)
            for k in run:
                run[k] += D_[pn][k]

        xnT_v = xnT.ap().rearrange("p (nd t) -> p nd t", t=T)

        def xnT_tile(nd, tb):
            return xnT_v[:, nd, tb * P:(tb + 1) * P]

        TMQ = NIC * NHC
        VMQ = NIT + NHC * NTBQ

        blk = e(nc.Block())

        @blk.sync
        def _(sync):
            b = B["P0"]
            sync.dma_start(out=eyeF[:], in_=eye_in).then_inc(qe, 16)
            for tb in range(NTB):
                if tb >= 2:
                    sync.wait_ge(vm, b["vm"] + 2 * (tb - 2) + 2)
                sync.dma_start(out=xb[tb % 2][:], in_=x_in[tb * P:(tb + 1) * P, :]
                               ).then_inc(qx[tb % 2], 16)
            sync.nop().then_inc(ph, 1)
            sync.wait_ge(ph, 5)
            b = B["P1"]
            for ic in range(NIC):
                i0, iw = ICB[ic], ICW[ic]
                for tb in range(NTB):
                    g = ic * NTB + tb
                    sync.wait_ge(sm, b["sm"] + g + 1)
                    sync.dma_start(out=s_sp[tb * P:(tb + 1) * P, i0:i0 + iw],
                                   in_=stage_s[g % 4][:, 0:iw]).then_inc(qt[g % 4], 16)
            for j in range(4):
                sync.wait_ge(qt[j], 16 * (NG // 4))
            sync.nop().then_inc(ph, 1)
            sync.wait_ge(ph, 10)
            b = B["P3"]
            for ic in range(NIC):
                i0, iw = ICB[ic], ICW[ic]
                for tb in range(NTB):
                    g = ic * NTB + tb
                    sync.wait_ge(sm, b["sm"] + g + 1)
                    sync.dma_start(out=u_sp[tb * P:(tb + 1) * P, i0:i0 + iw],
                                   in_=stage_u[g % 4][:, 0:iw]).then_inc(qt[g % 4], 16)
            for j in range(4):
                sync.wait_ge(qt[j], 16 * (2 * NG // 4))
            sync.nop().then_inc(ph, 1)
            sync.wait_ge(ph, 15)
            b = B["P4"]
            for tb in range(NTB):
                for h2 in range(2):
                    sync.wait_ge(vm, b["vm"] + 5 * tb + 4 + h2)
                    sync.dma_start(out=ffn_sp[tb * P:(tb + 1) * P,
                                              h2 * 2048:(h2 + 1) * 2048],
                                   in_=ffnst[h2][:]).then_inc(qff[h2], 16)
            for j in range(2):
                sync.wait_ge(qff[j], 16 * NTB)
            sync.nop().then_inc(ph, 1)
            sync.wait_ge(ph, 20)
            b = B["P2"]
            VMP = 2 * MED_ITERS + 4
            NPAIR = NTB // 2
            for p_ in range(NPAIR):
                for pi in range(2):
                    tb = 2 * p_ + pi
                    if p_ >= 1:
                        sync.wait_ge(vm, b["vm"] + (p_ - 1) * VMP + 2 * MED_ITERS + 3 + pi)
                    sync.dma_start(out=s_t[pi][:], in_=s_sp[tb * P:(tb + 1) * P, :]
                                   ).then_inc(qx[pi], 16)
                for pi in range(2):
                    tb = 2 * p_ + pi
                    sync.wait_ge(vm, b["vm"] + p_ * VMP + 2 * MED_ITERS + 3 + pi)
                    sync.dma_start(out=sm_sp[tb * P:(tb + 1) * P, :], in_=sm0_sb[pi][:]
                                   ).then_inc(qmP[pi], 16)
            for pi in range(2):
                sync.wait_ge(qmP[pi], 16 * NPAIR)
            sync.nop().then_inc(ph, 1)
            sync.wait_ge(ph, 25)
            b = B["P5"]
            for q in range(NQ):
                r0_ = q * TQ
                vmq = b["vm"] + q * VMQ
                NE = NHC * NTBQ
                def _xload(ev):
                    hc_, tb_ = divmod(ev, NTBQ)
                    if ev >= 4:
                        sync.wait_ge(vm, vmq + NIT + ev - 4 + 1)
                    sync.dma_start(
                        out=x_t[ev % 4][:],
                        in_=x_in[r0_ + tb_ * P:r0_ + (tb_ + 1) * P,
                                 hc_ * 512:(hc_ + 1) * 512]).then_inc(qxt[ev % 4], 16)
                _xload(0)
                _xload(1)
                for ev in range(NE):
                    if ev + 2 < NE:
                        _xload(ev + 2)
                    hc_, tb_ = divmod(ev, NTBQ)
                    sync.wait_ge(vm, vmq + NIT + ev + 1)
                    sync.dma_start(
                        out=y_out[r0_ + tb_ * P:r0_ + (tb_ + 1) * P,
                                  hc_ * 512:(hc_ + 1) * 512],
                        in_=outst[ev % 4][:]).then_inc(qout[ev % 4], 16)
            for j in range(4):
                sync.wait_ge(qout[j], 16 * (NQ * NHC * NTBQ // 4))
            sync.nop().then_inc(ph, 1)

        @blk.gpsimd
        def _(gp):
            gp.nop().then_inc(ph, 1)
            gp.wait_ge(ph, 5)
            b = B["P1"]
            for ic in range(NIC):
                i0, iw = ICB[ic], ICW[ic]
                if ic >= 2:
                    gp.wait_ge(tm, b["tm"] + (ic - 1) * NTB)
                dst = wslab[ic % 2].ap().rearrange("p (nd i) -> p nd i", i=IC)[:, :, 0:iw]
                gp.dma_start(out=dst, in_=wgT_t[:, :, i0:i0 + iw]).then_inc(qs[ic % 2], 16)
                if ic % 2 == 0:
                    c = ic // 2
                    r0c, rw = ICB[c], ICW[c]
                    gp.dma_start(out=wd_bf[r0c:r0c + rw, :],
                                 in_=wdT_in[r0c:r0c + rw, :]).then_inc(qwc, 16)
            gp.nop().then_inc(ph, 1)
            gp.wait_ge(ph, 10)
            b = B["P3"]
            for ic in range(NIC):
                i0, iw = ICB[ic], ICW[ic]
                if ic >= 2:
                    gp.wait_ge(tm, b["tm"] + (ic - 1) * NTB)
                dst = wslab[ic % 2].ap().rearrange("p (nd i) -> p nd i", i=IC)[:, :, 0:iw]
                gp.dma_start(out=dst, in_=wuT_t[:, :, i0:i0 + iw]).then_inc(qs[ic % 2], 16)
                if ic % 2 == 0:
                    c = 11 + ic // 2
                    r0c, rw = ICB[c], ICW[c]
                    gp.dma_start(out=wd_bf[r0c:r0c + rw, :],
                                 in_=wdT_in[r0c:r0c + rw, :]).then_inc(qwc, 16)
            gp.nop().then_inc(ph, 1)
            gp.wait_ge(ph, 15)
            gp.dma_start(out=aT_sb.ap().rearrange("p (nd c) -> p nd c", c=128),
                         in_=aT_t).then_inc(qc, 16)
            gp.dma_start(out=wrT_sb.ap().rearrange("p (nd c) -> p nd c", c=8),
                         in_=wrT_t).then_inc(qc, 16)
            gp.dma_start(out=bF_sb[:], in_=bF_in).then_inc(qc, 16)
            gp.dma_start(out=br_sb[:], in_=br_in).then_inc(qc, 16)
            gp.dma_start(out=eyeB[:], in_=eye_in).then_inc(qc, 16)
            gp.nop().then_inc(ph, 1)
            gp.wait_ge(ph, 20)
            gp.nop().then_inc(ph, 1)
            gp.wait_ge(ph, 25)
            gp.wait_ge(qwc, 16 * NIC)
            b = B["P5"]
            for q in range(NQ):
                r0_ = q * TQ
                tmq = b["tm"] + q * TMQ
                vmq = b["vm"] + q * VMQ
                for hc in range(NHC):
                    for grp in range(NIC):
                        gi0 = grp * 4
                        gw = min(4, NIT - gi0)
                        gidx = hc * NIC + grp
                        if gidx >= 4:
                            gp.wait_ge(tm, tmq + gidx - 4 + 1)
                        elif q >= 1:
                            gp.wait_ge(tm, tmq - TMQ + (NHC * NIC - 4 + gidx) + 1)
                        src = wd_bf[gi0 * P:gi0 * P + gw * P, hc * 512:(hc + 1) * 512]
                        gp.dma_start(
                            out=wdring[gidx % 4].ap().rearrange(
                                "p (g h) -> p g h", h=512)[:, 0:gw, :],
                            in_=src.rearrange("(g p) h -> p g h", p=P)
                        ).then_inc(qd[gidx % 4], 16)
                    for tb in range(NTBQ):
                        ev = hc * NTBQ + tb
                        if ev >= 4:
                            gp.wait_ge(vm, vmq + NIT + ev - 4 + 1)
                        elif q >= 1:
                            gp.wait_ge(vm, vmq - VMQ + NIT + (NHC * NTBQ - 4 + ev) + 1)
                        gp.dma_start(
                            out=ffn_t[ev % 4][:],
                            in_=ffn_sp[r0_ + tb * P:r0_ + (tb + 1) * P,
                                       hc * 512:(hc + 1) * 512]).then_inc(qf[ev % 4], 16)
            gp.nop().then_inc(ph, 1)

        @blk.tensor
        def _(t):
            b = B["P0"]
            t.wait_ge(qe, 16)
            pregs = [pA[:, 0:1024], pA[:, 1024:2048], pB[:, 0:1024]]
            for tb in range(NTB):
                t.wait_ge(vm, b["vm"] + 2 * tb + 2)
                for bt in range(4):
                    gb = 4 * tb + bt
                    if gb >= 3:
                        pt, pb_ = divmod(gb - 3, 4)
                        t.wait_ge(sm, b["sm"] + 5 * pt + 2 + pb_)
                    reg = pregs[gb % 3]
                    for j in range(8):
                        nd = 8 * bt + j
                        mi = t.transpose(reg[:, j * 128:(j + 1) * 128],
                                         xn_sb[:, nd * 128:(nd + 1) * 128], eyeF[:])
                    mi.then_inc(tm, 1)
            t.drain().then_inc(ph, 1)
            t.wait_ge(ph, 5)
            b = B["P1"]
            for ic in range(NIC):
                iw = ICW[ic]
                t.wait_ge(qs[ic % 2], 16 * (ic // 2 + 1))
                sl = wslab[ic % 2].ap().rearrange("p (nd i) -> p nd i", i=IC)
                for tb in range(NTB):
                    g = ic * NTB + tb
                    pb_ = pB[:, (g % 2) * 512:(g % 2) * 512 + iw]
                    if g >= 2:
                        t.wait_ge(sm, b["sm"] + g - 1)
                    for nd in range(ND):
                        mi = t.matmul(pb_, xnT_tile(nd, tb), sl[:, nd, 0:iw],
                                      start=(nd == 0), stop=(nd == ND - 1))
                    mi.then_inc(tm, 1)
            t.drain().then_inc(ph, 1)
            t.wait_ge(ph, 10)
            b = B["P3"]
            for ic in range(NIC):
                iw = ICW[ic]
                t.wait_ge(qs[ic % 2], 16 * (NIC // 2 + ic // 2 + 1))
                sl = wslab[ic % 2].ap().rearrange("p (nd i) -> p nd i", i=IC)
                for tb in range(NTB):
                    g = ic * NTB + tb
                    pb_ = pB[:, (g % 2) * 512:(g % 2) * 512 + iw]
                    if g >= 2:
                        t.wait_ge(sm, b["sm"] + g - 1)
                    for nd in range(ND):
                        mi = t.matmul(pb_, xnT_tile(nd, tb), sl[:, nd, 0:iw],
                                      start=(nd == 0), stop=(nd == ND - 1))
                    mi.then_inc(tm, 1)
            t.drain().then_inc(ph, 1)
            t.wait_ge(ph, 15)
            b = B["P4"]
            t.wait_ge(qc, 16 * 5)
            wr_v = wrT_sb.ap().rearrange("p (nd c) -> p nd c", c=8)
            aT_v = aT_sb.ap().rearrange("p (nd c) -> p nd c", c=128)
            for tb in range(NTB):
                if tb >= 1:
                    t.wait_ge(vm, b["vm"] + 5 * (tb - 1) + 5)
                for nd in range(ND):
                    t.matmul(pB[:, 0:128], xnT_tile(nd, tb), aT_v[:, nd, :],
                             start=(nd == 0), stop=(nd == ND - 1))
                    t.matmul(pB[:, 512:520], xnT_tile(nd, tb), wr_v[:, nd, :],
                             start=(nd == 0), stop=False)
                mi = t.matmul(pB[:, 512:520], ones1[:, :], br_sb[:, :],
                              start=False, stop=True)
                mi.then_inc(tm, 1)
                t.wait_ge(vm, b["vm"] + 5 * tb + 2)
                t.transpose(pB[:, 128:256], z_sb[:], eyeF[:]).then_inc(tm, 1)
                t.wait_ge(vm, b["vm"] + 5 * tb + 3)
                for h2 in range(2):
                    if h2 == 1:
                        t.wait_ge(vm, b["vm"] + 5 * tb + 4)
                    for k4 in range(4):
                        mi = t.matmul(pA[:, k4 * 512:(k4 + 1) * 512], zT_sb[:],
                                      bF_sb[:, h2 * 2048 + k4 * 512:
                                            h2 * 2048 + (k4 + 1) * 512],
                                      start=True, stop=True)
                    mi.then_inc(tm, 1)
            t.drain().then_inc(ph, 1)
            t.wait_ge(ph, 20)
            t.drain().then_inc(ph, 1)
            t.wait_ge(ph, 25)
            b = B["P5"]
            for q in range(NQ):
                vmq = b["vm"] + q * VMQ
                tmq = b["tm"] + q * TMQ
                smtq_v = smtq.ap().rearrange("p (it t) -> p it t", t=512)
                for hc in range(NHC):
                    if hc >= 1:
                        t.wait_ge(vm, vmq + NIT + hc * NTBQ)
                    elif q >= 1:
                        t.wait_ge(vm, vmq - VMQ + NIT + NHC * NTBQ)
                    for grp in range(NIC):
                        gi0 = grp * 4
                        gw = min(4, NIT - gi0)
                        gidx = hc * NIC + grp
                        if hc == 0:
                            t.wait_ge(vm, vmq + min(gi0 + gw, NIT))
                        t.wait_ge(qd[gidx % 4],
                                  16 * (q * (NHC * NIC // 4) + gidx // 4 + 1))
                        wdv = wdring[gidx % 4].ap().rearrange("p (g h) -> p g h", h=512)
                        for j in range(gw):
                            it = gi0 + j
                            for tb in range(NTBQ):
                                mi = t.matmul(pA[:, tb * 512:(tb + 1) * 512],
                                              smtq_v[:, it, tb * P:(tb + 1) * P],
                                              wdv[:, j, :],
                                              start=(it == 0), stop=(it == NIT - 1))
                        mi.then_inc(tm, 1)
            t.drain().then_inc(ph, 1)

        @blk.vector
        def _(v):
            vvc = [0]
            def edge(ins):
                vvc[0] += 1
                ins.then_inc(vv, 1)
                v.wait_ge(vv, vvc[0])
            b = B["P0"]
            v.memset(ones16[:], 1.0)
            v.memset(ones1[:], 1.0)
            v.memset(eps_t[:, 0:1], LN_EPS)
            for tb in range(NTB):
                xbuf = xb[tb % 2]
                v.wait_ge(qx[tb % 2], 16 * (tb // 2 + 1))
                edge(v.reduce_sum(sum1[:, 0:1], xbuf[:], axis=AX.X))
                edge(v.tensor_scalar_mul(mean[:, 0:1], sum1[:, 0:1], 1.0 / D))
                v.scalar_tensor_tensor(out=trash_sq[:], in0=xbuf[:], scalar=mean[:, 0:1],
                                       in1=xbuf[:], op0=ALU.subtract, op1=ALU.mult)
                edge(v.reduce_sum(ssq[:, 0:1], trash_sq[:], axis=AX.X))
                edge(v.tensor_scalar_mul(var[:, 0:1], ssq[:, 0:1], 1.0 / D))
                v.tensor_scalar_add(ve[:, 0:1], var[:, 0:1], LN_EPS).then_inc(vm, 1)
                v.wait_ge(sm, b["sm"] + 5 * tb + 1)
                edge(v.reciprocal(r0[:, 0:1], sq[:, 0:1]))
                edge(v.tensor_mul(t1[:, 0:1], r0[:, 0:1], r0[:, 0:1]))
                edge(v.tensor_mul(t2[:, 0:1], ve[:, 0:1], t1[:, 0:1]))
                edge(v.tensor_scalar(out=t3[:, 0:1], in0=t2[:, 0:1], scalar1=-0.5,
                                     scalar2=1.5, op0=ALU.mult, op1=ALU.add))
                edge(v.tensor_mul(rstd[:, 0:1], r0[:, 0:1], t3[:, 0:1]))
                if tb >= 1:
                    v.wait_ge(tm, b["tm"] + 4 * tb)
                v.tensor_scalar(out=xn_sb[:], in0=xbuf[:], scalar1=mean[:, 0:1],
                                scalar2=rstd[:, 0:1], op0=ALU.subtract,
                                op1=ALU.mult).then_inc(vm, 1)
            v.drain().then_inc(ph, 1)
            v.wait_ge(ph, 5)
            v.drain().then_inc(ph, 1)
            v.wait_ge(ph, 10)
            v.drain().then_inc(ph, 1)
            v.wait_ge(ph, 15)
            b = B["P4"]
            for tb in range(NTB):
                v.wait_ge(tm, b["tm"] + 4 * tb + 1)
                edge(v.reduce_max(mx[:, 0:1], pB[:, 512:520], axis=AX.X))
                v.tensor_scalar_mul(nmx[:, 0:1], mx[:, 0:1], -1.0).then_inc(vm, 1)
                v.wait_ge(sm, b["sm"] + tb + 1)
                edge(v.reciprocal(rs[:, 0:1], sume[:, 0:1]))
                edge(v.tensor_scalar(out=sco[:], in0=exps[:], scalar1=rs[:, 0:1],
                                     scalar2=16.0, op0=ALU.mult, op1=ALU.mult))
                for h in range(8):
                    ins = v.tensor_scalar_mul(srep[:, h * 16:(h + 1) * 16], ones16[:],
                                              sco[:, h:h + 1])
                edge(ins)
                v.tensor_mul(z_sb[:], pB[:, 0:128], srep[:]).then_inc(vm, 1)
                v.wait_ge(tm, b["tm"] + 4 * tb + 2)
                v.tensor_copy(zT_sb[:], pB[:, 128:256]).then_inc(vm, 1)
                for h2 in range(2):
                    v.wait_ge(tm, b["tm"] + 4 * tb + 3 + h2)
                    if tb >= 1:
                        v.wait_ge(qff[h2], 16 * tb)
                    v.tensor_copy(ffnst[h2][:], pA[:, 0:2048]).then_inc(vm, 1)
            v.drain().then_inc(ph, 1)
            v.wait_ge(ph, 20)
            b = B["P2"]
            VMP = 2 * MED_ITERS + 4
            SMP = 2 * MED_ITERS + 2
            NPAIR = NTB // 2
            for p_ in range(NPAIR):
                vmp = b["vm"] + p_ * VMP
                smp = b["sm"] + p_ * SMP
                for pi in range(2):
                    v.memset(loS[pi][0][:, 0:1], 0.0)
                    v.memset(hiS[pi][0][:, 0:1], MED_HI)
                    v.memset(cloS[pi][0][:, 0:1], float(I))
                    v.memset(chiS[pi][0][:, 0:1], -float(I))
                    v.memset(thrP[pi][:, 0:1], MED_THR0)
                    v.memset(nthrP[pi][:, 0:1], -MED_THR0).then_inc(vm, 1)
                for k in range(MED_ITERS):
                    for pi in range(2):
                        thr = thrP[pi]
                        nthr = nthrP[pi]
                        craw = crawP[pi]
                        cs, ns = k % 2, (k + 1) % 2
                        v.wait_ge(sm, smp + 2 * k + 3 + pi)
                        edge(v.tensor_single_scalar(mhi[:, 0:1], craw[:, 0:1], 0.0,
                                                    op=ALU.is_lt))
                        v.tensor_single_scalar(mlo[:, 0:1], craw[:, 0:1], 0.0,
                                               op=ALU.is_ge)
                        edge(v.select(hiS[pi][ns][:, 0:1], mhi[:, 0:1], thr[:, 0:1],
                                      hiS[pi][cs][:, 0:1]))
                        v.select(chiS[pi][ns][:, 0:1], mhi[:, 0:1], craw[:, 0:1],
                                 chiS[pi][cs][:, 0:1])
                        v.select(loS[pi][ns][:, 0:1], mlo[:, 0:1], thr[:, 0:1],
                                 loS[pi][cs][:, 0:1])
                        edge(v.select(cloS[pi][ns][:, 0:1], mlo[:, 0:1], craw[:, 0:1],
                                      cloS[pi][cs][:, 0:1]))
                        if k < MED_BISECT:
                            edge(v.tensor_add(tmp1[:, 0:1], loS[pi][ns][:, 0:1],
                                              hiS[pi][ns][:, 0:1]))
                            edge(v.tensor_scalar_mul(thr[:, 0:1], tmp1[:, 0:1], 0.5))
                        else:
                            v.tensor_sub(den[:, 0:1], cloS[pi][ns][:, 0:1],
                                         chiS[pi][ns][:, 0:1])
                            edge(v.tensor_sub(dd[:, 0:1], hiS[pi][ns][:, 0:1],
                                              loS[pi][ns][:, 0:1]))
                            v.reciprocal(tmp1[:, 0:1], den[:, 0:1])
                            edge(v.tensor_scalar_mul(marg[:, 0:1], dd[:, 0:1], MED_CLAMP))
                            v.tensor_mul(rq[:, 0:1], cloS[pi][ns][:, 0:1], tmp1[:, 0:1])
                            v.tensor_add(lob[:, 0:1], loS[pi][ns][:, 0:1], marg[:, 0:1])
                            edge(v.tensor_sub(hib[:, 0:1], hiS[pi][ns][:, 0:1],
                                              marg[:, 0:1]))
                            edge(v.tensor_mul(rd[:, 0:1], rq[:, 0:1], dd[:, 0:1]))
                            edge(v.tensor_add(thr[:, 0:1], loS[pi][ns][:, 0:1],
                                              rd[:, 0:1]))
                            edge(v.tensor_max(thr[:, 0:1], thr[:, 0:1], lob[:, 0:1]))
                            edge(v.tensor_tensor(thr[:, 0:1], thr[:, 0:1], hib[:, 0:1],
                                                 op=ALU.min))
                        v.tensor_scalar_mul(nthr[:, 0:1], thr[:, 0:1],
                                            -1.0).then_inc(vm, 1)
                for pi in range(2):
                    if p_ >= 1:
                        v.wait_ge(qmP[pi], 16 * p_)
                    v.scalar_tensor_tensor(out=sm0_sb[pi][:], in0=aP[pi][:],
                                           scalar=thrP[pi][:, 0:1],
                                           in1=s_t[pi][:], op0=ALU.is_gt,
                                           op1=ALU.mult).then_inc(vm, 1)
            v.drain().then_inc(ph, 1)
            v.wait_ge(ph, 25)
            b = B["P5"]
            for q in range(NQ):
                vmq = b["vm"] + q * VMQ
                tmq = b["tm"] + q * TMQ
                for it in range(NIT):
                    nslot = (NIT + 3 - (it % 4)) // 4
                    v.wait_ge(qp[it % 4], 16 * (q * nslot + it // 4 + 1))
                    v.wait_ge(qu[it % 4], 16 * (q * nslot + it // 4 + 1))
                    v.tensor_mul(smtq[:, it * 512:(it + 1) * 512], smt_in[it % 4][:],
                                 u_in[it % 4][:]).then_inc(vm, 1)
                for hc in range(NHC):
                    for tb in range(NTBQ):
                        ev = hc * NTBQ + tb
                        v.wait_ge(tm, tmq + (hc + 1) * NIC)
                        v.wait_ge(qxt[ev % 4],
                                  16 * (q * (NHC * NTBQ // 4) + ev // 4 + 1))
                        v.wait_ge(qf[ev % 4],
                                  16 * (q * (NHC * NTBQ // 4) + ev // 4 + 1))
                        v.tensor_add(t_a[ev % 2][:], pA[:, tb * 512:(tb + 1) * 512],
                                     x_t[ev % 4][:])
                        if ev >= 4 or q >= 1:
                            v.wait_ge(qout[ev % 4],
                                      16 * (q * (NHC * NTBQ // 4) + ev // 4))
                        v.tensor_add(outst[ev % 4][:], t_a[ev % 2][:],
                                     ffn_t[ev % 4][:]).then_inc(vm, 1)
            v.drain().then_inc(ph, 1)

        @blk.scalar
        def _(sc):
            b = B["P0"]
            pregs = [pA[:, 0:1024], pA[:, 1024:2048], pB[:, 0:1024]]
            for tb in range(NTB):
                sc.wait_ge(vm, b["vm"] + 2 * tb + 1)
                sc.activation(sq[:, 0:1], var[:, 0:1], AF.Sqrt,
                              bias=eps_t[:, 0:1]).then_inc(sm, 1)
                for bt in range(4):
                    gb = 4 * tb + bt
                    sc.wait_ge(tm, b["tm"] + gb + 1)
                    src = pregs[gb % 3].rearrange("p (j t) -> p j t", t=128)
                    dst = xnT_v[:, 8 * bt:8 * bt + 8, tb * P:(tb + 1) * P]
                    sc.copy(dst, src).then_inc(sm, 1)
            sc.drain().then_inc(ph, 1)
            sc.wait_ge(ph, 5)
            b = B["P1"]
            for ic in range(NIC):
                iw = ICW[ic]
                for tb in range(NTB):
                    g = ic * NTB + tb
                    sc.wait_ge(tm, b["tm"] + g + 1)
                    if g >= 4:
                        sc.wait_ge(qt[g % 4], 16 * (g // 4))
                    sc.activation(stage_s[g % 4][:, 0:iw],
                                  pB[:, (g % 2) * 512:(g % 2) * 512 + iw],
                                  AF.Silu).then_inc(sm, 1)
            sc.drain().then_inc(ph, 1)
            sc.wait_ge(ph, 10)
            b = B["P3"]
            for ic in range(NIC):
                iw = ICW[ic]
                for tb in range(NTB):
                    g = ic * NTB + tb
                    sc.wait_ge(tm, b["tm"] + g + 1)
                    if g >= 4:
                        sc.wait_ge(qt[g % 4], 16 * (NG // 4 + g // 4))
                    sc.copy(stage_u[g % 4][:, 0:iw],
                            pB[:, (g % 2) * 512:(g % 2) * 512 + iw]).then_inc(sm, 1)
            sc.drain().then_inc(ph, 1)
            sc.wait_ge(ph, 15)
            b = B["P4"]
            for tb in range(NTB):
                sc.wait_ge(vm, b["vm"] + 5 * tb + 1)
                sc.activation(exps[:], pB[:, 512:520], AF.Exp, bias=nmx[:, 0:1],
                              accum_out=sume[:, 0:1]).then_inc(sm, 1)
            sc.drain().then_inc(ph, 1)
            sc.wait_ge(ph, 20)
            b = B["P2"]
            VMP = 2 * MED_ITERS + 4
            SMP = 2 * MED_ITERS + 2
            NPAIR = NTB // 2
            for p_ in range(NPAIR):
                smp = b["sm"] + p_ * SMP
                vmp = b["vm"] + p_ * VMP
                for pi in range(2):
                    sc.wait_ge(qx[pi], 16 * (NTB // 2 + p_ + 1))
                    sc.activation(aP[pi][:], s_t[pi][:], AF.Abs).then_inc(sm, 1)
                for k in range(MED_ITERS):
                    for pi in range(2):
                        sc.wait_ge(vm, vmp + 2 * k + 1 + pi)
                        sc.activation(trash_m[:], aP[pi][:], AF.Sign,
                                      bias=nthrP[pi][:, 0:1],
                                      accum_out=crawP[pi][:, 0:1]).then_inc(sm, 1)
            sc.drain().then_inc(ph, 1)
            sc.wait_ge(ph, 25)
            b = B["P5"]
            for q in range(NQ):
                r0_ = q * TQ
                vmq = b["vm"] + q * VMQ
                for it in range(NIT):
                    if it >= 4:
                        sc.wait_ge(vm, vmq + (it - 4) + 1)
                    elif q >= 1:
                        last_it = ((NIT - 1 - it) // 4) * 4 + it
                        sc.wait_ge(vm, vmq - VMQ + last_it + 1)
                    sc.dma_start(out=smt_in[it % 4][:],
                                 in_=sm_sp[r0_:r0_ + TQ, it * P:(it + 1) * P],
                                 transpose=True).then_inc(qp[it % 4], 16)
                    sc.dma_start(out=u_in[it % 4][:],
                                 in_=u_sp[r0_:r0_ + TQ, it * P:(it + 1) * P],
                                 transpose=True).then_inc(qu[it % 4], 16)
            sc.drain().then_inc(ph, 1)

    return nc


_nc_cache = None


def kernel(L0_attn_x, L0_attn_w_gate, L0_attn_w_up, L0_attn_w_down,
           L0_ffn_A, L0_ffn_B, L0_ffn_Wr, L0_ffn_br):
    global _nc_cache
    x = np.ascontiguousarray(np.asarray(L0_attn_x, dtype=np.float32).reshape(16384, 4096))
    wgT = np.ascontiguousarray(np.asarray(L0_attn_w_gate, dtype=np.float32).T)
    wuT = np.ascontiguousarray(np.asarray(L0_attn_w_up, dtype=np.float32).T)
    wdT = np.ascontiguousarray(np.asarray(L0_attn_w_down, dtype=np.float32).T)
    aT = np.ascontiguousarray(np.asarray(L0_ffn_A, dtype=np.float32).reshape(128, 4096).T)
    bF = np.ascontiguousarray(
        np.asarray(L0_ffn_B, dtype=np.float32).transpose(0, 2, 1).reshape(128, 4096))
    wrT = np.ascontiguousarray(np.asarray(L0_ffn_Wr, dtype=np.float32).T)
    br1 = np.ascontiguousarray(np.asarray(L0_ffn_br, dtype=np.float32).reshape(1, 8))
    eye = np.eye(128, dtype=np.float32)

    if _nc_cache is None:
        _nc_cache = build_program()
    nc = _nc_cache

    in_maps = []
    for c in range(8):
        in_maps.append(dict(
            x_in=np.ascontiguousarray(x[c * 2048:(c + 1) * 2048]),
            wgT_in=wgT, wuT_in=wuT, wdT_in=wdT,
            aT_in=aT, bF_in=bF, wrT_in=wrT, br_in=br1, eye_in=eye))
    res = run_bass_kernel_spmd(nc, in_maps, core_ids=list(range(8)))
    y = np.concatenate([r["y_out"] for r in res.results], axis=0)
    return y.reshape(2, 8192, 4096)


# version 10
# speedup vs baseline: 1.1629x; 1.0398x over previous
import numpy as np
import concourse.bass as bass
import concourse.mybir as mybir
from concourse.bass_utils import run_bass_kernel_spmd
from contextlib import ExitStack

F32 = mybir.dt.float32
BF16 = mybir.dt.bfloat16
U8 = mybir.dt.uint8
AF = mybir.ActivationFunctionType
ALU = mybir.AluOpType
AX = mybir.AxisListType

P = 128
T = 2048
D = 4096
I = 11008
ND = D // P
NTB = T // P
NIT = I // P
K_KEEP = I // 2
IC = 512
ICB = list(range(0, I, IC))
ICW = [min(IC, I - s0) for s0 in ICB]
NIC = len(ICB)
NHC = D // 512
NQ = 4
TQ = T // NQ
NTBQ = TQ // P

MED_ITERS = 10
MED_BISECT = 3
MED_HI = 2.56
MED_THR0 = 0.27
MED_CLAMP = 0.02
LN_EPS = 1.0e-5

NG = NIC * NTB


def build_program():
    nc = bass.Bass("TRN2", target_bir_lowering=False, detect_race_conditions=False)

    x_in = nc.dram_tensor("x_in", [T, D], F32, kind="ExternalInput").ap()
    wgT_in = nc.dram_tensor("wgT_in", [D, I], F32, kind="ExternalInput").ap()
    wuT_in = nc.dram_tensor("wuT_in", [D, I], F32, kind="ExternalInput").ap()
    wdT_in = nc.dram_tensor("wdT_in", [I, D], F32, kind="ExternalInput").ap()
    aT_in = nc.dram_tensor("aT_in", [D, 128], F32, kind="ExternalInput").ap()
    bF_in = nc.dram_tensor("bF_in", [128, D], F32, kind="ExternalInput").ap()
    wrT_in = nc.dram_tensor("wrT_in", [D, 8], F32, kind="ExternalInput").ap()
    br_in = nc.dram_tensor("br_in", [1, 8], F32, kind="ExternalInput").ap()
    eye_in = nc.dram_tensor("eye_in", [128, 128], F32, kind="ExternalInput").ap()
    y_out = nc.dram_tensor("y_out", [T, D], F32, kind="ExternalOutput").ap()

    _SPK = "ExternalOutput" if globals().get("DEBUG_SPILL") else "Internal"
    s_sp = nc.dram_tensor("s_sp", [T, I], F32, kind=_SPK).ap()
    u_sp = nc.dram_tensor("u_sp", [T, I], BF16, kind=_SPK).ap()
    sm_sp = nc.dram_tensor("sm_sp", [T, I], BF16, kind=_SPK).ap()
    ffn_sp = nc.dram_tensor("ffn_sp", [T, D], BF16, kind=_SPK).ap()
    wd_bf = nc.dram_tensor("wd_bf", [I, D], BF16, kind="Internal").ap()

    wgT_t = wgT_in.rearrange("(nd p) i -> p nd i", p=P)
    wuT_t = wuT_in.rearrange("(nd p) i -> p nd i", p=P)
    aT_t = aT_in.rearrange("(nd p) c -> p nd c", p=P)
    wrT_t = wrT_in.rearrange("(nd p) c -> p nd c", p=P)

    with ExitStack() as ctx:
        e = ctx.enter_context

        eyeF = e(nc.sbuf_tensor("eyeF", [128, 128], F32))
        eyeB = e(nc.sbuf_tensor("eyeB", [128, 128], BF16))
        ones1 = e(nc.sbuf_tensor("ones1", [1, 128], BF16))
        br_sb = e(nc.sbuf_tensor("br_sb", [1, 8], BF16))

        def st(name):
            return e(nc.sbuf_tensor(name, [128, 1], F32))
        sum1, mean, ssq, var, ve, sq, r0, t1, t2, t3, rstd, eps_t = (
            st(n) for n in ["sum1", "mean", "ssq", "var", "ve", "sq", "r0",
                            "t1", "t2", "t3", "rstd", "eps_t"])
        loS = [[st(f"lo{pi}_{j}") for j in range(2)] for pi in range(2)]
        hiS = [[st(f"hi{pi}_{j}") for j in range(2)] for pi in range(2)]
        cloS = [[st(f"clo{pi}_{j}") for j in range(2)] for pi in range(2)]
        chiS = [[st(f"chi{pi}_{j}") for j in range(2)] for pi in range(2)]
        thrP = [st("thr0"), st("thr1")]
        nthrP = [st("nthr0"), st("nthr1")]
        crawP = [st("craw0"), st("craw1")]
        mhi = e(nc.sbuf_tensor("mhi", [128, 1], U8))
        mlo = e(nc.sbuf_tensor("mlo", [128, 1], U8))
        den, num, rq, dd, rd, marg, lob, hib, tmp1 = (
            st(n) for n in ["den", "num", "rq", "dd", "rd", "marg", "lob", "hib", "tmp1"])
        mx, nmx, sume, rs = (st(n) for n in ["mx", "nmx", "sume", "rs"])
        exps = e(nc.sbuf_tensor("exps", [128, 8], F32))
        sco = e(nc.sbuf_tensor("sco", [128, 8], F32))
        ones16 = e(nc.sbuf_tensor("ones16", [128, 16], F32))

        ARENA = 205824
        arena = e(nc.sbuf_tensor("arena", [128, ARENA], U8))
        base = nc.lookup_mloc(arena).addr

        def place(name, shape, dtype, off):
            assert off + int(np.prod(shape[1:])) * mybir.dt.size(dtype) <= ARENA, name
            return nc.alloc_sbuf_tensor_at(name, shape, dtype, offset=base + off)

        xnT = place("xnT", [128, ND * T], BF16, 0)
        xb = [place(f"xb{j}", [128, D], F32, 131072 + j * 16384) for j in range(2)]
        xn_sb = place("xn_sb", [128, D], F32, 163840)
        trash_sq = place("trash_sq", [128, D], F32, 180224)
        wslab = [place(f"wslab{j}", [128, ND * IC], BF16, 131072 + j * 32768)
                 for j in range(2)]
        stage_s = [place(f"stage_s{j}", [128, IC], F32, 196608 + j * 2048)
                   for j in range(4)]
        stage_u = [place(f"stage_u{j}", [128, IC], BF16, 196608 + j * 1024)
                   for j in range(4)]
        aT_sb = place("aT_sb", [128, ND * 128], BF16, 131072)
        wrT_sb = place("wrT_sb", [128, ND * 8], BF16, 139264)
        bF_sb = place("bF_sb", [128, D], BF16, 139776)
        ffnst = [place(f"ffnst{j}", [128, 2048], BF16, 147968 + j * 4096)
                 for j in range(2)]
        z_sb = place("z_sb", [128, 128], F32, 156160)
        zT_sb = place("zT_sb", [128, 128], BF16, 156672)
        srep = place("srep", [128, 128], F32, 156928)
        s_t = [place(f"s_t{j}", [128, I], F32, j * 44032) for j in range(2)]
        aP = [place(f"aP{j}", [128, I], BF16, 88064 + j * 22016) for j in range(2)]
        sm0_sb = place("sm0_sb", [128, I], BF16, 132096)
        u_t2 = place("u_t2", [128, I], BF16, 154112)
        trash_m = place("trash_m", [128, I], BF16, 176128)
        smtqB = [place(f"smtq{j}", [128, NIT * 512], BF16, j * 88064) for j in range(2)]
        wdring = [place(f"wdring{j}", [128, 2 * 512], BF16, 176128 + j * 2048) for j in range(4)]
        x_t = [place(f"x_t{j}", [128, 512], F32, 184320 + j * 2048) for j in range(3)]
        ffn_t = [place(f"ffn_t{j}", [128, 512], F32, 190464 + j * 2048) for j in range(3)]
        t_a = [place(f"t_a{j}", [128, 512], F32, 196608 + j * 2048) for j in range(2)]
        outst = [place(f"outst{j}", [128, 512], F32, 200704 + j * 2048) for j in range(2)]

        pA = nc.alloc_psum_tensor("pA", [128, 2048], F32)
        pB = nc.alloc_psum_tensor("pB", [128, 1024], F32)
        pE = nc.alloc_psum_tensor("pE", [128, 1024], F32)

        ph = e(nc.semaphore("ph"))
        tm = e(nc.semaphore("tm"))
        vm = e(nc.semaphore("vm"))
        sm = e(nc.semaphore("sm"))
        vv = e(nc.semaphore("vv"))
        qe = e(nc.semaphore("qe"))
        qx = [e(nc.semaphore(f"qx{j}")) for j in range(2)]
        qs = [e(nc.semaphore(f"qs{j}")) for j in range(2)]
        qt = [e(nc.semaphore(f"qt{j}")) for j in range(4)]
        qmP = [e(nc.semaphore(f"qm{j}")) for j in range(2)]
        qu2 = e(nc.semaphore("qu2p"))
        qc = e(nc.semaphore("qc"))
        qwc = e(nc.semaphore("qwc"))
        qff = [e(nc.semaphore(f"qff{j}")) for j in range(2)]
        qp = [e(nc.semaphore(f"qp{j}")) for j in range(4)]
        qu = [e(nc.semaphore(f"qu{j}")) for j in range(4)]
        qd = [e(nc.semaphore(f"qd{j}")) for j in range(4)]
        qxt = [e(nc.semaphore(f"qxt{j}")) for j in range(4)]
        qf = [e(nc.semaphore(f"qf{j}")) for j in range(4)]
        qout = [e(nc.semaphore(f"qout{j}")) for j in range(4)]

        D_ = {
            "P0": dict(tm=4 * NTB, vm=2 * NTB, sm=5 * NTB),
            "P1": dict(tm=NG, vm=0, sm=NG),
            "P3": dict(tm=NG, vm=0, sm=NG),
            "P4": dict(tm=4 * NTB, vm=5 * NTB, sm=NTB),
            "P2": dict(tm=0, vm=(MED_ITERS + 2) * NTB, sm=(1 + MED_ITERS) * NTB),
            "P5": dict(tm=NQ * (NIT // 2) * NHC, vm=NQ * NHC * NTBQ, sm=0),
        }
        PH_ORDER = ["P0", "P1", "P3", "P4", "P2", "P5"]
        B = {}
        run = dict(tm=0, vm=0, sm=0)
        for pn in PH_ORDER:
            B[pn] = dict(run)
            for k in run:
                run[k] += D_[pn][k]

        xnT_v = xnT.ap().rearrange("p (nd t) -> p nd t", t=T)

        def xnT_tile(nd, tb):
            return xnT_v[:, nd, tb * P:(tb + 1) * P]

        NGRP = NIT // 2
        TMQ = NGRP * NHC
        VMQ = NHC * NTBQ

        blk = e(nc.Block())

        @blk.sync
        def _(sync):
            b = B["P0"]
            sync.dma_start(out=eyeF[:], in_=eye_in).then_inc(qe, 16)
            for tb in range(NTB):
                if tb >= 2:
                    sync.wait_ge(vm, b["vm"] + 2 * (tb - 2) + 2)
                sync.dma_start(out=xb[tb % 2][:], in_=x_in[tb * P:(tb + 1) * P, :]
                               ).then_inc(qx[tb % 2], 16)
            sync.nop().then_inc(ph, 1)
            sync.wait_ge(ph, 5)
            b = B["P1"]
            for ic in range(NIC):
                i0, iw = ICB[ic], ICW[ic]
                for tb in range(NTB):
                    g = ic * NTB + tb
                    sync.wait_ge(sm, b["sm"] + g + 1)
                    sync.dma_start(out=s_sp[tb * P:(tb + 1) * P, i0:i0 + iw],
                                   in_=stage_s[g % 4][:, 0:iw]).then_inc(qt[g % 4], 16)
            for j in range(4):
                sync.wait_ge(qt[j], 16 * (NG // 4))
            sync.nop().then_inc(ph, 1)
            sync.wait_ge(ph, 10)
            b = B["P3"]
            for ic in range(NIC):
                i0, iw = ICB[ic], ICW[ic]
                for tb in range(NTB):
                    g = ic * NTB + tb
                    sync.wait_ge(sm, b["sm"] + g + 1)
                    sync.dma_start(out=u_sp[tb * P:(tb + 1) * P, i0:i0 + iw],
                                   in_=stage_u[g % 4][:, 0:iw]).then_inc(qt[g % 4], 16)
            for j in range(4):
                sync.wait_ge(qt[j], 16 * (2 * NG // 4))
            sync.nop().then_inc(ph, 1)
            sync.wait_ge(ph, 15)
            b = B["P4"]
            for tb in range(NTB):
                for h2 in range(2):
                    sync.wait_ge(vm, b["vm"] + 5 * tb + 4 + h2)
                    sync.dma_start(out=ffn_sp[tb * P:(tb + 1) * P,
                                              h2 * 2048:(h2 + 1) * 2048],
                                   in_=ffnst[h2][:]).then_inc(qff[h2], 16)
            for j in range(2):
                sync.wait_ge(qff[j], 16 * NTB)
            sync.nop().then_inc(ph, 1)
            sync.wait_ge(ph, 20)
            b = B["P2"]
            VMP = 2 * MED_ITERS + 4
            NPAIR = NTB // 2
            for p_ in range(NPAIR):
                for pi in range(2):
                    tb = 2 * p_ + pi
                    if p_ >= 1:
                        sync.wait_ge(vm, b["vm"] + (p_ - 1) * VMP + 2 * MED_ITERS + 3 + pi)
                    sync.dma_start(out=s_t[pi][:], in_=s_sp[tb * P:(tb + 1) * P, :]
                                   ).then_inc(qx[pi], 16)
                for pi in range(2):
                    tb = 2 * p_ + pi
                    m = 2 * p_ + pi
                    if m >= 1:
                        sync.wait_ge(vm, b["vm"] + (m - 1) // 2 * VMP
                                     + 2 * MED_ITERS + 3 + (m - 1) % 2)
                    sync.dma_start(out=u_t2[:], in_=u_sp[tb * P:(tb + 1) * P, :]
                                   ).then_inc(qu2, 16)
                    sync.wait_ge(vm, b["vm"] + p_ * VMP + 2 * MED_ITERS + 3 + pi)
                    sync.dma_start(out=sm_sp[tb * P:(tb + 1) * P, :], in_=sm0_sb[:]
                                   ).then_inc(qmP[pi], 16)
            for pi in range(2):
                sync.wait_ge(qmP[pi], 16 * NPAIR)
            sync.nop().then_inc(ph, 1)
            sync.wait_ge(ph, 25)
            b = B["P5"]
            N3 = [11, 11, 10]
            for q in range(NQ):
                r0_ = q * TQ
                vmq = b["vm"] + q * VMQ
                NE = NHC * NTBQ
                def _xload(ev):
                    hc_, tb_ = divmod(ev, NTBQ)
                    if ev >= 3:
                        sync.wait_ge(vm, vmq + ev - 3 + 1)
                    elif q >= 1:
                        sync.wait_ge(vm, vmq - VMQ + (NE - 3 + ev) + 1)
                    sync.dma_start(
                        out=x_t[ev % 3][:],
                        in_=x_in[r0_ + tb_ * P:r0_ + (tb_ + 1) * P,
                                 hc_ * 512:(hc_ + 1) * 512]).then_inc(qxt[ev % 3], 16)
                _xload(0)
                _xload(1)
                for ev in range(NE):
                    if ev + 2 < NE:
                        _xload(ev + 2)
                    hc_, tb_ = divmod(ev, NTBQ)
                    sync.wait_ge(vm, vmq + ev + 1)
                    sync.dma_start(
                        out=y_out[r0_ + tb_ * P:r0_ + (tb_ + 1) * P,
                                  hc_ * 512:(hc_ + 1) * 512],
                        in_=outst[ev % 2][:]).then_inc(qout[ev % 2], 16)
            for j in range(2):
                sync.wait_ge(qout[j], 16 * (NQ * NHC * NTBQ // 2))
            sync.nop().then_inc(ph, 1)

        @blk.gpsimd
        def _(gp):
            gp.nop().then_inc(ph, 1)
            gp.wait_ge(ph, 5)
            b = B["P1"]
            for ic in range(NIC):
                i0, iw = ICB[ic], ICW[ic]
                if ic >= 2:
                    gp.wait_ge(tm, b["tm"] + (ic - 1) * NTB)
                dst = wslab[ic % 2].ap().rearrange("p (nd i) -> p nd i", i=IC)[:, :, 0:iw]
                gp.dma_start(out=dst, in_=wgT_t[:, :, i0:i0 + iw]).then_inc(qs[ic % 2], 16)
                if ic % 2 == 0:
                    c = ic // 2
                    r0c, rw = ICB[c], ICW[c]
                    gp.dma_start(out=wd_bf[r0c:r0c + rw, :],
                                 in_=wdT_in[r0c:r0c + rw, :]).then_inc(qwc, 16)
            gp.nop().then_inc(ph, 1)
            gp.wait_ge(ph, 10)
            b = B["P3"]
            for ic in range(NIC):
                i0, iw = ICB[ic], ICW[ic]
                if ic >= 2:
                    gp.wait_ge(tm, b["tm"] + (ic - 1) * NTB)
                dst = wslab[ic % 2].ap().rearrange("p (nd i) -> p nd i", i=IC)[:, :, 0:iw]
                gp.dma_start(out=dst, in_=wuT_t[:, :, i0:i0 + iw]).then_inc(qs[ic % 2], 16)
                if ic % 2 == 0:
                    c = 11 + ic // 2
                    r0c, rw = ICB[c], ICW[c]
                    gp.dma_start(out=wd_bf[r0c:r0c + rw, :],
                                 in_=wdT_in[r0c:r0c + rw, :]).then_inc(qwc, 16)
            gp.nop().then_inc(ph, 1)
            gp.wait_ge(ph, 15)
            gp.dma_start(out=aT_sb.ap().rearrange("p (nd c) -> p nd c", c=128),
                         in_=aT_t).then_inc(qc, 16)
            gp.dma_start(out=wrT_sb.ap().rearrange("p (nd c) -> p nd c", c=8),
                         in_=wrT_t).then_inc(qc, 16)
            gp.dma_start(out=bF_sb[:], in_=bF_in).then_inc(qc, 16)
            gp.dma_start(out=br_sb[:], in_=br_in).then_inc(qc, 16)
            gp.dma_start(out=eyeB[:], in_=eye_in).then_inc(qc, 16)
            gp.nop().then_inc(ph, 1)
            gp.wait_ge(ph, 20)
            gp.nop().then_inc(ph, 1)
            gp.wait_ge(ph, 25)
            gp.wait_ge(qwc, 16 * NIC)
            b = B["P5"]
            for q in range(NQ):
                r0_ = q * TQ
                tmq = b["tm"] + q * TMQ
                vmq = b["vm"] + q * VMQ
                NE = NHC * NTBQ
                for hc in range(NHC):
                    for g in range(NGRP):
                        gidx = hc * NGRP + g
                        if gidx >= 4:
                            gp.wait_ge(tm, tmq + gidx - 4 + 1)
                        elif q >= 1:
                            gp.wait_ge(tm, tmq - TMQ + (TMQ - 4 + gidx) + 1)
                        it0 = 2 * g
                        src_ = wd_bf[it0 * P:it0 * P + 2 * P, hc * 512:(hc + 1) * 512]
                        gp.dma_start(
                            out=wdring[gidx % 4].ap().rearrange(
                                "p (g2 h) -> p g2 h", h=512),
                            in_=src_.rearrange("(g2 p) h -> p g2 h", p=P)
                        ).then_inc(qd[gidx % 4], 16)
                    for tb in range(NTBQ):
                        ev = hc * NTBQ + tb
                        if ev >= 3:
                            gp.wait_ge(vm, vmq + ev - 3 + 1)
                        elif q >= 1:
                            gp.wait_ge(vm, vmq - VMQ + (NE - 3 + ev) + 1)
                        gp.dma_start(
                            out=ffn_t[ev % 3][:],
                            in_=ffn_sp[r0_ + tb * P:r0_ + (tb + 1) * P,
                                       hc * 512:(hc + 1) * 512]).then_inc(qf[ev % 3], 16)
            gp.nop().then_inc(ph, 1)

        @blk.tensor
        def _(t):
            b = B["P0"]
            t.wait_ge(qe, 16)
            pregs = [pA[:, 0:1024], pA[:, 1024:2048], pB[:, 0:1024]]
            for tb in range(NTB):
                t.wait_ge(vm, b["vm"] + 2 * tb + 2)
                for bt in range(4):
                    gb = 4 * tb + bt
                    if gb >= 3:
                        pt, pb_ = divmod(gb - 3, 4)
                        t.wait_ge(sm, b["sm"] + 5 * pt + 2 + pb_)
                    reg = pregs[gb % 3]
                    for j in range(8):
                        nd = 8 * bt + j
                        mi = t.transpose(reg[:, j * 128:(j + 1) * 128],
                                         xn_sb[:, nd * 128:(nd + 1) * 128], eyeF[:])
                    mi.then_inc(tm, 1)
            t.drain().then_inc(ph, 1)
            t.wait_ge(ph, 5)
            b = B["P1"]
            for ic in range(NIC):
                iw = ICW[ic]
                t.wait_ge(qs[ic % 2], 16 * (ic // 2 + 1))
                sl = wslab[ic % 2].ap().rearrange("p (nd i) -> p nd i", i=IC)
                for tb in range(NTB):
                    g = ic * NTB + tb
                    pb_ = pB[:, (g % 2) * 512:(g % 2) * 512 + iw]
                    if g >= 2:
                        t.wait_ge(sm, b["sm"] + g - 1)
                    for nd in range(ND):
                        mi = t.matmul(pb_, xnT_tile(nd, tb), sl[:, nd, 0:iw],
                                      start=(nd == 0), stop=(nd == ND - 1))
                    mi.then_inc(tm, 1)
            t.drain().then_inc(ph, 1)
            t.wait_ge(ph, 10)
            b = B["P3"]
            for ic in range(NIC):
                iw = ICW[ic]
                t.wait_ge(qs[ic % 2], 16 * (NIC // 2 + ic // 2 + 1))
                sl = wslab[ic % 2].ap().rearrange("p (nd i) -> p nd i", i=IC)
                for tb in range(NTB):
                    g = ic * NTB + tb
                    pb_ = pB[:, (g % 2) * 512:(g % 2) * 512 + iw]
                    if g >= 2:
                        t.wait_ge(sm, b["sm"] + g - 1)
                    for nd in range(ND):
                        mi = t.matmul(pb_, xnT_tile(nd, tb), sl[:, nd, 0:iw],
                                      start=(nd == 0), stop=(nd == ND - 1))
                    mi.then_inc(tm, 1)
            t.drain().then_inc(ph, 1)
            t.wait_ge(ph, 15)
            b = B["P4"]
            t.wait_ge(qc, 16 * 5)
            wr_v = wrT_sb.ap().rearrange("p (nd c) -> p nd c", c=8)
            aT_v = aT_sb.ap().rearrange("p (nd c) -> p nd c", c=128)
            for tb in range(NTB):
                if tb >= 1:
                    t.wait_ge(vm, b["vm"] + 5 * (tb - 1) + 5)
                for nd in range(ND):
                    t.matmul(pB[:, 0:128], xnT_tile(nd, tb), aT_v[:, nd, :],
                             start=(nd == 0), stop=(nd == ND - 1))
                    t.matmul(pB[:, 512:520], xnT_tile(nd, tb), wr_v[:, nd, :],
                             start=(nd == 0), stop=False)
                mi = t.matmul(pB[:, 512:520], ones1[:, :], br_sb[:, :],
                              start=False, stop=True)
                mi.then_inc(tm, 1)
                t.wait_ge(vm, b["vm"] + 5 * tb + 2)
                t.transpose(pB[:, 128:256], z_sb[:], eyeF[:]).then_inc(tm, 1)
                t.wait_ge(vm, b["vm"] + 5 * tb + 3)
                for h2 in range(2):
                    if h2 == 1:
                        t.wait_ge(vm, b["vm"] + 5 * tb + 4)
                    for k4 in range(4):
                        mi = t.matmul(pA[:, k4 * 512:(k4 + 1) * 512], zT_sb[:],
                                      bF_sb[:, h2 * 2048 + k4 * 512:
                                            h2 * 2048 + (k4 + 1) * 512],
                                      start=True, stop=True)
                    mi.then_inc(tm, 1)
            t.drain().then_inc(ph, 1)
            t.wait_ge(ph, 20)
            t.drain().then_inc(ph, 1)
            t.wait_ge(ph, 25)
            b = B["P5"]
            NLJ = [11, 11, 11, 11, 11, 11, 10, 10]
            qp8 = qp + qu
            for q in range(NQ):
                vmq = b["vm"] + q * VMQ
                tmq = b["tm"] + q * TMQ
                smtq = smtqB[q % 2]
                smtq_v = smtq.ap().rearrange("p (it t) -> p it t", t=512)
                for j in range(8):
                    t.wait_ge(qp8[j], 16 * (q + 1) * NLJ[j])
                psets = [
                    [pA[:, 0:512], pA[:, 512:1024], pA[:, 1024:1536], pA[:, 1536:2048]],
                    [pB[:, 0:512], pB[:, 512:1024], pE[:, 0:512], pE[:, 512:1024]],
                ]
                for hc in range(NHC):
                    ps = psets[hc % 2]
                    if hc >= 2:
                        t.wait_ge(vm, vmq + (hc - 1) * NTBQ)
                    elif q >= 1:
                        t.wait_ge(vm, vmq - VMQ + (hc + 7) * NTBQ)
                    for g in range(NGRP):
                        gidx = hc * NGRP + g
                        t.wait_ge(qd[gidx % 4],
                                  16 * ((q * TMQ + gidx) // 4 + 1))
                        wdv = wdring[gidx % 4].ap().rearrange("p (g2 h) -> p g2 h", h=512)
                        for j2 in range(2):
                            it = 2 * g + j2
                            for tb in range(NTBQ):
                                mi = t.matmul(ps[tb],
                                              smtq_v[:, it, tb * P:(tb + 1) * P],
                                              wdv[:, j2, :],
                                              start=(it == 0), stop=(it == NIT - 1))
                        mi.then_inc(tm, 1)
            t.drain().then_inc(ph, 1)

        @blk.vector
        def _(v):
            vvc = [0]
            def edge(ins):
                vvc[0] += 1
                ins.then_inc(vv, 1)
                v.wait_ge(vv, vvc[0])
            b = B["P0"]
            v.memset(ones16[:], 1.0)
            v.memset(ones1[:], 1.0)
            v.memset(eps_t[:, 0:1], LN_EPS)
            for tb in range(NTB):
                xbuf = xb[tb % 2]
                v.wait_ge(qx[tb % 2], 16 * (tb // 2 + 1))
                edge(v.reduce_sum(sum1[:, 0:1], xbuf[:], axis=AX.X))
                edge(v.tensor_scalar_mul(mean[:, 0:1], sum1[:, 0:1], 1.0 / D))
                v.scalar_tensor_tensor(out=trash_sq[:], in0=xbuf[:], scalar=mean[:, 0:1],
                                       in1=xbuf[:], op0=ALU.subtract, op1=ALU.mult)
                edge(v.reduce_sum(ssq[:, 0:1], trash_sq[:], axis=AX.X))
                edge(v.tensor_scalar_mul(var[:, 0:1], ssq[:, 0:1], 1.0 / D))
                v.tensor_scalar_add(ve[:, 0:1], var[:, 0:1], LN_EPS).then_inc(vm, 1)
                v.wait_ge(sm, b["sm"] + 5 * tb + 1)
                edge(v.reciprocal(r0[:, 0:1], sq[:, 0:1]))
                edge(v.tensor_mul(t1[:, 0:1], r0[:, 0:1], r0[:, 0:1]))
                edge(v.tensor_mul(t2[:, 0:1], ve[:, 0:1], t1[:, 0:1]))
                edge(v.tensor_scalar(out=t3[:, 0:1], in0=t2[:, 0:1], scalar1=-0.5,
                                     scalar2=1.5, op0=ALU.mult, op1=ALU.add))
                edge(v.tensor_mul(rstd[:, 0:1], r0[:, 0:1], t3[:, 0:1]))
                if tb >= 1:
                    v.wait_ge(tm, b["tm"] + 4 * tb)
                v.tensor_scalar(out=xn_sb[:], in0=xbuf[:], scalar1=mean[:, 0:1],
                                scalar2=rstd[:, 0:1], op0=ALU.subtract,
                                op1=ALU.mult).then_inc(vm, 1)
            v.drain().then_inc(ph, 1)
            v.wait_ge(ph, 5)
            v.drain().then_inc(ph, 1)
            v.wait_ge(ph, 10)
            v.drain().then_inc(ph, 1)
            v.wait_ge(ph, 15)
            b = B["P4"]
            for tb in range(NTB):
                v.wait_ge(tm, b["tm"] + 4 * tb + 1)
                edge(v.reduce_max(mx[:, 0:1], pB[:, 512:520], axis=AX.X))
                v.tensor_scalar_mul(nmx[:, 0:1], mx[:, 0:1], -1.0).then_inc(vm, 1)
                v.wait_ge(sm, b["sm"] + tb + 1)
                edge(v.reciprocal(rs[:, 0:1], sume[:, 0:1]))
                edge(v.tensor_scalar(out=sco[:], in0=exps[:], scalar1=rs[:, 0:1],
                                     scalar2=16.0, op0=ALU.mult, op1=ALU.mult))
                for h in range(8):
                    ins = v.tensor_scalar_mul(srep[:, h * 16:(h + 1) * 16], ones16[:],
                                              sco[:, h:h + 1])
                edge(ins)
                v.tensor_mul(z_sb[:], pB[:, 0:128], srep[:]).then_inc(vm, 1)
                v.wait_ge(tm, b["tm"] + 4 * tb + 2)
                v.tensor_copy(zT_sb[:], pB[:, 128:256]).then_inc(vm, 1)
                for h2 in range(2):
                    v.wait_ge(tm, b["tm"] + 4 * tb + 3 + h2)
                    if tb >= 1:
                        v.wait_ge(qff[h2], 16 * tb)
                    v.tensor_copy(ffnst[h2][:], pA[:, 0:2048]).then_inc(vm, 1)
            v.drain().then_inc(ph, 1)
            v.wait_ge(ph, 20)
            b = B["P2"]
            VMP = 2 * MED_ITERS + 4
            SMP = 2 * MED_ITERS + 2
            NPAIR = NTB // 2
            for p_ in range(NPAIR):
                vmp = b["vm"] + p_ * VMP
                smp = b["sm"] + p_ * SMP
                for pi in range(2):
                    v.memset(loS[pi][0][:, 0:1], 0.0)
                    v.memset(hiS[pi][0][:, 0:1], MED_HI)
                    v.memset(cloS[pi][0][:, 0:1], float(I))
                    v.memset(chiS[pi][0][:, 0:1], -float(I))
                    v.memset(thrP[pi][:, 0:1], MED_THR0)
                    v.memset(nthrP[pi][:, 0:1], -MED_THR0).then_inc(vm, 1)
                for k in range(MED_ITERS):
                    for pi in range(2):
                        thr = thrP[pi]
                        nthr = nthrP[pi]
                        craw = crawP[pi]
                        cs, ns = k % 2, (k + 1) % 2
                        v.wait_ge(sm, smp + 2 * k + 3 + pi)
                        edge(v.tensor_single_scalar(mhi[:, 0:1], craw[:, 0:1], 0.0,
                                                    op=ALU.is_lt))
                        v.tensor_single_scalar(mlo[:, 0:1], craw[:, 0:1], 0.0,
                                               op=ALU.is_ge)
                        edge(v.select(hiS[pi][ns][:, 0:1], mhi[:, 0:1], thr[:, 0:1],
                                      hiS[pi][cs][:, 0:1]))
                        v.select(chiS[pi][ns][:, 0:1], mhi[:, 0:1], craw[:, 0:1],
                                 chiS[pi][cs][:, 0:1])
                        v.select(loS[pi][ns][:, 0:1], mlo[:, 0:1], thr[:, 0:1],
                                 loS[pi][cs][:, 0:1])
                        edge(v.select(cloS[pi][ns][:, 0:1], mlo[:, 0:1], craw[:, 0:1],
                                      cloS[pi][cs][:, 0:1]))
                        if k < MED_BISECT:
                            edge(v.tensor_add(tmp1[:, 0:1], loS[pi][ns][:, 0:1],
                                              hiS[pi][ns][:, 0:1]))
                            edge(v.tensor_scalar_mul(thr[:, 0:1], tmp1[:, 0:1], 0.5))
                        else:
                            v.tensor_sub(den[:, 0:1], cloS[pi][ns][:, 0:1],
                                         chiS[pi][ns][:, 0:1])
                            edge(v.tensor_sub(dd[:, 0:1], hiS[pi][ns][:, 0:1],
                                              loS[pi][ns][:, 0:1]))
                            v.reciprocal(tmp1[:, 0:1], den[:, 0:1])
                            edge(v.tensor_scalar_mul(marg[:, 0:1], dd[:, 0:1], MED_CLAMP))
                            v.tensor_mul(rq[:, 0:1], cloS[pi][ns][:, 0:1], tmp1[:, 0:1])
                            v.tensor_add(lob[:, 0:1], loS[pi][ns][:, 0:1], marg[:, 0:1])
                            edge(v.tensor_sub(hib[:, 0:1], hiS[pi][ns][:, 0:1],
                                              marg[:, 0:1]))
                            edge(v.tensor_mul(rd[:, 0:1], rq[:, 0:1], dd[:, 0:1]))
                            edge(v.tensor_add(thr[:, 0:1], loS[pi][ns][:, 0:1],
                                              rd[:, 0:1]))
                            edge(v.tensor_max(thr[:, 0:1], thr[:, 0:1], lob[:, 0:1]))
                            edge(v.tensor_tensor(thr[:, 0:1], thr[:, 0:1], hib[:, 0:1],
                                                 op=ALU.min))
                        v.tensor_scalar_mul(nthr[:, 0:1], thr[:, 0:1],
                                            -1.0).then_inc(vm, 1)
                for pi in range(2):
                    m = 2 * p_ + pi
                    if m >= 1:
                        v.wait_ge(qmP[(m - 1) % 2], 16 * ((m - 1) // 2 + 1))
                    v.scalar_tensor_tensor(out=sm0_sb[:], in0=aP[pi][:],
                                           scalar=thrP[pi][:, 0:1],
                                           in1=s_t[pi][:], op0=ALU.is_gt,
                                           op1=ALU.mult)
                    v.wait_ge(qu2, 16 * (m + 1))
                    v.tensor_mul(sm0_sb[:], sm0_sb[:], u_t2[:]).then_inc(vm, 1)
            v.drain().then_inc(ph, 1)
            v.wait_ge(ph, 25)
            b = B["P5"]
            N3 = [11, 11, 10]
            psets = [
                [pA[:, 0:512], pA[:, 512:1024], pA[:, 1024:1536], pA[:, 1536:2048]],
                [pB[:, 0:512], pB[:, 512:1024], pE[:, 0:512], pE[:, 512:1024]],
            ]
            for q in range(NQ):
                vmq = b["vm"] + q * VMQ
                tmq = b["tm"] + q * TMQ
                for hc in range(NHC):
                    ps = psets[hc % 2]
                    for tb in range(NTBQ):
                        ev = hc * NTBQ + tb
                        v.wait_ge(tm, tmq + (hc + 1) * NGRP)
                        v.wait_ge(qxt[ev % 3], 16 * (q * N3[ev % 3] + ev // 3 + 1))
                        v.wait_ge(qf[ev % 3], 16 * (q * N3[ev % 3] + ev // 3 + 1))
                        v.tensor_add(t_a[ev % 2][:], ps[tb], x_t[ev % 3][:])
                        if ev >= 2 or q >= 1:
                            v.wait_ge(qout[ev % 2],
                                      16 * (q * (NHC * NTBQ // 2) + ev // 2))
                        v.tensor_add(outst[ev % 2][:], t_a[ev % 2][:],
                                     ffn_t[ev % 3][:]).then_inc(vm, 1)
            v.drain().then_inc(ph, 1)

        @blk.scalar
        def _(sc):
            b = B["P0"]
            pregs = [pA[:, 0:1024], pA[:, 1024:2048], pB[:, 0:1024]]
            for tb in range(NTB):
                sc.wait_ge(vm, b["vm"] + 2 * tb + 1)
                sc.activation(sq[:, 0:1], var[:, 0:1], AF.Sqrt,
                              bias=eps_t[:, 0:1]).then_inc(sm, 1)
                for bt in range(4):
                    gb = 4 * tb + bt
                    sc.wait_ge(tm, b["tm"] + gb + 1)
                    src = pregs[gb % 3].rearrange("p (j t) -> p j t", t=128)
                    dst = xnT_v[:, 8 * bt:8 * bt + 8, tb * P:(tb + 1) * P]
                    sc.copy(dst, src).then_inc(sm, 1)
            sc.drain().then_inc(ph, 1)
            sc.wait_ge(ph, 5)
            b = B["P1"]
            for ic in range(NIC):
                iw = ICW[ic]
                for tb in range(NTB):
                    g = ic * NTB + tb
                    sc.wait_ge(tm, b["tm"] + g + 1)
                    if g >= 4:
                        sc.wait_ge(qt[g % 4], 16 * (g // 4))
                    sc.activation(stage_s[g % 4][:, 0:iw],
                                  pB[:, (g % 2) * 512:(g % 2) * 512 + iw],
                                  AF.Silu).then_inc(sm, 1)
            sc.drain().then_inc(ph, 1)
            sc.wait_ge(ph, 10)
            b = B["P3"]
            for ic in range(NIC):
                iw = ICW[ic]
                for tb in range(NTB):
                    g = ic * NTB + tb
                    sc.wait_ge(tm, b["tm"] + g + 1)
                    if g >= 4:
                        sc.wait_ge(qt[g % 4], 16 * (NG // 4 + g // 4))
                    sc.copy(stage_u[g % 4][:, 0:iw],
                            pB[:, (g % 2) * 512:(g % 2) * 512 + iw]).then_inc(sm, 1)
            sc.drain().then_inc(ph, 1)
            sc.wait_ge(ph, 15)
            b = B["P4"]
            for tb in range(NTB):
                sc.wait_ge(vm, b["vm"] + 5 * tb + 1)
                sc.activation(exps[:], pB[:, 512:520], AF.Exp, bias=nmx[:, 0:1],
                              accum_out=sume[:, 0:1]).then_inc(sm, 1)
            sc.drain().then_inc(ph, 1)
            sc.wait_ge(ph, 20)
            b = B["P2"]
            VMP = 2 * MED_ITERS + 4
            SMP = 2 * MED_ITERS + 2
            NPAIR = NTB // 2
            for p_ in range(NPAIR):
                smp = b["sm"] + p_ * SMP
                vmp = b["vm"] + p_ * VMP
                for pi in range(2):
                    sc.wait_ge(qx[pi], 16 * (NTB // 2 + p_ + 1))
                    sc.activation(aP[pi][:], s_t[pi][:], AF.Abs).then_inc(sm, 1)
                for k in range(MED_ITERS):
                    for pi in range(2):
                        sc.wait_ge(vm, vmp + 2 * k + 1 + pi)
                        sc.activation(trash_m[:], aP[pi][:], AF.Sign,
                                      bias=nthrP[pi][:, 0:1],
                                      accum_out=crawP[pi][:, 0:1]).then_inc(sm, 1)
            sc.drain().then_inc(ph, 1)
            sc.wait_ge(ph, 25)
            b = B["P5"]
            qp8 = qp + qu
            for q in range(NQ):
                r0_ = q * TQ
                if q >= 2:
                    sc.wait_ge(tm, b["tm"] + (q - 1) * TMQ)
                for it in range(NIT):
                    sc.dma_start(out=smtqB[q % 2][:, it * 512:(it + 1) * 512],
                                 in_=sm_sp[r0_:r0_ + TQ, it * P:(it + 1) * P],
                                 transpose=True).then_inc(qp8[it % 8], 16)
            sc.drain().then_inc(ph, 1)

    return nc


_nc_cache = None


def kernel(L0_attn_x, L0_attn_w_gate, L0_attn_w_up, L0_attn_w_down,
           L0_ffn_A, L0_ffn_B, L0_ffn_Wr, L0_ffn_br):
    global _nc_cache
    x = np.ascontiguousarray(np.asarray(L0_attn_x, dtype=np.float32).reshape(16384, 4096))
    wgT = np.ascontiguousarray(np.asarray(L0_attn_w_gate, dtype=np.float32).T)
    wuT = np.ascontiguousarray(np.asarray(L0_attn_w_up, dtype=np.float32).T)
    wdT = np.ascontiguousarray(np.asarray(L0_attn_w_down, dtype=np.float32).T)
    aT = np.ascontiguousarray(np.asarray(L0_ffn_A, dtype=np.float32).reshape(128, 4096).T)
    bF = np.ascontiguousarray(
        np.asarray(L0_ffn_B, dtype=np.float32).transpose(0, 2, 1).reshape(128, 4096))
    wrT = np.ascontiguousarray(np.asarray(L0_ffn_Wr, dtype=np.float32).T)
    br1 = np.ascontiguousarray(np.asarray(L0_ffn_br, dtype=np.float32).reshape(1, 8))
    eye = np.eye(128, dtype=np.float32)

    if _nc_cache is None:
        _nc_cache = build_program()
    nc = _nc_cache

    in_maps = []
    for c in range(8):
        in_maps.append(dict(
            x_in=np.ascontiguousarray(x[c * 2048:(c + 1) * 2048]),
            wgT_in=wgT, wuT_in=wuT, wdT_in=wdT,
            aT_in=aT, bF_in=bF, wrT_in=wrT, br_in=br1, eye_in=eye))
    res = run_bass_kernel_spmd(nc, in_maps, core_ids=list(range(8)))
    y = np.concatenate([r["y_out"] for r in res.results], axis=0)
    return y.reshape(2, 8192, 4096)
